# version 28
# baseline (speedup 1.0000x reference)
import numpy as np
from contextlib import ExitStack

import concourse.bass as bass
import concourse.mybir as mybir
from concourse.bass import broadcast_tensor_aps
from concourse.bass_utils import run_bass_kernel_spmd

F32 = mybir.dt.float32
BF16 = mybir.dt.bfloat16
AF = mybir.ActivationFunctionType
ALU = mybir.AluOpType
AX = mybir.AxisListType

NCORES = 8
BFULL, T, D, DI = 32, 2048, 1024, 2816
B = BFULL // NCORES
CH_A = 512
NCH_A = B * T // CH_A
RPC = CH_A // 128
NCH_B = B * T // 128
DC = D // 128
IC = DI // 128
NB_X = D // 16
EPS = 1e-5
NIB = (DI + 511) // 512
IBS = [(i * 512, min(512, DI - i * 512)) for i in range(NIB)]
NGD = (IC + 3) // 4


class Prog:

    def __init__(self, prev_tok):
        self.sems = {}
        self.tok = {}
        self.prev = prev_tok

    def add_sem(self, name, handle):
        self.sems[name] = [handle, 0]

    def inc(self, instr, semname, key=None, k=1):
        s = self.sems[semname]
        if instr is not None and k > 0:
            instr.then_inc(s[0], k)
        s[1] += k
        if key is not None:
            assert key not in self.tok, key
            self.tok[key] = (semname, s[1])

    def wait(self, eng, key):
        if self.prev is None:
            return
        semname, v = self.prev[key]
        if v > 0:
            eng.wait_ge(self.sems[semname][0], v)


def _q_pre(P, vec, src, n, m16, sc, rc, sch, tax, tx, abs_key, tx_key, sp):
    vec.tensor_reduce(
        out=m16.rearrange("p (nb one) -> p nb one", one=1),
        in_=src.rearrange("p (nb k) -> p nb k", k=16),
        axis=AX.X, op=ALU.max, apply_absolute_value=True)
    sp()
    vec.tensor_scalar(out=sc, in0=m16, scalar1=1.0 / 6.0, scalar2=1e-12,
                      op0=ALU.mult, op1=ALU.max)
    sp()
    vec.reciprocal(out=rc, in_=sc)
    vec.tensor_scalar_mul(sch, sc, 0.5)
    sp()
    P.wait(vec, abs_key)
    b0, b1 = broadcast_tensor_aps(
        tax.rearrange("p (nb k) -> p nb k", k=16),
        rc.rearrange("p (nb one) -> p nb one", one=1))
    ins = vec.tensor_tensor(out=tx.rearrange("p (nb k) -> p nb k", k=16),
                            in0=b0, in1=b1, op=ALU.mult)
    if tx_key is not None:
        P.inc(ins, "s_dve", tx_key)


def _q_prex(P, vec, tax, n, m16, sc, rc, sch, tx, abs_key, tx_key, sp,
            rstd_ap=None):
    P.wait(vec, abs_key)
    vec.tensor_reduce(
        out=m16.rearrange("p (nb one) -> p nb one", one=1),
        in_=tax.rearrange("p (nb k) -> p nb k", k=16),
        axis=AX.X, op=ALU.max)
    sp()
    vec.tensor_scalar(out=sc, in0=m16, scalar1=1.0 / 6.0, scalar2=1e-12,
                      op0=ALU.mult, op1=ALU.max)
    sp()
    vec.reciprocal(out=rc, in_=sc)
    if rstd_ap is not None:
        vec.tensor_scalar(out=sch, in0=sc, scalar1=rstd_ap, scalar2=0.5,
                          op0=ALU.mult, op1=ALU.mult)
    else:
        vec.tensor_scalar_mul(sch, sc, 0.5)
    sp()
    b0, b1 = broadcast_tensor_aps(
        tax.rearrange("p (nb k) -> p nb k", k=16),
        rc.rearrange("p (nb one) -> p nb one", one=1))
    ins = vec.tensor_tensor(out=tx.rearrange("p (nb k) -> p nb k", k=16),
                            in0=b0, in1=b1, op=ALU.mult)
    if tx_key is not None:
        P.inc(ins, "s_dve", tx_key)


def _q_chainAB(vec, cA, cB, tx):
    vec.tensor_scalar(out=cA, in0=tx, scalar1=0.25, scalar2=1.0,
                      op0=ALU.is_gt, op1=ALU.mult)
    vec.tensor_scalar(out=cB, in0=tx, scalar1=5.0, scalar2=2.0,
                      op0=ALU.is_gt, op1=ALU.mult)
    vec.scalar_tensor_tensor(out=cA, in0=tx, scalar=0.75, in1=cA,
                             op0=ALU.is_gt, op1=ALU.add)
    vec.scalar_tensor_tensor(out=cB, in0=tx, scalar=2.5, in1=cB,
                             op0=ALU.is_gt, op1=ALU.add)
    vec.scalar_tensor_tensor(out=cA, in0=tx, scalar=1.25, in1=cA,
                             op0=ALU.is_gt, op1=ALU.add)
    vec.scalar_tensor_tensor(out=cB, in0=tx, scalar=3.5, in1=cB,
                             op0=ALU.is_gt, op1=ALU.add)
    return vec.scalar_tensor_tensor(out=cA, in0=tx, scalar=1.75, in1=cA,
                                    op0=ALU.is_gt, op1=ALU.add)


def _q_chainA(vec, cA, tx):
    vec.tensor_scalar(out=cA, in0=tx, scalar1=0.25, scalar2=1.0,
                      op0=ALU.is_gt, op1=ALU.mult)
    for thr in (0.75, 1.25, 1.75):
        ins = vec.scalar_tensor_tensor(out=cA, in0=tx, scalar=thr, in1=cA,
                                       op0=ALU.is_gt, op1=ALU.add)
    return ins


def _q_chainB(eng, cB, tx):
    eng.tensor_scalar(out=cB, in0=tx, scalar1=5.0, scalar2=2.0,
                      op0=ALU.is_gt, op1=ALU.mult)
    for thr in (2.5, 3.5):
        ins = eng.scalar_tensor_tensor(out=cB, in0=tx, scalar=thr, in1=cB,
                                       op0=ALU.is_gt, op1=ALU.add)
    return ins


def _q_chainB_pool(eng, cB, tmp, tx):
    eng.tensor_single_scalar(out=tmp, in_=tx, scalar=2.5, op=ALU.is_gt)
    eng.tensor_scalar(out=cB, in0=tx, scalar1=5.0, scalar2=2.0,
                      op0=ALU.is_gt, op1=ALU.mult)
    eng.tensor_add(cB, cB, tmp)
    eng.tensor_single_scalar(out=tmp, in_=tx, scalar=3.5, op=ALU.is_gt)
    return eng.tensor_add(cB, cB, tmp)


def _q_post(P, vec, cA, cB, sgn, sch, q_out, s_key, q_key, su_key=None):
    ins = vec.scalar_tensor_tensor(out=cA, in0=cB, scalar=2.0, in1=cA,
                                   op0=ALU.mult, op1=ALU.add)
    if s_key is not None:
        P.inc(ins, "s_dve", s_key)
    ins = vec.tensor_mul(cA, cA, sgn)
    if su_key is not None:
        P.inc(ins, "s_dve", su_key)
    b0, b1 = broadcast_tensor_aps(
        cA.rearrange("p (nb k) -> p nb k", k=16),
        sch.rearrange("p (nb one) -> p nb one", one=1))
    ins = vec.tensor_tensor(out=q_out.rearrange("p (nb k) -> p nb k", k=16),
                            in0=b0, in1=b1, op=ALU.mult)
    if q_key is not None:
        P.inc(ins, "s_dve", q_key)


def _nr_rsqrt(vec, y, v, t1, iters=8):
    vec.tensor_scalar_max(t1, v, 1.0)
    vec.reciprocal(out=y, in_=t1)
    for _ in range(iters):
        vec.tensor_mul(t1, y, y)
        vec.tensor_mul(t1, t1, v)
        vec.tensor_scalar(out=t1, in0=t1, scalar1=-0.5, scalar2=1.5,
                          op0=ALU.mult, op1=ALU.add)
        vec.tensor_mul(y, y, t1)
    return


def build_program(prev_tok=None):
    import os as _osrd
    nc = bass.Bass("TRN2", target_bir_lowering=False, debug=False,
                   detect_race_conditions=bool(_osrd.environ.get("K_RACEDET")))
    P = Prog(prev_tok)

    x_h = nc.dram_tensor("x", [B, T, D], F32, kind="ExternalInput").ap()
    wz_h = nc.dram_tensor("wz", [D, D], F32, kind="ExternalInput").ap()
    bz_h = nc.dram_tensor("bz", [D], F32, kind="ExternalInput").ap()
    wh_h = nc.dram_tensor("wh", [D, D], F32, kind="ExternalInput").ap()
    bh_h = nc.dram_tensor("bh", [D], F32, kind="ExternalInput").ap()
    wg_h = nc.dram_tensor("wg", [DI, D], F32, kind="ExternalInput").ap()
    wu_h = nc.dram_tensor("wu", [DI, D], F32, kind="ExternalInput").ap()
    wd_h = nc.dram_tensor("wd", [D, DI], F32, kind="ExternalInput").ap()
    out_h = nc.dram_tensor("out", [B, T, D], F32, kind="ExternalOutput").ap()
    import os as _osd
    _dbg2 = _osd.environ.get("K_DBG2")
    if _dbg2:
        dbg2_h = nc.dram_tensor("dbg2", [128, 8, NCH_B], F32, kind="ExternalOutput").ap()
    _xkind = "ExternalOutput" if _osd.environ.get("K_DBGX2S") else "Internal"
    x2s_h = nc.dram_tensor("x2s", [B, T, D], F32, kind=_xkind).ap()
    _qkind = "ExternalOutput" if _osd.environ.get("K_DBGQW") else "Internal"
    qwg_h = nc.dram_tensor("qwg_s", [128, DC, DI], BF16, kind=_qkind).ap()
    qwu_h = nc.dram_tensor("qwu_s", [128, DC, DI], BF16, kind=_qkind).ap()
    qwd_h = nc.dram_tensor("qwd_s", [128, IC, D], BF16, kind=_qkind).ap()

    outer = ExitStack()
    osb = lambda name, shape, dt: outer.enter_context(nc.sbuf_tensor(name, shape, dt))

    identb = osb("identb", [128, 128], BF16)
    st2_sum = osb("st2_sum", [128, NCH_B], F32)
    st2_sq = osb("st2_sq", [128, NCH_B], F32)
    m2a = osb("m2a", [128, NCH_B], F32)
    rstd2a = osb("rstd2a", [128, NCH_B], F32)
    nm2a = osb("nm2a", [128, NCH_B], F32)
    qwg = osb("qwg", [128, DC, DI], BF16)
    spcA = osb("spcA", [128, 512], F32)
    spcC = osb("spcC", [128, 512], F32)
    carry = osb("carry", [128, DC, B], F32)
    span_pa = ExitStack()
    ssb = lambda name, shape, dt: span_pa.enter_context(nc.sbuf_tensor(name, shape, dt))
    spcB = ssb("spcB", [128, 512], F32)
    identf = ssb("identf", [128, 128], F32)
    st1_sum = ssb("st1_sum", [128, NCH_B], F32)
    st1_sq = ssb("st1_sq", [128, NCH_B], F32)
    m1a = ssb("m1a", [128, NCH_B], F32)
    rstd1a = ssb("rstd1a", [128, NCH_B], F32)
    sqscr2 = ssb("sqscr2", [128, NCH_B], F32)
    msqs = ssb("msqs", [128, NCH_B], F32)
    wzT = ssb("wzT", [128, DC, D], BF16)
    whT = ssb("whT", [128, DC, D], BF16)
    bz2 = ssb("bz2", [128, DC], F32)
    bh2 = ssb("bh2", [128, DC], F32)
    bh05 = ssb("bh05", [128, DC], F32)

    for name in ("s_ldx", "s_ldB", "s_ldQ", "s_ldS", "s_ldSB", "s_st",
                 "s_stB", "s_pe", "s_act", "s_dve", "s_pool", "s_dbg"):
        P.add_sem(name, outer.enter_context(nc.semaphore(name)))
    SEM = {k: v[0] for k, v in P.sems.items()}
    _dbgaw = _osd.environ.get("K_DBGAW")
    if _dbgaw:
        dbgw_h = nc.dram_tensor("dbgw", [128, 2, DC, D], BF16,
                                kind="ExternalOutput").ap()
        dbgb_h = nc.dram_tensor("dbgb", [128, 3, DC], F32,
                                kind="ExternalOutput").ap()
        dbgs_h = nc.dram_tensor("dbgs", [128, 6, NCH_B], F32,
                                kind="ExternalOutput").ap()
        dbgi_h = nc.dram_tensor("dbgi", [128, 128], BF16,
                                kind="ExternalOutput").ap()
    _dbgwz = _osd.environ.get("K_DBGWZ")
    if _dbgwz:
        dbgz_h = nc.dram_tensor("dbgz", [128, 2, 8192], F32,
                                kind="ExternalOutput").ap()

    gu_slices = []
    for wi in (0, 1):
        for s in range((IC + 3) // 4):
            gu_slices.append((wi, s * 4, min(4, IC - s * 4)))
    all_slices = list(gu_slices) + [(2, dcb, 0) for dcb in range(DC)]
    NSL = len(all_slices)

    def slice_n(si):
        wi = all_slices[si][0]
        return (all_slices[si][2] * D) if wi != 2 else DI

    prep = ExitStack()
    psb = lambda name, shape, dt: prep.enter_context(nc.sbuf_tensor(name, shape, dt))
    w32 = psb("w32", [128, DC * D], F32)
    wbf = psb("wbf", [128, DC * D], BF16)
    bbz = psb("bbz", [8, 128], F32)
    bbh = psb("bbh", [8, 128], F32)
    wsl = [psb(f"wsl{i}", [128, 4 * D], F32) for i in range(2)]
    _qwgflat = qwg[:].rearrange("p a b -> p (a b)")
    q_tax = _qwgflat.bitcast(F32)[:, 0:4 * D]
    q_tx = q_tax
    q_cA = _qwgflat[:, 2 * 4 * D:3 * 4 * D]
    q_cB = _qwgflat[:, 3 * 4 * D:4 * 4 * D]
    q_sgn = _qwgflat[:, 4 * 4 * D:5 * 4 * D]
    q_m16 = psb("q_m16", [128, 4 * D // 16], F32)
    q_sc = psb("q_sc", [128, 4 * D // 16], F32)
    q_rc = psb("q_rc", [128, 4 * D // 16], F32)
    q_sch = psb("q_sch", [128, 4 * D // 16], F32)
    qsl = psb("qsl", [128, 4 * D], BF16)
    stg = psb("stg", [128, DC, 512], BF16)
    stgd = psb("stgd", [128, IC, 128], BF16)
    xst = [w32[:, i * RPC * D:(i + 1) * RPC * D].rearrange(
        "p (r d) -> p r d", d=D) for i in range(2)]
    sqs = wbf[:].bitcast(F32).rearrange("p (r d) -> p r d", d=D)
    psum_w = [prep.enter_context(nc.psum_tensor(f"psw{i}", [128, 1024], BF16))
              for i in range(2)]
    psum_b = prep.enter_context(nc.psum_tensor("psb", [128, 8], F32))

    with nc.Block("prep") as blk:

        @blk.sync
        def _(sync):
            if _osd.environ.get("K_TBL1"):
                P.wait(sync, ("tbl1",))
            else:
                P.wait(sync, ("tbl0",))
            sync.dma_start(out=w32[:].rearrange("p (ec d) -> p ec d", d=D),
                           in_=wz_h.rearrange("(ec p) d -> p ec d", p=128)
                           ).then_inc(SEM["s_ldx"], 16)
            P.inc(None, "s_ldx", ("ld_wz",), 16)
            sync.dma_start(out=bbz[:], in_=bz_h.rearrange("(ec p) -> ec p", p=128)
                           ).then_inc(SEM["s_ldx"], 16)
            P.inc(None, "s_ldx", ("ld_bbz",), 16)
            sync.dma_start(out=bbh[:], in_=bh_h.rearrange("(ec p) -> ec p", p=128)
                           ).then_inc(SEM["s_ldx"], 16)
            P.inc(None, "s_ldx", ("ld_bbh",), 16)
            if _dbgwz:
                P.wait(sync, ("ld_wz",))
                sync.dma_start(out=dbgz_h[:, 0, :],
                               in_=w32[:]).then_inc(SEM["s_dbg"], 16)
            P.wait(sync, ("wzb",))
            if _dbgwz:
                sync.dma_start(out=dbgz_h[:, 1, 0:4096],
                               in_=wbf[:].bitcast(F32)).then_inc(SEM["s_dbg"], 16)
                sync.wait_ge(SEM["s_dbg"], 32)
            sync.dma_start(out=w32[:].rearrange("p (ec d) -> p ec d", d=D),
                           in_=wh_h.rearrange("(ec p) d -> p ec d", p=128)
                           ).then_inc(SEM["s_ldQ"], 16)
            P.inc(None, "s_ldQ", ("ld_wh",), 16)
            for si in range(NSL):
                wi = all_slices[si][0]
                sem_sl = "s_ldx" if si % 2 == 0 else "s_ldB"
                if si >= 2:
                    P.wait(sync, ("fabs", si - 2))
                    P.wait(sync, ("txp", si - 2))
                if wi != 2:
                    _, icb, nic = all_slices[si]
                    src = wg_h if wi == 0 else wu_h
                    sync.dma_start(
                        out=wsl[si % 2][:].rearrange("p (ic d) -> p ic d", d=D)[
                            :, 0:nic, :],
                        in_=src.rearrange("(ic p) d -> p ic d", p=128)[
                            :, icb:icb + nic, :],
                    ).then_inc(SEM[sem_sl], 16)
                else:
                    dcb = all_slices[si][1]
                    sync.dma_start(
                        out=wsl[si % 2][:, 0:DI],
                        in_=wd_h[dcb * 128:(dcb + 1) * 128, :],
                    ).then_inc(SEM[sem_sl], 16)
                P.inc(None, sem_sl, ("ld_sl", si), 16)

        @blk.gpsimd
        def _(gpsimd):
            def stats_loads():
                P.wait(gpsimd, ("whb",))
                for k in range(NCH_A):
                    b, c = k % B, k // B
                    sem_x = "s_ldS" if k % 2 == 0 else "s_ldSB"
                    if k >= 2:
                        P.wait(gpsimd, ("ssq", k - 2))
                    gpsimd.dma_start(
                        out=xst[k % 2],
                        in_=x_h[b, c * CH_A:(c + 1) * CH_A, :].rearrange(
                            "(r p) d -> p r d", p=128),
                    ).then_inc(SEM[sem_x], 16)
                    P.inc(None, sem_x, ("xs", k), 16)
            P.wait(gpsimd, ("tbl0",))
            gpsimd.memset(carry[:], 0.0)
            gpsimd.memset(spcA[:], 1.0)
            gpsimd.memset(spcB[:], 1.0)
            gpsimd.memset(spcC[:], 1.0)
            gpsimd.memset(identf[:], 1.0)
            gpsimd.affine_select(out=identf[:], in_=identf[:],
                                 pattern=[[-1, 128]], compare_op=ALU.is_equal,
                                 fill=0.0, base=0, channel_multiplier=1)
            ins = gpsimd.tensor_copy(identb[:], identf[:])
            P.inc(ins, "s_pool", ("ident",))
            stats_loads()

        @blk.tensor
        def _(tensor):
            psw_last = {0: None, 1: None}
            P.wait(tensor, ("ident",))
            if _osd.environ.get("K_WARM"):
                ins = tensor.matmul(psum_w[0][:, 0:128], identb[:], identb[:],
                                    is_transpose=True, start=True, stop=True)
                P.inc(ins, "s_pe", ("warmT",))
                psw_last[0] = ("warmE",)
            for wi in (0, 1):
                P.wait(tensor, ("wzb",) if wi == 0 else ("whb",))
                for dc in range(DC):
                    for h in range(2):
                        bank = (dc * 2 + h) % 2
                        if psw_last[bank] is not None:
                            P.wait(tensor, psw_last[bank])
                        for ec in range(4 * h, 4 * h + 4):
                            ins = tensor.matmul(
                                psum_w[bank][:, (ec - 4 * h) * 128:
                                             (ec - 4 * h + 1) * 128],
                                wbf[:].rearrange("p (ec d) -> p ec d", d=D)[
                                    :, ec, dc * 128:(dc + 1) * 128],
                                identb[:], is_transpose=True,
                                start=(ec == 4 * h), stop=(ec == 4 * h + 3))
                        P.inc(ins, "s_pe", ("wT", wi, dc, h))
                        psw_last[bank] = ("wTe", wi, dc, h)
            P.wait(tensor, ("ld_bbh",))
            ins = tensor.transpose(psum_b[:], bbz[:], identf[0:8, 0:8])
            P.inc(ins, "s_pe", ("bzT",))
            P.wait(tensor, ("bz_evac",))
            P.wait(tensor, ("ld_bbh",))
            ins = tensor.transpose(psum_b[:], bbh[:], identf[0:8, 0:8])
            P.inc(ins, "s_pe", ("bhT",))
            for si in range(NSL):
                wi = all_slices[si][0]
                P.wait(tensor, ("fq", si))
                if wi != 2:
                    nic = all_slices[si][2]
                    for dc in range(DC):
                        if psw_last[dc % 2] is not None:
                            P.wait(tensor, psw_last[dc % 2])
                        for ici in range(nic):
                            ins = tensor.matmul(
                                psum_w[dc % 2][:, ici * 128:(ici + 1) * 128],
                                qsl[:].rearrange("p (ic d) -> p ic d", d=D)[
                                    :, ici, dc * 128:(dc + 1) * 128],
                                identb[:], is_transpose=True,
                                start=(ici == 0), stop=(ici == nic - 1))
                        P.inc(ins, "s_pe", ("fT", si, dc))
                        psw_last[dc % 2] = ("fTe", si, dc)
                else:
                    for g in range(NGD):
                        ics = list(range(g * 4, min(IC, g * 4 + 4)))
                        if psw_last[g % 2] is not None:
                            P.wait(tensor, psw_last[g % 2])
                        for ici in ics:
                            ins = tensor.matmul(
                                psum_w[g % 2][:, (ici - g * 4) * 128:
                                              (ici - g * 4 + 1) * 128],
                                qsl[:, ici * 128:(ici + 1) * 128], identb[:],
                                is_transpose=True,
                                start=(ici == ics[0]), stop=(ici == ics[-1]))
                        P.inc(ins, "s_pe", ("fT", si, g))
                        psw_last[g % 2] = ("fTe", si, g)
            P.inc(tensor.nop(), "s_pe", ("bar0_pe",))

        @blk.vector
        def _(vector):
            def _spv():
                vector.tensor_copy(spcC[:], spcA[:])
            if _osd.environ.get("K_WARM"):
                P.wait(vector, ("warmT",))
                ins = vector.tensor_copy(spcC[:].bitcast(BF16)[:, 0:128],
                                         psum_w[0][:, 0:128])
                P.inc(ins, "s_dve", ("warmE",))
            P.wait(vector, ("ld_bbh",))
            ins = vector.tensor_copy(wbf[:], w32[:])
            P.inc(ins, "s_dve", ("wzb",))
            for dc in range(DC):
                for h in range(2):
                    bank = h % 2
                    P.wait(vector, ("wT", 0, dc, h))
                    ins = vector.tensor_copy(
                        wzT[:, dc, h * 512:(h + 1) * 512],
                        psum_w[bank][:, 0:512])
                    P.inc(ins, "s_dve", ("wTe", 0, dc, h))
            P.wait(vector, ("ld_wh",))
            ins = vector.tensor_copy(wbf[:], w32[:])
            P.inc(ins, "s_dve", ("whb",))
            for dc in range(DC):
                for h in range(2):
                    bank = h % 2
                    P.wait(vector, ("wT", 1, dc, h))
                    ins = vector.tensor_copy(
                        whT[:, dc, h * 512:(h + 1) * 512],
                        psum_w[bank][:, 0:512])
                    P.inc(ins, "s_dve", ("wTe", 1, dc, h))
            P.wait(vector, ("bzT",))
            ins = vector.tensor_scalar_mul(bz2[:], psum_b[:], 0.5)
            P.inc(ins, "s_dve", ("bz_evac",))
            P.wait(vector, ("bhT",))
            vector.tensor_scalar_mul(bh2[:], psum_b[:], 0.5)
            vector.tensor_scalar_add(bh05[:], psum_b[:], 0.5)
            for si in range(NSL):
                n = slice_n(si)
                nblk = n // 16
                wi = all_slices[si][0]
                P.wait(vector, ("ld_sl", si))
                src = wsl[si % 2][:, 0:n]
                _q_pre(P, vector, src, n, q_m16[:, 0:nblk], q_sc[:, 0:nblk],
                       q_rc[:, 0:nblk], q_sch[:, 0:nblk], q_tax[:, 0:n],
                       q_tx[:, 0:n], ("fabs", si), ("txp", si), _spv)
                _q_chainA(vector, q_cA[:, 0:n], q_tx[:, 0:n])
                _q_chainB(vector, q_cB[:, 0:n], q_tx[:, 0:n])
                _q_post(P, vector, q_cA[:, 0:n], q_cB[:, 0:n], q_sgn[:, 0:n],
                        q_sch[:, 0:nblk], qsl[:, 0:n], None, ("fq", si))
                if si >= 1:
                    P.wait(vector, ("fst", si - 1))
                if wi != 2:
                    nic = all_slices[si][2]
                    for dc in range(DC):
                        P.wait(vector, ("fT", si, dc))
                        ins = vector.tensor_copy(
                            stg[:, dc, 0:nic * 128],
                            psum_w[dc % 2][:, 0:nic * 128])
                        P.inc(ins, "s_dve", ("fTe", si, dc))
                else:
                    for g in range(NGD):
                        ngc = min(IC, g * 4 + 4) - g * 4
                        P.wait(vector, ("fT", si, g))
                        ins = vector.tensor_copy(
                            stgd[:, g * 4:g * 4 + ngc, :].rearrange(
                                "p a b -> p (a b)"),
                            psum_w[g % 2][:, 0:ngc * 128])
                        P.inc(ins, "s_dve", ("fTe", si, g))
            P.wait(vector, ("ssqF",))
            _spv()
            vector.tensor_scalar_mul(m1a[:], st1_sum[:], 1.0 / D)
            vector.tensor_scalar(out=sqscr2[:], in0=st1_sq[:], scalar1=1.0 / D,
                                 scalar2=EPS, op0=ALU.mult, op1=ALU.add)
            _spv()
            vector.tensor_mul(msqs[:], m1a[:], m1a[:])
            _spv()
            vector.memset(spcA[:, NCH_B:512], 1.0)
            vector.tensor_sub(spcA[:, 0:NCH_B], sqscr2[:], msqs[:])
            _spv()
            _nr_rsqrt(vector, spcB[:], spcA[:], spcC[:])
            _spv()
            ins = vector.tensor_copy(rstd1a[:], spcB[:, 0:NCH_B])
            P.inc(ins, "s_dve", ("rstd1",))
            P.inc(vector.engine_nop(), "s_dve", ("bar0_dve",))

        @blk.scalar
        def _(scalar):
            ins = scalar.preload_act_table(AF.Silu)
            P.inc(ins, "s_act", ("tbl0",))
            if _osd.environ.get("K_TBL1"):
                for _t1 in range(4):
                    ins = scalar.activation(out=w32[:, 0:2048],
                                            in_=w32[:, 0:2048], func=AF.Square)
                P.inc(ins, "s_act", ("tbl1",))
            for si in range(NSL):
                if si < NCH_A:
                    k = si
                    gt0 = (k % B) * 16 + (k // B) * RPC
                    P.wait(scalar, ("xs", k))
                    if k == 0:
                        P.wait(scalar, ("wTe", 1, DC - 1, 1))
                    for r in range(RPC):
                        scalar.activation(out=sqs[:, r, :],
                                          in_=xst[k % 2][:, r, :],
                                          func=AF.Identity,
                                          accum_out=st1_sum[:, gt0 + r:
                                                            gt0 + r + 1])
                        ins = scalar.activation(out=sqs[:, r, :],
                                                in_=xst[k % 2][:, r, :],
                                                func=AF.Square,
                                                accum_out=st1_sq[:, gt0 + r:
                                                                 gt0 + r + 1])
                    P.inc(ins, "s_act", ("ssq", k))
                n = slice_n(si)
                wi = all_slices[si][0]
                P.wait(scalar, ("ld_sl", si))
                if si >= 1:
                    P.wait(scalar, ("fq", si - 1))
                src = wsl[si % 2][:, 0:n]
                scalar.activation(out=q_tax[:, 0:n], in_=src, func=AF.Abs)
                ins = scalar.activation(out=q_sgn[:, 0:n], in_=src, func=AF.Sign)
                P.inc(ins, "s_act", ("fabs", si))
                if wi != 2:
                    _, icb, nic = all_slices[si]
                    dsth = qwg_h if wi == 0 else qwu_h
                    P.wait(scalar, ("fTe", si, DC - 1))
                    st = scalar.dma_start(
                        out=dsth[:, :, icb * 128:(icb + nic) * 128],
                        in_=stg[:, :, 0:nic * 128])
                else:
                    dcb = all_slices[si][1]
                    P.wait(scalar, ("fTe", si, NGD - 1))
                    st = scalar.dma_start(
                        out=qwd_h[:, :, dcb * 128:(dcb + 1) * 128], in_=stgd[:])
                st.then_inc(SEM["s_st"], 16)
                P.inc(None, "s_st", ("fst", si), 16)
            scalar.activation(out=sqs[:, 0, 0:NCH_B], in_=st1_sum[:],
                              func=AF.Identity)
            ins = scalar.activation(out=sqs[:, 0, 0:NCH_B], in_=st1_sq[:],
                                    func=AF.Identity)
            P.inc(ins, "s_act", ("ssqF",))
            P.inc(scalar.nop(), "s_act", ("bar0_act",))
            P.inc(None, "s_st", ("bar0_st",), 0)

    prep.close()

    pa = ExitStack()
    asb = lambda name, shape, dt: pa.enter_context(nc.sbuf_tensor(name, shape, dt))
    xin = [asb(f"xin{i}", [128, RPC, D], F32) for i in range(2)]
    x2 = [asb(f"x2_{i}", [128, RPC, D], F32) for i in range(2)]
    xln = asb("xln", [128, RPC, D], BF16)
    xlnT = asb("xlnT", [128, DC, CH_A], BF16)
    hsc = asb("hsc", [128, DC, CH_A], BF16)
    abuf = [asb(f"abuf{i}", [128, CH_A], F32) for i in range(2)]
    zbuf = [asb(f"zbuf{i}", [128, CH_A], F32) for i in range(2)]
    sbuf = [asb(f"sbuf{i}", [128, CH_A], F32) for i in range(2)]
    tbuf = [asb(f"tbuf{i}", [128, CH_A], F32) for i in range(2)]
    sqscr = asb("sqscr", [128, D], F32)
    psx = [pa.enter_context(nc.psum_tensor(f"psx{i}", [128, CH_A], BF16))
           for i in range(2)]
    psk = [pa.enter_context(nc.psum_tensor(f"psk{i}", [128, CH_A], F32))
           for i in range(2)]
    psh = [pa.enter_context(nc.psum_tensor(f"psh{i}", [128, CH_A], F32))
           for i in range(2)]
    psh2 = [pa.enter_context(nc.psum_tensor(f"psh2{i}", [128, D], BF16))
            for i in range(2)]

    with nc.Block("phA") as blk:

        @blk.sync
        def _(sync):
            for k in range(NCH_A):
                b, c = k % B, k // B
                sem_x = "s_ldx" if k % 2 == 0 else "s_ldB"
                if k >= 2:
                    P.wait(sync, ("resid_r", k - 2, RPC - 1))
                else:
                    P.wait(sync, ("rstd1",))
                sync.dma_start(
                    out=xin[k % 2][:],
                    in_=x_h[b, c * CH_A:(c + 1) * CH_A, :].rearrange(
                        "(r p) d -> p r d", p=128),
                ).then_inc(SEM[sem_x], 16)
                P.inc(None, sem_x, ("xin", k), 16)
            if _dbgaw:
                P.wait(sync, ("barA_pe",))
                P.wait(sync, ("barA_dve",))
                P.wait(sync, ("barA_act",))
                for di, src in enumerate((wzT, whT)):
                    sync.dma_start(out=dbgw_h[:, di],
                                   in_=src[:]).then_inc(SEM["s_dbg"], 16)
                for di, src in enumerate((bz2, bh2, bh05)):
                    sync.dma_start(out=dbgb_h[:, di],
                                   in_=src[:]).then_inc(SEM["s_dbg"], 16)
                for di, src in enumerate((m1a, rstd1a, m2a, rstd2a,
                                          st1_sum, st1_sq)):
                    sync.dma_start(out=dbgs_h[:, di],
                                   in_=src[:]).then_inc(SEM["s_dbg"], 16)
                sync.dma_start(out=dbgi_h[:],
                               in_=identb[:]).then_inc(SEM["s_dbg"], 16)
                sync.wait_ge(SEM["s_dbg"], 12 * 16)

        @blk.gpsimd
        def _(gpsimd):
            P.wait(gpsimd, ("bar0_st",))
            gpsimd.dma_start(out=qwg[:], in_=qwg_h).then_inc(SEM["s_ldS"], 16)
            P.inc(None, "s_ldS", ("ld_qwg",), 16)


        @blk.tensor
        def _(tensor):
            for k in range(NCH_A):
                P.wait(tensor, ("xln", k))
                for dc in range(DC):
                    pd = ("xTe", k, dc - 2) if dc >= 2 else (
                        ("xTe", k - 1, DC - 2 + dc) if k >= 1 else None)
                    if pd is not None:
                        P.wait(tensor, pd)
                    for r in range(RPC):
                        ins = tensor.matmul(
                            psx[dc % 2][:, r * 128:(r + 1) * 128],
                            xln[:, r, dc * 128:(dc + 1) * 128], identb[:],
                            is_transpose=True,
                            start=(r == 0), stop=(r == RPC - 1))
                    P.inc(ins, "s_pe", ("xT", k, dc))
                P.wait(tensor, ("xTe", k, DC - 1))
                for ec in range(DC):
                    par = ec % 2
                    pk = ("az", k, ec - 2) if ec >= 2 else (
                        ("az", k - 1, DC - 2 + ec) if k >= 1 else None)
                    if pk is not None:
                        P.wait(tensor, pk)
                    for dc in range(DC):
                        ins = tensor.matmul(
                            psk[par][:], wzT[:, dc, ec * 128:(ec + 1) * 128],
                            xlnT[:, dc, :],
                            start=(dc == 0), stop=(dc == DC - 1))
                    P.inc(ins, "s_pe", ("pk", k, ec))
                    ph = ("st1", k, ec - 2) if ec >= 2 else (
                        ("st1", k - 1, DC - 2 + ec) if k >= 1 else None)
                    if ph is not None:
                        P.wait(tensor, ph)
                    for dc in range(DC):
                        ins = tensor.matmul(
                            psh[par][:], whT[:, dc, ec * 128:(ec + 1) * 128],
                            xlnT[:, dc, :],
                            start=(dc == 0), stop=(dc == DC - 1))
                    P.inc(ins, "s_pe", ("ph", k, ec))
                P.wait(tensor, ("scan", k, DC - 1))
                for r in range(RPC):
                    pr = ("resid_r", k, r - 2) if r >= 2 else (
                        ("resid_r", k - 1, RPC - 2 + r) if k >= 1 else None)
                    if pr is not None:
                        P.wait(tensor, pr)
                    for ec in range(DC):
                        ins = tensor.matmul(
                            psh2[r % 2][:, ec * 128:(ec + 1) * 128],
                            hsc[:, ec, r * 128:(r + 1) * 128], identb[:],
                            is_transpose=True,
                            start=(ec == 0), stop=(ec == DC - 1))
                    P.inc(ins, "s_pe", ("h2", k, r))
            P.inc(tensor.nop(), "s_pe", ("barA_pe",))

        @blk.vector
        def _(vector):
            for k in range(NCH_A):
                b, c = k % B, k // B
                i = k % 2
                gt0 = b * 16 + c * RPC
                P.wait(vector, ("xin", k))
                if k == 0:
                    P.wait(vector, ("ident",))
                    if _dbg2:
                        P.wait(vector, ("d2", 6))
                for r in range(RPC):
                    ins = vector.tensor_scalar(
                        out=xln[:, r, :], in0=xin[i][:, r, :],
                        scalar1=m1a[:, gt0 + r:gt0 + r + 1],
                        scalar2=rstd1a[:, gt0 + r:gt0 + r + 1],
                        op0=ALU.subtract, op1=ALU.mult)
                P.inc(ins, "s_dve", ("xln", k))
                for ec in range(DC):
                    par = ec % 2
                    P.wait(vector, ("st1", k, ec))
                    vector.tensor_scalar(out=abuf[par][:], in0=abuf[par][:],
                                         scalar1=-0.5, scalar2=0.5,
                                         op0=ALU.mult, op1=ALU.add)
                    vector.tensor_scalar(out=sbuf[par][:], in0=sbuf[par][:],
                                         scalar1=0.5, scalar2=0.5,
                                         op0=ALU.mult, op1=ALU.add)
                    vector.tensor_max(tbuf[par][:], sbuf[par][:], tbuf[par][:])
                    vector.tensor_scalar(out=zbuf[par][:], in0=abuf[par][:],
                                         scalar1=-1.0, scalar2=1.0,
                                         op0=ALU.mult, op1=ALU.add)
                    vector.tensor_mul(zbuf[par][:], zbuf[par][:], tbuf[par][:])
                    ins = vector.tensor_tensor_scan(
                        out=hsc[:, ec, :], data0=abuf[par][:], data1=zbuf[par][:],
                        initial=carry[:, ec, b:b + 1], op0=ALU.mult, op1=ALU.add)
                    P.inc(ins, "s_dve", ("scan", k, ec))
                vector.tensor_copy(spcC[:], spcA[:])
                for ec in range(DC):
                    vector.tensor_copy(carry[:, ec, b:b + 1],
                                       hsc[:, ec, CH_A - 1:CH_A])
                for r in range(RPC):
                    P.wait(vector, ("h2", k, r))
                    if k >= 2 and r == 0:
                        P.wait(vector, ("x2st", k - 2))
                        P.wait(vector, ("ss2", k - 2))
                    ins = vector.tensor_tensor(
                        out=x2[i][:, r, :], in0=psh2[r % 2][:],
                        in1=xin[i][:, r, :], op=ALU.add)
                    P.inc(ins, "s_dve", ("resid_r", k, r))
            def _sp():
                vector.tensor_copy(spcC[:], spcA[:])
            P.wait(vector, ("ss2F",))
            _sp()
            vector.tensor_scalar_mul(m2a[:], st2_sum[:], 1.0 / D)
            vector.tensor_scalar(out=sqscr2[:], in0=st2_sq[:], scalar1=1.0 / D,
                                 scalar2=EPS, op0=ALU.mult, op1=ALU.add)
            _sp()
            vector.tensor_mul(msqs[:], m2a[:], m2a[:])
            _sp()
            vector.memset(spcA[:, NCH_B:512], 1.0)
            vector.tensor_sub(spcA[:, 0:NCH_B], sqscr2[:], msqs[:])
            _sp()
            _nr_rsqrt(vector, spcB[:], spcA[:], spcC[:])
            _sp()
            vector.tensor_copy(rstd2a[:], spcB[:, 0:NCH_B])
            vector.tensor_scalar_mul(nm2a[:], m2a[:], -1.0)
            _sp()
            P.inc(vector.engine_nop(), "s_dve", ("barA_dve",))

        @blk.scalar
        def _(scalar):
            for k in range(NCH_A):
                b, c = k % B, k // B
                i = k % 2
                P.wait(scalar, ("xin", k))
                for dc in range(DC):
                    P.wait(scalar, ("xT", k, dc))
                    ins = scalar.activation(out=xlnT[:, dc, :],
                                            in_=psx[dc % 2][:], func=AF.Copy)
                    P.inc(ins, "s_act", ("xTe", k, dc))

                for ec in range(DC):
                    par = ec % 2
                    P.wait(scalar, ("pk", k, ec))
                    ps = ("scan", k, ec - 2) if ec >= 2 else (
                        ("scan", k - 1, DC - 2 + ec) if k >= 1 else None)
                    if ps is not None:
                        P.wait(scalar, ps)
                    ins = scalar.activation(out=abuf[par][:], in_=psk[par][:],
                                            func=AF.Tanh, scale=0.5,
                                            bias=bz2[:, ec:ec + 1])
                    P.inc(ins, "s_act", ("az", k, ec))
                    P.wait(scalar, ("ph", k, ec))
                    scalar.activation(out=sbuf[par][:], in_=psh[par][:],
                                      func=AF.Tanh, scale=0.5,
                                      bias=bh2[:, ec:ec + 1])
                    ins = scalar.activation(out=tbuf[par][:], in_=psh[par][:],
                                            func=AF.Identity,
                                            bias=bh05[:, ec:ec + 1])
                    P.inc(ins, "s_act", ("st1", k, ec))
                P.wait(scalar, ("resid_r", k, RPC - 1))
                gt0s = (k % B) * 16 + (k // B) * RPC
                for r in range(RPC):
                    scalar.activation(
                        out=sqscr[:], in_=x2[i][:, r, :],
                        func=AF.Identity,
                        accum_out=st2_sum[:, gt0s + r:gt0s + r + 1])
                    ins = scalar.activation(
                        out=sqscr[:], in_=x2[i][:, r, :], func=AF.Square,
                        accum_out=st2_sq[:, gt0s + r:gt0s + r + 1])
                P.inc(ins, "s_act", ("ss2", k))
                sem_s = "s_st" if k % 2 == 0 else "s_stB"
                st = scalar.dma_start(
                    out=x2s_h[b, c * CH_A:(c + 1) * CH_A, :].rearrange(
                        "(r p) d -> p r d", p=128),
                    in_=x2[i][:])
                st.then_inc(SEM[sem_s], 16)
                P.inc(None, sem_s, ("x2st", k), 16)
            scalar.activation(out=sqscr[:, 0:NCH_B], in_=st2_sum[:],
                              func=AF.Identity)
            ins = scalar.activation(out=sqscr[:, 0:NCH_B], in_=st2_sq[:],
                                    func=AF.Identity)
            P.inc(ins, "s_act", ("ss2F",))
            P.inc(scalar.nop(), "s_act", ("barA_act",))
            P.inc(None, "s_st", ("barA_st",), 0)
            P.inc(None, "s_stB", ("barA_stB",), 0)

    pa.close()
    span_pa.close()

    pb = ExitStack()
    bsb = lambda name, shape, dt: pb.enter_context(nc.sbuf_tensor(name, shape, dt))
    qwu = bsb("qwu", [128, DC, DI], BF16)
    qwd = bsb("qwd", [128, IC, D], BF16)
    x2c = [bsb(f"x2c{i}", [128, D], F32) for i in range(3)]
    tax = bsb("tax", [128, D], F32)
    tx = bsb("tx", [128, D], F32)
    cAx = bsb("cAx", [128, D], BF16)
    cBx = bsb("cBx", [128, D], BF16)
    sgx = bsb("sgx", [128, D], BF16)
    qx = bsb("qx", [128, D], BF16)
    qxT = bsb("qxT", [128, DC, 128], BF16)
    m16x = bsb("m16x", [128, NB_X], F32)
    scx = bsb("scx", [128, NB_X], F32)
    rcx = bsb("rcx", [128, NB_X], F32)
    schx = bsb("schx", [128, NB_X], F32)
    sg_sb = [bsb(f"sg_sb{i}", [128, 512], F32) for i in range(2)]
    HG = [(0, 1024), (1024, 1024), (2048, DI - 2048)]
    tax2 = bsb("tax2", [128, 1024], F32)
    tx2 = bsb("tx2", [128, 1024], F32)
    cAx2 = bsb("cAx2", [128, 1024], BF16)
    cBx2 = bsb("cBx2", [128, 1024], BF16)
    sgx2 = bsb("sgx2", [128, 1024], BF16)
    m16x2 = bsb("m16x2", [128, NB_X], F32)
    scx2 = bsb("scx2", [128, NB_X], F32)
    rcx2 = bsb("rcx2", [128, NB_X], F32)
    schx2 = bsb("schx2", [128, NB_X], F32)
    qh = bsb("qh", [128, DI], BF16)
    taxs, txs, sgxs = [tax, tax2], [tx, tx2], [sgx, sgx2]
    cAs, cBs = [cAx, cAx2], [cBx, cBx2]
    m16s, scs, rcs, schs = ([m16x, m16x2], [scx, scx2],
                            [rcx, rcx2], [schx, schx2])
    taxX = bsb("taxX", [128, 1024], F32)
    qhT = bsb("qhT", [128, IC, 128], BF16)
    psq = [pb.enter_context(nc.psum_tensor(f"psq{i}", [128, 1024], BF16))
           for i in range(2)]
    psg = [pb.enter_context(nc.psum_tensor(f"psg{i}", [128, 512], F32))
           for i in range(2)]
    psu = [pb.enter_context(nc.psum_tensor(f"psu{i}", [128, 512], F32))
           for i in range(2)]
    psd = [pb.enter_context(nc.psum_tensor(f"psd{i}", [128, 512], F32))
           for i in range(2)]

    with nc.Block("phB") as blk:

        @blk.sync
        def _(sync):
            P.wait(sync, ("barA_pe",))
            P.wait(sync, ("barA_dve",))
            P.wait(sync, ("barA_st",))
            P.wait(sync, ("barA_stB",))
            for nm, hbm, dst in (("qwu", qwu_h, qwu),
                                 ("qwd", qwd_h, qwd)):
                sync.dma_start(out=dst[:], in_=hbm).then_inc(SEM["s_ldQ"], 16)
                P.inc(None, "s_ldQ", ("ld_" + nm,), 16)
            for j in range(NCH_B):
                b, rt = j // 16, j % 16
                sem_x = "s_ldx" if j % 2 == 0 else "s_ldB"
                if j >= 3:
                    P.wait(sync, ("outst", j - 3))
                sync.dma_start(
                    out=x2c[j % 3][:],
                    in_=x2s_h[b, rt * 128:(rt + 1) * 128, :],
                ).then_inc(SEM[sem_x], 16)
                P.inc(None, sem_x, ("x2c", j), 16)
            P.wait(sync, ("final_st",))
            P.wait(sync, ("final_stB",))

        @blk.tensor
        def _(tensor):
            def emit_qxT(jj):
                P.wait(tensor, ("qx", jj))
                if jj >= 1:
                    P.wait(tensor, ("qhTe", jj - 1, NGD - 1))
                for dc in range(DC):
                    ins = tensor.matmul(
                        psq[0][:, dc * 128:(dc + 1) * 128],
                        qx[:, dc * 128:(dc + 1) * 128], identb[:],
                        is_transpose=True,
                        start=(dc == 0), stop=(dc == DC - 1))
                P.inc(ins, "s_pe", ("qxT", jj))

            def emit_down(jj):
                P.wait(tensor, ("qhTe", jj, NGD - 1))
                if jj == 0:
                    P.wait(tensor, ("ld_qwd",))
                for nb in range(2):
                    if jj >= 1:
                        P.wait(tensor, ("resid2nb", jj - 1, nb))
                    for ici in range(IC):
                        ins = tensor.matmul(
                            psd[nb][:], qhT[:, ici, :],
                            qwd[:, ici, nb * 512:(nb + 1) * 512],
                            start=(ici == 0), stop=(ici == IC - 1))
                    P.inc(ins, "s_pe", ("pd", jj, nb))

            P.wait(tensor, ("ld_qwg",))
            emit_qxT(0)
            for j in range(NCH_B):
                P.wait(tensor, ("qxTe", j))
                for ib in range(NIB):
                    par = ib % 2
                    i0, ni = IBS[ib]
                    pg = ("silu", j, ib - 2) if ib >= 2 else (
                        ("silu", j - 1, NIB - 2 + ib) if j >= 1 else None)
                    if pg is not None:
                        P.wait(tensor, pg)
                    for dc in range(DC):
                        ins = tensor.matmul(
                            psg[par][:, 0:ni], qxT[:, dc, :],
                            qwg[:, dc, i0:i0 + ni],
                            start=(dc == 0), stop=(dc == DC - 1))
                    P.inc(ins, "s_pe", ("pg", j, ib))
                    pu = ("hb", j, ib - 2) if ib >= 2 else (
                        ("hb", j - 1, NIB - 2 + ib) if j >= 1 else None)
                    if pu is not None:
                        P.wait(tensor, pu)
                    if j == 0 and ib == 0:
                        P.wait(tensor, ("ld_qwu",))
                    for dc in range(DC):
                        ins = tensor.matmul(
                            psu[par][:, 0:ni], qxT[:, dc, :],
                            qwu[:, dc, i0:i0 + ni],
                            start=(dc == 0), stop=(dc == DC - 1))
                    P.inc(ins, "s_pe", ("pu", j, ib))
                for g in range(NGD):
                    ics = list(range(g * 4, min(IC, g * 4 + 4)))
                    P.wait(tensor, ("qhg", j, g // 2))
                    if g == 0:
                        pg2 = ("qhTe", j - 1, NGD - 2) if j >= 1 else None
                    elif g == 1:
                        pg2 = ("qxTe", j)
                    else:
                        pg2 = ("qhTe", j, g - 2)
                    if pg2 is not None:
                        P.wait(tensor, pg2)
                    pt = psq[1] if g % 2 == 0 else psq[0]
                    for ici in ics:
                        ins = tensor.matmul(
                            pt[:, (ici - g * 4) * 128:(ici - g * 4 + 1) * 128],
                            qh[:, ici * 128:(ici + 1) * 128], identb[:],
                            is_transpose=True,
                            start=(ici == ics[0]), stop=(ici == ics[-1]))
                    P.inc(ins, "s_pe", ("qhT", j, g))
                emit_down(j)
                if j + 1 < NCH_B:
                    emit_qxT(j + 1)

        @blk.vector
        def _(vector):
            def _spv():
                vector.tensor_copy(spcC[:], spcA[:])

            def emit_hm(j, hg):
                h0 = HG[hg][0]
                ta = taxs[(hg + 1) % 2]
                for ib in (2 * hg, 2 * hg + 1):
                    par = ib % 2
                    i0, ni = IBS[ib]
                    P.wait(vector, ("silu", j, ib))
                    P.wait(vector, ("pu", j, ib))
                    ins = vector.tensor_tensor(
                        out=ta[:, i0 - h0:i0 - h0 + ni],
                        in0=sg_sb[par][:, 0:ni],
                        in1=psu[par][:, 0:ni], op=ALU.mult)
                    P.inc(ins, "s_dve", ("hb", j, ib))

            def emit_resid(j, nb):
                i = j % 3
                P.wait(vector, ("pd", j, nb))
                ins = vector.tensor_tensor(
                    out=x2c[i][:, nb * 512:(nb + 1) * 512],
                    in0=psd[nb][:],
                    in1=x2c[i][:, nb * 512:(nb + 1) * 512], op=ALU.add)
                P.inc(ins, "s_dve", ("resid2nb", j, nb))

            def emit_xq(jj):
                P.wait(vector, ("x2c", jj))
                _q_prex(P, vector, taxX[:, 0:D], D, m16x[:], scx[:], rcx[:],
                        schx[:], tx[:, 0:D], ("absx", jj), ("tx", jj), _spv,
                        rstd_ap=rstd2a[:, jj:jj + 1])
                ins = _q_chainAB(vector, cAx[:], cBx[:], tx[:, 0:D])
                P.inc(ins, "s_dve", ("txd", jj))
                if jj >= 1:
                    P.wait(vector, ("qxT", jj - 1))
                _q_post(P, vector, cAx[:], cBx[:], sgx[:], schx[:], qx[:],
                        ("Sx", jj), ("qx", jj), su_key=("sux", jj))

            emit_xq(0)
            for j in range(NCH_B):
                b, rt = j // 16, j % 16
                g = b * 16 + rt
                i = j % 2
                if j + 1 < NCH_B:
                    emit_xq(j + 1)
                if j >= 1:
                    emit_resid(j - 1, 0)
                    emit_resid(j - 1, 1)
                emit_hm(j, 0)
                emit_hm(j, 1)
                for hg in range(3):
                    h0, ng = HG[hg]
                    nbh = ng // 16
                    sp = (hg + 1) % 2
                    ta, tb = taxs[sp], txs[sp]
                    if hg == 1:
                        emit_hm(j, 2)
                    P.wait(vector, ("habs", j, hg))
                    vector.tensor_reduce(
                        out=m16s[sp][:, 0:nbh].rearrange(
                            "p (nb one) -> p nb one", one=1),
                        in_=tb[:, 0:ng].rearrange("p (nb k) -> p nb k", k=16),
                        axis=AX.X, op=ALU.max)
                    _spv()
                    vector.tensor_scalar(out=scs[sp][:, 0:nbh],
                                         in0=m16s[sp][:, 0:nbh],
                                         scalar1=1.0 / 6.0, scalar2=1e-12,
                                         op0=ALU.mult, op1=ALU.max)
                    _spv()
                    vector.reciprocal(out=rcs[sp][:, 0:nbh],
                                      in_=scs[sp][:, 0:nbh])
                    vector.tensor_scalar_mul(schs[sp][:, 0:nbh],
                                             scs[sp][:, 0:nbh], 0.5)
                    _spv()
                    b0, b1 = broadcast_tensor_aps(
                        tb[:, 0:ng].rearrange("p (nb k) -> p nb k", k=16),
                        rcs[sp][:, 0:nbh].rearrange(
                            "p (nb one) -> p nb one", one=1))
                    ins = vector.tensor_tensor(
                        out=ta[:, 0:ng].rearrange("p (nb k) -> p nb k", k=16),
                        in0=b0, in1=b1, op=ALU.mult)
                    P.inc(ins, "s_dve", ("htx", j, hg))
                    ins = _q_chainAB(vector, cAs[sp][:, 0:ng],
                                     cBs[sp][:, 0:ng], ta[:, 0:ng])
                    P.inc(ins, "s_dve", ("hchd", j, hg))
                    _q_post(P, vector, cAs[sp][:, 0:ng], cBs[sp][:, 0:ng],
                            sgxs[sp][:, 0:ng], schs[sp][:, 0:nbh],
                            qh[:, h0:h0 + ng], None, ("qhg", j, hg),
                            su_key=("hsu", j, hg))
            emit_resid(NCH_B - 1, 0)
            emit_resid(NCH_B - 1, 1)


        @blk.scalar
        def _(scalar):
            def emit_outst(jj):
                bs, rts = jj // 16, jj % 16
                P.wait(scalar, ("resid2nb", jj, 1))
                sem_o = "s_st" if jj % 2 == 0 else "s_stB"
                st = scalar.dma_start(
                    out=out_h[bs, rts * 128:(rts + 1) * 128, :],
                    in_=x2c[jj % 3][:])
                st.then_inc(SEM[sem_o], 16)
                P.inc(None, sem_o, ("outst", jj), 16)

            def emit_absx(jj):
                P.wait(scalar, ("x2c", jj))
                if jj >= 1:
                    P.wait(scalar, ("tx", jj - 1))
                    if jj >= 2:
                        P.wait(scalar, ("hsu", jj - 2, 1))
                    else:
                        P.wait(scalar, ("sux", jj - 1))
                scalar.activation(out=taxX[:, 0:D], in_=x2c[jj % 3][:],
                                  func=AF.Abs, bias=nm2a[:, jj:jj + 1])
                ins = scalar.activation(out=sgx[:], in_=x2c[jj % 3][:],
                                        func=AF.Sign, bias=nm2a[:, jj:jj + 1])
                P.inc(ins, "s_act", ("absx", jj))

            def emit_qxTe(jj):
                P.wait(scalar, ("qxT", jj))
                ins = scalar.activation(
                    out=qxT[:].rearrange("p a b -> p (a b)"), in_=psq[0][:],
                    func=AF.Copy)
                P.inc(ins, "s_act", ("qxTe", jj))

            emit_absx(0)
            emit_absx(1)
            emit_qxTe(0)
            for j in range(NCH_B):
                b, rt = j // 16, j % 16
                g = b * 16 + rt
                i = j % 2
                for hg in range(3):
                    h0, ng = HG[hg]
                    for ib in (2 * hg, 2 * hg + 1):
                        par = ib % 2
                        ni = IBS[ib][1]
                        P.wait(scalar, ("pg", j, ib))
                        ins = scalar.activation(out=sg_sb[par][:, 0:ni],
                                                in_=psg[par][:, 0:ni],
                                                func=AF.Silu)
                        P.inc(ins, "s_act", ("silu", j, ib))
                    P.wait(scalar, ("hb", j, 2 * hg + 1))
                    if hg == 0:
                        if j >= 1:
                            P.wait(scalar, ("hchd", j - 1, 2))
                            P.wait(scalar, ("hsu", j - 1, 2))
                    elif hg == 1:
                        if j >= 1:
                            P.wait(scalar, ("hchd", j - 1, 1))
                        P.wait(scalar, ("sux", min(j + 1, NCH_B - 1)))
                        P.wait(scalar, ("txd", min(j + 1, NCH_B - 1)))
                    else:
                        P.wait(scalar, ("hchd", j, 0))
                        P.wait(scalar, ("hsu", j, 0))
                    sp = (hg + 1) % 2
                    scalar.activation(out=txs[sp][:, 0:ng],
                                      in_=taxs[sp][:, 0:ng], func=AF.Abs)
                    ins = scalar.activation(out=sgxs[sp][:, 0:ng],
                                            in_=taxs[sp][:, 0:ng],
                                            func=AF.Sign)
                    P.inc(ins, "s_act", ("habs", j, hg))
                    if hg == 0 and j >= 1:
                        emit_outst(j - 1)
                if j + 2 < NCH_B:
                    emit_absx(j + 2)
                for gq in range(NGD):
                    ngc = min(IC, gq * 4 + 4) - gq * 4
                    P.wait(scalar, ("qhT", j, gq))
                    pt = psq[1] if gq % 2 == 0 else psq[0]
                    ins = scalar.activation(
                        out=qhT[:, gq * 4:gq * 4 + ngc, :].rearrange(
                            "p a b -> p (a b)"),
                        in_=pt[:, 0:ngc * 128], func=AF.Copy)
                    P.inc(ins, "s_act", ("qhTe", j, gq))
                if j + 1 < NCH_B:
                    emit_qxTe(j + 1)
            emit_outst(NCH_B - 1)
            P.inc(None, "s_st", ("final_st",), 0)
            P.inc(None, "s_stB", ("final_stB",), 0)

    pb.close()
    outer.close()
    return nc, P.tok


def kernel(**inputs):
    x = np.ascontiguousarray(np.asarray(inputs["L0_attn_h_prev_layer"], dtype=np.float32))
    wz = np.ascontiguousarray(np.asarray(inputs["L0_attn_W_z"], dtype=np.float32))
    bz = np.ascontiguousarray(np.asarray(inputs["L0_attn_b_z"], dtype=np.float32))
    wh = np.ascontiguousarray(np.asarray(inputs["L0_attn_W_h"], dtype=np.float32))
    bh = np.ascontiguousarray(np.asarray(inputs["L0_attn_b_h"], dtype=np.float32))
    wg = np.ascontiguousarray(np.asarray(inputs["L0_ffn_w_gate"], dtype=np.float32))
    wu = np.ascontiguousarray(np.asarray(inputs["L0_ffn_w_up"], dtype=np.float32))
    wd = np.ascontiguousarray(np.asarray(inputs["L0_ffn_w_down"], dtype=np.float32))

    _, tok = build_program(None)
    nc, _ = build_program(tok)

    in_maps = [
        dict(x=x[c * B:(c + 1) * B], wz=wz, bz=bz, wh=wh, bh=bh,
             wg=wg, wu=wu, wd=wd)
        for c in range(NCORES)
    ]
    res = run_bass_kernel_spmd(nc, in_maps, core_ids=list(range(NCORES)))
    out = np.empty((BFULL, T, D), np.float32)
    for c in range(NCORES):
        out[c * B:(c + 1) * B] = np.asarray(res.results[c]["out"])
    return out


# version 31
# speedup vs baseline: 1.0970x; 1.0970x over previous
import numpy as np
from contextlib import ExitStack

import concourse.bass as bass
import concourse.mybir as mybir
from concourse.bass import broadcast_tensor_aps
from concourse.bass_utils import run_bass_kernel_spmd

F32 = mybir.dt.float32
BF16 = mybir.dt.bfloat16
AF = mybir.ActivationFunctionType
ALU = mybir.AluOpType
AX = mybir.AxisListType

NCORES = 8
BFULL, T, D, DI = 32, 2048, 1024, 2816
B = BFULL // NCORES
CH_A = 512
NCH_A = B * T // CH_A
RPC = CH_A // 128
NCH_B = B * T // 128
DC = D // 128
IC = DI // 128
NB_X = D // 16
EPS = 1e-5
NIB = (DI + 511) // 512
IBS = [(i * 512, min(512, DI - i * 512)) for i in range(NIB)]
NGD = (IC + 3) // 4


class Prog:

    def __init__(self, prev_tok):
        self.sems = {}
        self.tok = {}
        self.prev = prev_tok

    def add_sem(self, name, handle):
        self.sems[name] = [handle, 0]

    def inc(self, instr, semname, key=None, k=1):
        s = self.sems[semname]
        if instr is not None and k > 0:
            instr.then_inc(s[0], k)
        s[1] += k
        if key is not None:
            assert key not in self.tok, key
            self.tok[key] = (semname, s[1])

    def wait(self, eng, key):
        if self.prev is None:
            return
        semname, v = self.prev[key]
        if v > 0:
            eng.wait_ge(self.sems[semname][0], v)


def _q_pre(P, vec, src, n, m16, sc, rc, sch, tax, tx, abs_key, tx_key, sp):
    vec.tensor_reduce(
        out=m16.rearrange("p (nb one) -> p nb one", one=1),
        in_=src.rearrange("p (nb k) -> p nb k", k=16),
        axis=AX.X, op=ALU.max, apply_absolute_value=True)
    sp()
    vec.tensor_scalar(out=sc, in0=m16, scalar1=1.0 / 6.0, scalar2=1e-12,
                      op0=ALU.mult, op1=ALU.max)
    sp()
    vec.reciprocal(out=rc, in_=sc)
    vec.tensor_scalar_mul(sch, sc, 0.5)
    sp()
    P.wait(vec, abs_key)
    b0, b1 = broadcast_tensor_aps(
        tax.rearrange("p (nb k) -> p nb k", k=16),
        rc.rearrange("p (nb one) -> p nb one", one=1))
    ins = vec.tensor_tensor(out=tx.rearrange("p (nb k) -> p nb k", k=16),
                            in0=b0, in1=b1, op=ALU.mult)
    if tx_key is not None:
        P.inc(ins, "s_dve", tx_key)


def _q_prex(P, vec, tax, n, m16, sc, rc, sch, tx, abs_key, tx_key, sp,
            rstd_ap=None):
    P.wait(vec, abs_key)
    vec.tensor_reduce(
        out=m16.rearrange("p (nb one) -> p nb one", one=1),
        in_=tax.rearrange("p (nb k) -> p nb k", k=16),
        axis=AX.X, op=ALU.max)
    sp()
    vec.tensor_scalar(out=sc, in0=m16, scalar1=1.0 / 6.0, scalar2=1e-12,
                      op0=ALU.mult, op1=ALU.max)
    sp()
    vec.reciprocal(out=rc, in_=sc)
    if rstd_ap is not None:
        vec.tensor_scalar(out=sch, in0=sc, scalar1=rstd_ap, scalar2=0.5,
                          op0=ALU.mult, op1=ALU.mult)
    else:
        vec.tensor_scalar_mul(sch, sc, 0.5)
    sp()
    b0, b1 = broadcast_tensor_aps(
        tax.rearrange("p (nb k) -> p nb k", k=16),
        rc.rearrange("p (nb one) -> p nb one", one=1))
    ins = vec.tensor_tensor(out=tx.rearrange("p (nb k) -> p nb k", k=16),
                            in0=b0, in1=b1, op=ALU.mult)
    if tx_key is not None:
        P.inc(ins, "s_dve", tx_key)


def _q_chainAB(vec, cA, cB, tx):
    vec.tensor_scalar(out=cA, in0=tx, scalar1=0.25, scalar2=1.0,
                      op0=ALU.is_gt, op1=ALU.mult)
    vec.tensor_scalar(out=cB, in0=tx, scalar1=5.0, scalar2=2.0,
                      op0=ALU.is_gt, op1=ALU.mult)
    vec.scalar_tensor_tensor(out=cA, in0=tx, scalar=0.75, in1=cA,
                             op0=ALU.is_gt, op1=ALU.add)
    vec.scalar_tensor_tensor(out=cB, in0=tx, scalar=2.5, in1=cB,
                             op0=ALU.is_gt, op1=ALU.add)
    vec.scalar_tensor_tensor(out=cA, in0=tx, scalar=1.25, in1=cA,
                             op0=ALU.is_gt, op1=ALU.add)
    vec.scalar_tensor_tensor(out=cB, in0=tx, scalar=3.5, in1=cB,
                             op0=ALU.is_gt, op1=ALU.add)
    return vec.scalar_tensor_tensor(out=cA, in0=tx, scalar=1.75, in1=cA,
                                    op0=ALU.is_gt, op1=ALU.add)


def _q_chainA(vec, cA, tx):
    vec.tensor_scalar(out=cA, in0=tx, scalar1=0.25, scalar2=1.0,
                      op0=ALU.is_gt, op1=ALU.mult)
    for thr in (0.75, 1.25, 1.75):
        ins = vec.scalar_tensor_tensor(out=cA, in0=tx, scalar=thr, in1=cA,
                                       op0=ALU.is_gt, op1=ALU.add)
    return ins


def _q_chainB(eng, cB, tx):
    eng.tensor_scalar(out=cB, in0=tx, scalar1=5.0, scalar2=2.0,
                      op0=ALU.is_gt, op1=ALU.mult)
    for thr in (2.5, 3.5):
        ins = eng.scalar_tensor_tensor(out=cB, in0=tx, scalar=thr, in1=cB,
                                       op0=ALU.is_gt, op1=ALU.add)
    return ins


def _q_chainB_pool(eng, cB, tmp, tx):
    eng.tensor_single_scalar(out=tmp, in_=tx, scalar=2.5, op=ALU.is_gt)
    eng.tensor_scalar(out=cB, in0=tx, scalar1=5.0, scalar2=2.0,
                      op0=ALU.is_gt, op1=ALU.mult)
    eng.tensor_add(cB, cB, tmp)
    eng.tensor_single_scalar(out=tmp, in_=tx, scalar=3.5, op=ALU.is_gt)
    return eng.tensor_add(cB, cB, tmp)


def _q_post(P, vec, cA, cB, sgn, sch, q_out, s_key, q_key, su_key=None):
    ins = vec.scalar_tensor_tensor(out=cA, in0=cB, scalar=2.0, in1=cA,
                                   op0=ALU.mult, op1=ALU.add)
    if s_key is not None:
        P.inc(ins, "s_dve", s_key)
    ins = vec.tensor_mul(cA, cA, sgn)
    if su_key is not None:
        P.inc(ins, "s_dve", su_key)
    b0, b1 = broadcast_tensor_aps(
        cA.rearrange("p (nb k) -> p nb k", k=16),
        sch.rearrange("p (nb one) -> p nb one", one=1))
    ins = vec.tensor_tensor(out=q_out.rearrange("p (nb k) -> p nb k", k=16),
                            in0=b0, in1=b1, op=ALU.mult)
    if q_key is not None:
        P.inc(ins, "s_dve", q_key)


def _nr_rsqrt(vec, y, v, t1, iters=8):
    vec.tensor_scalar_max(t1, v, 1.0)
    vec.reciprocal(out=y, in_=t1)
    for _ in range(iters):
        vec.tensor_mul(t1, y, y)
        vec.tensor_mul(t1, t1, v)
        vec.tensor_scalar(out=t1, in0=t1, scalar1=-0.5, scalar2=1.5,
                          op0=ALU.mult, op1=ALU.add)
        vec.tensor_mul(y, y, t1)
    return


def build_program(prev_tok=None):
    import os as _osrd
    nc = bass.Bass("TRN2", target_bir_lowering=False, debug=False,
                   detect_race_conditions=bool(_osrd.environ.get("K_RACEDET")))
    P = Prog(prev_tok)

    x_h = nc.dram_tensor("x", [B, T, D], F32, kind="ExternalInput").ap()
    wz_h = nc.dram_tensor("wz", [D, D], F32, kind="ExternalInput").ap()
    bz_h = nc.dram_tensor("bz", [D], F32, kind="ExternalInput").ap()
    wh_h = nc.dram_tensor("wh", [D, D], F32, kind="ExternalInput").ap()
    bh_h = nc.dram_tensor("bh", [D], F32, kind="ExternalInput").ap()
    wg_h = nc.dram_tensor("wg", [DI, D], F32, kind="ExternalInput").ap()
    wu_h = nc.dram_tensor("wu", [DI, D], F32, kind="ExternalInput").ap()
    wd_h = nc.dram_tensor("wd", [D, DI], F32, kind="ExternalInput").ap()
    out_h = nc.dram_tensor("out", [B, T, D], F32, kind="ExternalOutput").ap()
    import os as _osd
    _dbg2 = _osd.environ.get("K_DBG2")
    if _dbg2:
        dbg2_h = nc.dram_tensor("dbg2", [128, 8, NCH_B], F32, kind="ExternalOutput").ap()
    _xkind = "ExternalOutput" if _osd.environ.get("K_DBGX2S") else "Internal"
    x2s_h = nc.dram_tensor("x2s", [B, T, D], F32, kind=_xkind).ap()
    _qkind = "ExternalOutput" if _osd.environ.get("K_DBGQW") else "Internal"
    qwg_h = nc.dram_tensor("qwg_s", [128, DC, DI], BF16, kind=_qkind).ap()
    qwu_h = nc.dram_tensor("qwu_s", [128, DC, DI], BF16, kind=_qkind).ap()
    qwd_h = nc.dram_tensor("qwd_s", [128, IC, D], BF16, kind=_qkind).ap()

    outer = ExitStack()
    osb = lambda name, shape, dt: outer.enter_context(nc.sbuf_tensor(name, shape, dt))

    identb = osb("identb", [128, 128], BF16)
    st2_sum = osb("st2_sum", [128, NCH_B], F32)
    st2_sq = osb("st2_sq", [128, NCH_B], F32)
    m2a = osb("m2a", [128, NCH_B], F32)
    rstd2a = osb("rstd2a", [128, NCH_B], F32)
    nm2a = osb("nm2a", [128, NCH_B], F32)
    qwg = osb("qwg", [128, DC, DI], BF16)
    spcA = osb("spcA", [128, 512], F32)
    spcC = osb("spcC", [128, 512], F32)
    carry = osb("carry", [128, DC, B], F32)
    span_pa = ExitStack()
    ssb = lambda name, shape, dt: span_pa.enter_context(nc.sbuf_tensor(name, shape, dt))
    spcB = ssb("spcB", [128, 512], F32)
    identf = ssb("identf", [128, 128], F32)
    st1_sum = ssb("st1_sum", [128, NCH_B], F32)
    st1_sq = ssb("st1_sq", [128, NCH_B], F32)
    m1a = ssb("m1a", [128, NCH_B], F32)
    rstd1a = ssb("rstd1a", [128, NCH_B], F32)
    sqscr2 = ssb("sqscr2", [128, NCH_B], F32)
    msqs = ssb("msqs", [128, NCH_B], F32)
    wzT = ssb("wzT", [128, DC, D], BF16)
    whT = ssb("whT", [128, DC, D], BF16)
    bz2 = ssb("bz2", [128, DC], F32)
    bh2 = ssb("bh2", [128, DC], F32)
    bh05 = ssb("bh05", [128, DC], F32)

    for name in ("s_ldx", "s_ldB", "s_ldQ", "s_ldS", "s_ldSB", "s_st",
                 "s_stB", "s_pe", "s_act", "s_dve", "s_pool", "s_dbg"):
        P.add_sem(name, outer.enter_context(nc.semaphore(name)))
    SEM = {k: v[0] for k, v in P.sems.items()}
    _dbgaw = _osd.environ.get("K_DBGAW")
    if _dbgaw:
        dbgw_h = nc.dram_tensor("dbgw", [128, 2, DC, D], BF16,
                                kind="ExternalOutput").ap()
        dbgb_h = nc.dram_tensor("dbgb", [128, 3, DC], F32,
                                kind="ExternalOutput").ap()
        dbgs_h = nc.dram_tensor("dbgs", [128, 6, NCH_B], F32,
                                kind="ExternalOutput").ap()
        dbgi_h = nc.dram_tensor("dbgi", [128, 128], BF16,
                                kind="ExternalOutput").ap()
    _dbgwz = _osd.environ.get("K_DBGWZ")
    if _dbgwz:
        dbgz_h = nc.dram_tensor("dbgz", [128, 2, 8192], F32,
                                kind="ExternalOutput").ap()

    gu_slices = []
    for wi in (0, 1):
        for s in range((IC + 3) // 4):
            gu_slices.append((wi, s * 4, min(4, IC - s * 4)))
    all_slices = list(gu_slices) + [(2, dcb, 0) for dcb in range(DC)]
    NSL = len(all_slices)

    def slice_n(si):
        wi = all_slices[si][0]
        return (all_slices[si][2] * D) if wi != 2 else DI

    prep = ExitStack()
    psb = lambda name, shape, dt: prep.enter_context(nc.sbuf_tensor(name, shape, dt))
    w32 = psb("w32", [128, DC * D], F32)
    wbf = psb("wbf", [128, DC * D], BF16)
    bbz = psb("bbz", [8, 128], F32)
    bbh = psb("bbh", [8, 128], F32)
    wsl = [psb(f"wsl{i}", [128, 4 * D], F32) for i in range(2)]
    _qwgflat = qwg[:].rearrange("p a b -> p (a b)")
    q_tax = _qwgflat.bitcast(F32)[:, 0:4 * D]
    q_tx = q_tax
    q_cA = _qwgflat[:, 2 * 4 * D:3 * 4 * D]
    q_cB = _qwgflat[:, 3 * 4 * D:4 * 4 * D]
    q_sgn = _qwgflat[:, 4 * 4 * D:5 * 4 * D]
    q_m16 = psb("q_m16", [128, 4 * D // 16], F32)
    q_sc = psb("q_sc", [128, 4 * D // 16], F32)
    q_rc = psb("q_rc", [128, 4 * D // 16], F32)
    q_sch = psb("q_sch", [128, 4 * D // 16], F32)
    qsl = psb("qsl", [128, 4 * D], BF16)
    stg = psb("stg", [128, DC, 512], BF16)
    stgd = psb("stgd", [128, IC, 128], BF16)
    xst = [w32[:, i * RPC * D:(i + 1) * RPC * D].rearrange(
        "p (r d) -> p r d", d=D) for i in range(2)]
    sqs = wbf[:].bitcast(F32).rearrange("p (r d) -> p r d", d=D)
    psum_w = [prep.enter_context(nc.psum_tensor(f"psw{i}", [128, 1024], BF16))
              for i in range(2)]
    psum_b = prep.enter_context(nc.psum_tensor("psb", [128, 8], F32))

    with nc.Block("prep") as blk:

        @blk.sync
        def _(sync):
            if _osd.environ.get("K_TBL1"):
                P.wait(sync, ("tbl1",))
            else:
                P.wait(sync, ("tbl0",))
            sync.dma_start(out=w32[:].rearrange("p (ec d) -> p ec d", d=D),
                           in_=wz_h.rearrange("(ec p) d -> p ec d", p=128)
                           ).then_inc(SEM["s_ldx"], 16)
            P.inc(None, "s_ldx", ("ld_wz",), 16)
            sync.dma_start(out=bbz[:], in_=bz_h.rearrange("(ec p) -> ec p", p=128)
                           ).then_inc(SEM["s_ldx"], 16)
            P.inc(None, "s_ldx", ("ld_bbz",), 16)
            sync.dma_start(out=bbh[:], in_=bh_h.rearrange("(ec p) -> ec p", p=128)
                           ).then_inc(SEM["s_ldx"], 16)
            P.inc(None, "s_ldx", ("ld_bbh",), 16)
            if _dbgwz:
                P.wait(sync, ("ld_wz",))
                sync.dma_start(out=dbgz_h[:, 0, :],
                               in_=w32[:]).then_inc(SEM["s_dbg"], 16)
            P.wait(sync, ("wzb",))
            if _dbgwz:
                sync.dma_start(out=dbgz_h[:, 1, 0:4096],
                               in_=wbf[:].bitcast(F32)).then_inc(SEM["s_dbg"], 16)
                sync.wait_ge(SEM["s_dbg"], 32)
            sync.dma_start(out=w32[:].rearrange("p (ec d) -> p ec d", d=D),
                           in_=wh_h.rearrange("(ec p) d -> p ec d", p=128)
                           ).then_inc(SEM["s_ldQ"], 16)
            P.inc(None, "s_ldQ", ("ld_wh",), 16)
            for si in range(NSL):
                wi = all_slices[si][0]
                sem_sl = "s_ldx" if si % 2 == 0 else "s_ldB"
                if si >= 2:
                    P.wait(sync, ("fabs", si - 2))
                    P.wait(sync, ("txp", si - 2))
                if wi != 2:
                    _, icb, nic = all_slices[si]
                    src = wg_h if wi == 0 else wu_h
                    sync.dma_start(
                        out=wsl[si % 2][:].rearrange("p (ic d) -> p ic d", d=D)[
                            :, 0:nic, :],
                        in_=src.rearrange("(ic p) d -> p ic d", p=128)[
                            :, icb:icb + nic, :],
                    ).then_inc(SEM[sem_sl], 16)
                else:
                    dcb = all_slices[si][1]
                    sync.dma_start(
                        out=wsl[si % 2][:, 0:DI],
                        in_=wd_h[dcb * 128:(dcb + 1) * 128, :],
                    ).then_inc(SEM[sem_sl], 16)
                P.inc(None, sem_sl, ("ld_sl", si), 16)

        @blk.gpsimd
        def _(gpsimd):
            def stats_loads():
                P.wait(gpsimd, ("whb",))
                for k in range(NCH_A):
                    b, c = k % B, k // B
                    sem_x = "s_ldS" if k % 2 == 0 else "s_ldSB"
                    if k >= 2:
                        P.wait(gpsimd, ("ssq", k - 2))
                    gpsimd.dma_start(
                        out=xst[k % 2],
                        in_=x_h[b, c * CH_A:(c + 1) * CH_A, :].rearrange(
                            "(r p) d -> p r d", p=128),
                    ).then_inc(SEM[sem_x], 16)
                    P.inc(None, sem_x, ("xs", k), 16)
            P.wait(gpsimd, ("tbl0",))
            gpsimd.memset(carry[:], 0.0)
            gpsimd.memset(spcA[:], 1.0)
            gpsimd.memset(spcB[:], 1.0)
            gpsimd.memset(spcC[:], 1.0)
            gpsimd.memset(identf[:], 1.0)
            gpsimd.affine_select(out=identf[:], in_=identf[:],
                                 pattern=[[-1, 128]], compare_op=ALU.is_equal,
                                 fill=0.0, base=0, channel_multiplier=1)
            ins = gpsimd.tensor_copy(identb[:], identf[:])
            P.inc(ins, "s_pool", ("ident",))
            stats_loads()

        @blk.tensor
        def _(tensor):
            psw_last = {0: None, 1: None}
            P.wait(tensor, ("ident",))
            if _osd.environ.get("K_WARM"):
                ins = tensor.matmul(psum_w[0][:, 0:128], identb[:], identb[:],
                                    is_transpose=True, start=True, stop=True)
                P.inc(ins, "s_pe", ("warmT",))
                psw_last[0] = ("warmE",)
            for wi in (0, 1):
                P.wait(tensor, ("wzb",) if wi == 0 else ("whb",))
                for dc in range(DC):
                    for h in range(2):
                        bank = (dc * 2 + h) % 2
                        if psw_last[bank] is not None:
                            P.wait(tensor, psw_last[bank])
                        for ec in range(4 * h, 4 * h + 4):
                            ins = tensor.matmul(
                                psum_w[bank][:, (ec - 4 * h) * 128:
                                             (ec - 4 * h + 1) * 128],
                                wbf[:].rearrange("p (ec d) -> p ec d", d=D)[
                                    :, ec, dc * 128:(dc + 1) * 128],
                                identb[:], is_transpose=True,
                                start=(ec == 4 * h), stop=(ec == 4 * h + 3))
                        P.inc(ins, "s_pe", ("wT", wi, dc, h))
                        psw_last[bank] = ("wTe", wi, dc, h)
            P.wait(tensor, ("ld_bbh",))
            ins = tensor.transpose(psum_b[:], bbz[:], identf[0:8, 0:8])
            P.inc(ins, "s_pe", ("bzT",))
            P.wait(tensor, ("bz_evac",))
            P.wait(tensor, ("ld_bbh",))
            ins = tensor.transpose(psum_b[:], bbh[:], identf[0:8, 0:8])
            P.inc(ins, "s_pe", ("bhT",))
            for si in range(NSL):
                wi = all_slices[si][0]
                P.wait(tensor, ("fq", si))
                if wi != 2:
                    nic = all_slices[si][2]
                    for dc in range(DC):
                        if psw_last[dc % 2] is not None:
                            P.wait(tensor, psw_last[dc % 2])
                        for ici in range(nic):
                            ins = tensor.matmul(
                                psum_w[dc % 2][:, ici * 128:(ici + 1) * 128],
                                qsl[:].rearrange("p (ic d) -> p ic d", d=D)[
                                    :, ici, dc * 128:(dc + 1) * 128],
                                identb[:], is_transpose=True,
                                start=(ici == 0), stop=(ici == nic - 1))
                        P.inc(ins, "s_pe", ("fT", si, dc))
                        psw_last[dc % 2] = ("fTe", si, dc)
                else:
                    for g in range(NGD):
                        ics = list(range(g * 4, min(IC, g * 4 + 4)))
                        if psw_last[g % 2] is not None:
                            P.wait(tensor, psw_last[g % 2])
                        for ici in ics:
                            ins = tensor.matmul(
                                psum_w[g % 2][:, (ici - g * 4) * 128:
                                              (ici - g * 4 + 1) * 128],
                                qsl[:, ici * 128:(ici + 1) * 128], identb[:],
                                is_transpose=True,
                                start=(ici == ics[0]), stop=(ici == ics[-1]))
                        P.inc(ins, "s_pe", ("fT", si, g))
                        psw_last[g % 2] = ("fTe", si, g)
            P.inc(tensor.nop(), "s_pe", ("bar0_pe",))

        @blk.vector
        def _(vector):
            def _spv():
                vector.tensor_copy(spcC[:], spcA[:])
            if _osd.environ.get("K_WARM"):
                P.wait(vector, ("warmT",))
                ins = vector.tensor_copy(spcC[:].bitcast(BF16)[:, 0:128],
                                         psum_w[0][:, 0:128])
                P.inc(ins, "s_dve", ("warmE",))
            P.wait(vector, ("ld_bbh",))
            ins = vector.tensor_copy(wbf[:], w32[:])
            P.inc(ins, "s_dve", ("wzb",))
            for dc in range(DC):
                for h in range(2):
                    bank = h % 2
                    P.wait(vector, ("wT", 0, dc, h))
                    ins = vector.tensor_copy(
                        wzT[:, dc, h * 512:(h + 1) * 512],
                        psum_w[bank][:, 0:512])
                    P.inc(ins, "s_dve", ("wTe", 0, dc, h))
            P.wait(vector, ("ld_wh",))
            ins = vector.tensor_copy(wbf[:], w32[:])
            P.inc(ins, "s_dve", ("whb",))
            for dc in range(DC):
                for h in range(2):
                    bank = h % 2
                    P.wait(vector, ("wT", 1, dc, h))
                    ins = vector.tensor_copy(
                        whT[:, dc, h * 512:(h + 1) * 512],
                        psum_w[bank][:, 0:512])
                    P.inc(ins, "s_dve", ("wTe", 1, dc, h))
            P.wait(vector, ("bzT",))
            ins = vector.tensor_scalar_mul(bz2[:], psum_b[:], 0.5)
            P.inc(ins, "s_dve", ("bz_evac",))
            P.wait(vector, ("bhT",))
            vector.tensor_scalar_mul(bh2[:], psum_b[:], 0.5)
            vector.tensor_scalar_add(bh05[:], psum_b[:], 0.5)
            for si in range(NSL):
                n = slice_n(si)
                nblk = n // 16
                wi = all_slices[si][0]
                P.wait(vector, ("ld_sl", si))
                src = wsl[si % 2][:, 0:n]
                _q_pre(P, vector, src, n, q_m16[:, 0:nblk], q_sc[:, 0:nblk],
                       q_rc[:, 0:nblk], q_sch[:, 0:nblk], q_tax[:, 0:n],
                       q_tx[:, 0:n], ("fabs", si), ("txp", si), _spv)
                _q_chainA(vector, q_cA[:, 0:n], q_tx[:, 0:n])
                _q_chainB(vector, q_cB[:, 0:n], q_tx[:, 0:n])
                _q_post(P, vector, q_cA[:, 0:n], q_cB[:, 0:n], q_sgn[:, 0:n],
                        q_sch[:, 0:nblk], qsl[:, 0:n], None, ("fq", si))
                if si >= 1:
                    P.wait(vector, ("fst", si - 1))
                if wi != 2:
                    nic = all_slices[si][2]
                    for dc in range(DC):
                        P.wait(vector, ("fT", si, dc))
                        ins = vector.tensor_copy(
                            stg[:, dc, 0:nic * 128],
                            psum_w[dc % 2][:, 0:nic * 128])
                        P.inc(ins, "s_dve", ("fTe", si, dc))
                else:
                    for g in range(NGD):
                        ngc = min(IC, g * 4 + 4) - g * 4
                        P.wait(vector, ("fT", si, g))
                        ins = vector.tensor_copy(
                            stgd[:, g * 4:g * 4 + ngc, :].rearrange(
                                "p a b -> p (a b)"),
                            psum_w[g % 2][:, 0:ngc * 128])
                        P.inc(ins, "s_dve", ("fTe", si, g))
            P.wait(vector, ("ssqF",))
            _spv()
            vector.tensor_scalar_mul(m1a[:], st1_sum[:], 1.0 / D)
            vector.tensor_scalar(out=sqscr2[:], in0=st1_sq[:], scalar1=1.0 / D,
                                 scalar2=EPS, op0=ALU.mult, op1=ALU.add)
            _spv()
            vector.tensor_mul(msqs[:], m1a[:], m1a[:])
            _spv()
            vector.memset(spcA[:, NCH_B:512], 1.0)
            vector.tensor_sub(spcA[:, 0:NCH_B], sqscr2[:], msqs[:])
            _spv()
            _nr_rsqrt(vector, spcB[:], spcA[:], spcC[:])
            _spv()
            ins = vector.tensor_copy(rstd1a[:], spcB[:, 0:NCH_B])
            P.inc(ins, "s_dve", ("rstd1",))
            P.inc(vector.engine_nop(), "s_dve", ("bar0_dve",))

        @blk.scalar
        def _(scalar):
            ins = scalar.preload_act_table(AF.Silu)
            P.inc(ins, "s_act", ("tbl0",))
            if _osd.environ.get("K_TBL1"):
                for _t1 in range(4):
                    ins = scalar.activation(out=w32[:, 0:2048],
                                            in_=w32[:, 0:2048], func=AF.Square)
                P.inc(ins, "s_act", ("tbl1",))
            for si in range(NSL):
                if si < NCH_A:
                    k = si
                    gt0 = (k % B) * 16 + (k // B) * RPC
                    P.wait(scalar, ("xs", k))
                    if k == 0:
                        P.wait(scalar, ("wTe", 1, DC - 1, 1))
                    for r in range(RPC):
                        scalar.activation(out=sqs[:, r, :],
                                          in_=xst[k % 2][:, r, :],
                                          func=AF.Identity,
                                          accum_out=st1_sum[:, gt0 + r:
                                                            gt0 + r + 1])
                        ins = scalar.activation(out=sqs[:, r, :],
                                                in_=xst[k % 2][:, r, :],
                                                func=AF.Square,
                                                accum_out=st1_sq[:, gt0 + r:
                                                                 gt0 + r + 1])
                    P.inc(ins, "s_act", ("ssq", k))
                n = slice_n(si)
                wi = all_slices[si][0]
                P.wait(scalar, ("ld_sl", si))
                if si >= 1:
                    P.wait(scalar, ("fq", si - 1))
                src = wsl[si % 2][:, 0:n]
                scalar.activation(out=q_tax[:, 0:n], in_=src, func=AF.Abs)
                ins = scalar.activation(out=q_sgn[:, 0:n], in_=src, func=AF.Sign)
                P.inc(ins, "s_act", ("fabs", si))
                if wi != 2:
                    _, icb, nic = all_slices[si]
                    dsth = qwg_h if wi == 0 else qwu_h
                    P.wait(scalar, ("fTe", si, DC - 1))
                    st = scalar.dma_start(
                        out=dsth[:, :, icb * 128:(icb + nic) * 128],
                        in_=stg[:, :, 0:nic * 128])
                else:
                    dcb = all_slices[si][1]
                    P.wait(scalar, ("fTe", si, NGD - 1))
                    st = scalar.dma_start(
                        out=qwd_h[:, :, dcb * 128:(dcb + 1) * 128], in_=stgd[:])
                st.then_inc(SEM["s_st"], 16)
                P.inc(None, "s_st", ("fst", si), 16)
            scalar.activation(out=sqs[:, 0, 0:NCH_B], in_=st1_sum[:],
                              func=AF.Identity)
            ins = scalar.activation(out=sqs[:, 0, 0:NCH_B], in_=st1_sq[:],
                                    func=AF.Identity)
            P.inc(ins, "s_act", ("ssqF",))
            P.inc(scalar.nop(), "s_act", ("bar0_act",))
            P.inc(None, "s_st", ("bar0_st",), 0)

    prep.close()

    pa = ExitStack()
    asb = lambda name, shape, dt: pa.enter_context(nc.sbuf_tensor(name, shape, dt))
    xin = [asb(f"xin{i}", [128, RPC, D], F32) for i in range(2)]
    x2 = [asb(f"x2_{i}", [128, RPC, D], F32) for i in range(2)]
    xln = asb("xln", [128, RPC, D], BF16)
    xlnT = asb("xlnT", [128, DC, CH_A], BF16)
    hsc = asb("hsc", [128, DC, CH_A], BF16)
    abuf = [asb(f"abuf{i}", [128, CH_A], F32) for i in range(2)]
    zbuf = [asb(f"zbuf{i}", [128, CH_A], F32) for i in range(2)]
    sbuf = [asb(f"sbuf{i}", [128, CH_A], F32) for i in range(2)]
    tbuf = [asb(f"tbuf{i}", [128, CH_A], F32) for i in range(2)]
    sqscr = asb("sqscr", [128, D], F32)
    psx = [pa.enter_context(nc.psum_tensor(f"psx{i}", [128, CH_A], BF16))
           for i in range(2)]
    psk = [pa.enter_context(nc.psum_tensor(f"psk{i}", [128, CH_A], F32))
           for i in range(2)]
    psh = [pa.enter_context(nc.psum_tensor(f"psh{i}", [128, CH_A], F32))
           for i in range(2)]
    psh2 = [pa.enter_context(nc.psum_tensor(f"psh2{i}", [128, D], BF16))
            for i in range(2)]

    with nc.Block("phA") as blk:

        @blk.sync
        def _(sync):
            for k in range(NCH_A):
                b, c = k % B, k // B
                sem_x = "s_ldx" if k % 2 == 0 else "s_ldB"
                if k >= 2:
                    P.wait(sync, ("resid_r", k - 2, RPC - 1))
                else:
                    P.wait(sync, ("rstd1",))
                sync.dma_start(
                    out=xin[k % 2][:],
                    in_=x_h[b, c * CH_A:(c + 1) * CH_A, :].rearrange(
                        "(r p) d -> p r d", p=128),
                ).then_inc(SEM[sem_x], 16)
                P.inc(None, sem_x, ("xin", k), 16)
            if _dbgaw:
                P.wait(sync, ("barA_pe",))
                P.wait(sync, ("barA_dve",))
                P.wait(sync, ("barA_act",))
                for di, src in enumerate((wzT, whT)):
                    sync.dma_start(out=dbgw_h[:, di],
                                   in_=src[:]).then_inc(SEM["s_dbg"], 16)
                for di, src in enumerate((bz2, bh2, bh05)):
                    sync.dma_start(out=dbgb_h[:, di],
                                   in_=src[:]).then_inc(SEM["s_dbg"], 16)
                for di, src in enumerate((m1a, rstd1a, m2a, rstd2a,
                                          st1_sum, st1_sq)):
                    sync.dma_start(out=dbgs_h[:, di],
                                   in_=src[:]).then_inc(SEM["s_dbg"], 16)
                sync.dma_start(out=dbgi_h[:],
                               in_=identb[:]).then_inc(SEM["s_dbg"], 16)
                sync.wait_ge(SEM["s_dbg"], 12 * 16)

        @blk.gpsimd
        def _(gpsimd):
            P.wait(gpsimd, ("bar0_st",))
            gpsimd.dma_start(out=qwg[:], in_=qwg_h).then_inc(SEM["s_ldS"], 16)
            P.inc(None, "s_ldS", ("ld_qwg",), 16)


        @blk.tensor
        def _(tensor):
            for k in range(NCH_A):
                P.wait(tensor, ("xln", k))
                for dc in range(DC):
                    pd = ("xTe", k, dc - 2) if dc >= 2 else (
                        ("xTe", k - 1, DC - 2 + dc) if k >= 1 else None)
                    if pd is not None:
                        P.wait(tensor, pd)
                    for r in range(RPC):
                        ins = tensor.matmul(
                            psx[dc % 2][:, r * 128:(r + 1) * 128],
                            xln[:, r, dc * 128:(dc + 1) * 128], identb[:],
                            is_transpose=True,
                            start=(r == 0), stop=(r == RPC - 1))
                    P.inc(ins, "s_pe", ("xT", k, dc))
                P.wait(tensor, ("xTe", k, DC - 1))
                for ec in range(DC):
                    par = ec % 2
                    pk = ("az", k, ec - 2) if ec >= 2 else (
                        ("az", k - 1, DC - 2 + ec) if k >= 1 else None)
                    if pk is not None:
                        P.wait(tensor, pk)
                    for dc in range(DC):
                        ins = tensor.matmul(
                            psk[par][:], wzT[:, dc, ec * 128:(ec + 1) * 128],
                            xlnT[:, dc, :],
                            start=(dc == 0), stop=(dc == DC - 1))
                    P.inc(ins, "s_pe", ("pk", k, ec))
                    ph = ("st1", k, ec - 2) if ec >= 2 else (
                        ("st1", k - 1, DC - 2 + ec) if k >= 1 else None)
                    if ph is not None:
                        P.wait(tensor, ph)
                    for dc in range(DC):
                        ins = tensor.matmul(
                            psh[par][:], whT[:, dc, ec * 128:(ec + 1) * 128],
                            xlnT[:, dc, :],
                            start=(dc == 0), stop=(dc == DC - 1))
                    P.inc(ins, "s_pe", ("ph", k, ec))
                P.wait(tensor, ("scan", k, DC - 1))
                for r in range(RPC):
                    pr = ("resid_r", k, r - 2) if r >= 2 else (
                        ("resid_r", k - 1, RPC - 2 + r) if k >= 1 else None)
                    if pr is not None:
                        P.wait(tensor, pr)
                    for ec in range(DC):
                        ins = tensor.matmul(
                            psh2[r % 2][:, ec * 128:(ec + 1) * 128],
                            hsc[:, ec, r * 128:(r + 1) * 128], identb[:],
                            is_transpose=True,
                            start=(ec == 0), stop=(ec == DC - 1))
                    P.inc(ins, "s_pe", ("h2", k, r))
            P.inc(tensor.nop(), "s_pe", ("barA_pe",))

        @blk.vector
        def _(vector):
            for k in range(NCH_A):
                b, c = k % B, k // B
                i = k % 2
                gt0 = b * 16 + c * RPC
                P.wait(vector, ("xin", k))
                if k == 0:
                    P.wait(vector, ("ident",))
                    if _dbg2:
                        P.wait(vector, ("d2", 6))
                for r in range(RPC):
                    ins = vector.tensor_scalar(
                        out=xln[:, r, :], in0=xin[i][:, r, :],
                        scalar1=m1a[:, gt0 + r:gt0 + r + 1],
                        scalar2=rstd1a[:, gt0 + r:gt0 + r + 1],
                        op0=ALU.subtract, op1=ALU.mult)
                P.inc(ins, "s_dve", ("xln", k))
                for ec in range(DC):
                    par = ec % 2
                    P.wait(vector, ("st1", k, ec))
                    vector.tensor_scalar(out=abuf[par][:], in0=abuf[par][:],
                                         scalar1=-0.5, scalar2=0.5,
                                         op0=ALU.mult, op1=ALU.add)
                    vector.tensor_scalar(out=sbuf[par][:], in0=sbuf[par][:],
                                         scalar1=0.5, scalar2=0.5,
                                         op0=ALU.mult, op1=ALU.add)
                    vector.tensor_max(tbuf[par][:], sbuf[par][:], tbuf[par][:])
                    vector.tensor_scalar(out=zbuf[par][:], in0=abuf[par][:],
                                         scalar1=-1.0, scalar2=1.0,
                                         op0=ALU.mult, op1=ALU.add)
                    vector.tensor_mul(zbuf[par][:], zbuf[par][:], tbuf[par][:])
                    ins = vector.tensor_tensor_scan(
                        out=hsc[:, ec, :], data0=abuf[par][:], data1=zbuf[par][:],
                        initial=carry[:, ec, b:b + 1], op0=ALU.mult, op1=ALU.add)
                    P.inc(ins, "s_dve", ("scan", k, ec))
                vector.tensor_copy(spcC[:], spcA[:])
                for ec in range(DC):
                    vector.tensor_copy(carry[:, ec, b:b + 1],
                                       hsc[:, ec, CH_A - 1:CH_A])
                for r in range(RPC):
                    P.wait(vector, ("h2", k, r))
                    if k >= 2 and r == 0:
                        P.wait(vector, ("x2st", k - 2))
                        P.wait(vector, ("ss2", k - 2))
                    ins = vector.tensor_tensor(
                        out=x2[i][:, r, :], in0=psh2[r % 2][:],
                        in1=xin[i][:, r, :], op=ALU.add)
                    P.inc(ins, "s_dve", ("resid_r", k, r))
            def _sp():
                vector.tensor_copy(spcC[:], spcA[:])
            P.wait(vector, ("ss2F",))
            _sp()
            vector.tensor_scalar_mul(m2a[:], st2_sum[:], 1.0 / D)
            vector.tensor_scalar(out=sqscr2[:], in0=st2_sq[:], scalar1=1.0 / D,
                                 scalar2=EPS, op0=ALU.mult, op1=ALU.add)
            _sp()
            vector.tensor_mul(msqs[:], m2a[:], m2a[:])
            _sp()
            vector.memset(spcA[:, NCH_B:512], 1.0)
            vector.tensor_sub(spcA[:, 0:NCH_B], sqscr2[:], msqs[:])
            _sp()
            _nr_rsqrt(vector, spcB[:], spcA[:], spcC[:])
            _sp()
            vector.tensor_copy(rstd2a[:], spcB[:, 0:NCH_B])
            vector.tensor_scalar_mul(nm2a[:], m2a[:], -1.0)
            _sp()
            P.inc(vector.engine_nop(), "s_dve", ("barA_dve",))

        @blk.scalar
        def _(scalar):
            for k in range(NCH_A):
                b, c = k % B, k // B
                i = k % 2
                P.wait(scalar, ("xin", k))
                for dc in range(DC):
                    P.wait(scalar, ("xT", k, dc))
                    ins = scalar.activation(out=xlnT[:, dc, :],
                                            in_=psx[dc % 2][:], func=AF.Copy)
                    P.inc(ins, "s_act", ("xTe", k, dc))

                for ec in range(DC):
                    par = ec % 2
                    P.wait(scalar, ("pk", k, ec))
                    ps = ("scan", k, ec - 2) if ec >= 2 else (
                        ("scan", k - 1, DC - 2 + ec) if k >= 1 else None)
                    if ps is not None:
                        P.wait(scalar, ps)
                    ins = scalar.activation(out=abuf[par][:], in_=psk[par][:],
                                            func=AF.Tanh, scale=0.5,
                                            bias=bz2[:, ec:ec + 1])
                    P.inc(ins, "s_act", ("az", k, ec))
                    P.wait(scalar, ("ph", k, ec))
                    scalar.activation(out=sbuf[par][:], in_=psh[par][:],
                                      func=AF.Tanh, scale=0.5,
                                      bias=bh2[:, ec:ec + 1])
                    ins = scalar.activation(out=tbuf[par][:], in_=psh[par][:],
                                            func=AF.Identity,
                                            bias=bh05[:, ec:ec + 1])
                    P.inc(ins, "s_act", ("st1", k, ec))
                P.wait(scalar, ("resid_r", k, RPC - 1))
                gt0s = (k % B) * 16 + (k // B) * RPC
                for r in range(RPC):
                    scalar.activation(
                        out=sqscr[:], in_=x2[i][:, r, :],
                        func=AF.Identity,
                        accum_out=st2_sum[:, gt0s + r:gt0s + r + 1])
                    ins = scalar.activation(
                        out=sqscr[:], in_=x2[i][:, r, :], func=AF.Square,
                        accum_out=st2_sq[:, gt0s + r:gt0s + r + 1])
                P.inc(ins, "s_act", ("ss2", k))
                sem_s = "s_st" if k % 2 == 0 else "s_stB"
                st = scalar.dma_start(
                    out=x2s_h[b, c * CH_A:(c + 1) * CH_A, :].rearrange(
                        "(r p) d -> p r d", p=128),
                    in_=x2[i][:])
                st.then_inc(SEM[sem_s], 16)
                P.inc(None, sem_s, ("x2st", k), 16)
            scalar.activation(out=sqscr[:, 0:NCH_B], in_=st2_sum[:],
                              func=AF.Identity)
            ins = scalar.activation(out=sqscr[:, 0:NCH_B], in_=st2_sq[:],
                                    func=AF.Identity)
            P.inc(ins, "s_act", ("ss2F",))
            P.inc(scalar.nop(), "s_act", ("barA_act",))
            P.inc(None, "s_st", ("barA_st",), 0)
            P.inc(None, "s_stB", ("barA_stB",), 0)

    pa.close()
    span_pa.close()

    pb = ExitStack()
    bsb = lambda name, shape, dt: pb.enter_context(nc.sbuf_tensor(name, shape, dt))
    qwu = bsb("qwu", [128, DC, DI], BF16)
    qwd = bsb("qwd", [128, IC, D], BF16)
    x2c = [bsb(f"x2c{i}", [128, D], F32) for i in range(3)]
    tax = bsb("tax", [128, D], F32)
    tx = bsb("tx", [128, D], F32)
    cAx = bsb("cAx", [128, D], BF16)
    cBx = bsb("cBx", [128, D], BF16)
    sgx = bsb("sgx", [128, D], BF16)
    qx = bsb("qx", [128, D], BF16)
    qxT = bsb("qxT", [128, DC, 128], BF16)
    m16x = bsb("m16x", [128, NB_X], F32)
    scx = bsb("scx", [128, NB_X], F32)
    rcx = bsb("rcx", [128, NB_X], F32)
    schx = bsb("schx", [128, NB_X], F32)
    sg_sb = [bsb(f"sg_sb{i}", [128, 512], F32) for i in range(2)]
    HG = [(0, 1024), (1024, 1024), (2048, DI - 2048)]
    tax2 = bsb("tax2", [128, 1024], F32)
    tx2 = bsb("tx2", [128, 1024], F32)
    cAx2 = bsb("cAx2", [128, 1024], BF16)
    cBx2 = bsb("cBx2", [128, 1024], BF16)
    sgx2 = bsb("sgx2", [128, 1024], BF16)
    m16x2 = bsb("m16x2", [128, NB_X], F32)
    scx2 = bsb("scx2", [128, NB_X], F32)
    rcx2 = bsb("rcx2", [128, NB_X], F32)
    schx2 = bsb("schx2", [128, NB_X], F32)
    qh = bsb("qh", [128, DI], BF16)
    taxs, txs, sgxs = [tax, tax2], [tx, tx2], [sgx, sgx2]
    cAs, cBs = [cAx, cAx2], [cBx, cBx2]
    m16s, scs, rcs, schs = ([m16x, m16x2], [scx, scx2],
                            [rcx, rcx2], [schx, schx2])
    taxX = bsb("taxX", [128, 1024], F32)
    qhT = bsb("qhT", [128, IC, 128], BF16)
    psq = [pb.enter_context(nc.psum_tensor(f"psq{i}", [128, 1024], BF16))
           for i in range(2)]
    psg = [pb.enter_context(nc.psum_tensor(f"psg{i}", [128, 512], F32))
           for i in range(2)]
    psu = [pb.enter_context(nc.psum_tensor(f"psu{i}", [128, 512], F32))
           for i in range(2)]
    psd = [pb.enter_context(nc.psum_tensor(f"psd{i}", [128, 512], F32))
           for i in range(2)]

    with nc.Block("phB") as blk:

        @blk.sync
        def _(sync):
            P.wait(sync, ("barA_pe",))
            P.wait(sync, ("barA_dve",))
            P.wait(sync, ("barA_st",))
            P.wait(sync, ("barA_stB",))
            for nm, hbm, dst in (("qwu", qwu_h, qwu),
                                 ("qwd", qwd_h, qwd)):
                sync.dma_start(out=dst[:], in_=hbm).then_inc(SEM["s_ldQ"], 16)
                P.inc(None, "s_ldQ", ("ld_" + nm,), 16)
            for j in range(NCH_B):
                b, rt = j // 16, j % 16
                sem_x = "s_ldx" if j % 2 == 0 else "s_ldB"
                if j >= 3:
                    P.wait(sync, ("outst", j - 3))
                sync.dma_start(
                    out=x2c[j % 3][:],
                    in_=x2s_h[b, rt * 128:(rt + 1) * 128, :],
                ).then_inc(SEM[sem_x], 16)
                P.inc(None, sem_x, ("x2c", j), 16)
            P.wait(sync, ("final_st",))
            P.wait(sync, ("final_stB",))

        @blk.tensor
        def _(tensor):
            def emit_qxT(jj):
                P.wait(tensor, ("qx", jj))
                if jj >= 1:
                    P.wait(tensor, ("qhTe", jj - 1, NGD - 1))
                for dc in range(DC):
                    ins = tensor.matmul(
                        psq[0][:, dc * 128:(dc + 1) * 128],
                        qx[:, dc * 128:(dc + 1) * 128], identb[:],
                        is_transpose=True,
                        start=(dc == 0), stop=(dc == DC - 1))
                P.inc(ins, "s_pe", ("qxT", jj))

            def emit_down(jj, part):
                ic0, ic1 = (0, 16) if part == 0 else (16, IC)
                P.wait(tensor, ("qhTe", jj, 3 if part == 0 else NGD - 1))
                if jj == 0 and part == 0:
                    P.wait(tensor, ("ld_qwd",))
                for nb in range(2):
                    if part == 0 and jj >= 1:
                        P.wait(tensor, ("resid2nb", jj - 1, nb))
                    for ici in range(ic0, ic1):
                        ins = tensor.matmul(
                            psd[nb][:], qhT[:, ici, :],
                            qwd[:, ici, nb * 512:(nb + 1) * 512],
                            start=(ici == 0), stop=(ici == IC - 1))
                    if part == 1:
                        P.inc(ins, "s_pe", ("pd", jj, nb))

            P.wait(tensor, ("ld_qwg",))
            emit_qxT(0)
            for j in range(NCH_B):
                P.wait(tensor, ("qxTe", j))
                for ib in range(NIB):
                    par = ib % 2
                    i0, ni = IBS[ib]
                    pg = ("silu", j, ib - 2) if ib >= 2 else (
                        ("silu", j - 1, NIB - 2 + ib) if j >= 1 else None)
                    if pg is not None:
                        P.wait(tensor, pg)
                    for dc in range(DC):
                        ins = tensor.matmul(
                            psg[par][:, 0:ni], qxT[:, dc, :],
                            qwg[:, dc, i0:i0 + ni],
                            start=(dc == 0), stop=(dc == DC - 1))
                    P.inc(ins, "s_pe", ("pg", j, ib))
                    pu = ("hb", j, ib - 2) if ib >= 2 else (
                        ("hb", j - 1, NIB - 2 + ib) if j >= 1 else None)
                    if pu is not None:
                        P.wait(tensor, pu)
                    if j == 0 and ib == 0:
                        P.wait(tensor, ("ld_qwu",))
                    for dc in range(DC):
                        ins = tensor.matmul(
                            psu[par][:, 0:ni], qxT[:, dc, :],
                            qwu[:, dc, i0:i0 + ni],
                            start=(dc == 0), stop=(dc == DC - 1))
                    P.inc(ins, "s_pe", ("pu", j, ib))
                    if ib == 0 and j >= 1:
                        emit_down(j - 1, 1)
                for g in range(NGD):
                    ics = list(range(g * 4, min(IC, g * 4 + 4)))
                    P.wait(tensor, ("qhg", j, g // 2))
                    if g == 0:
                        pg2 = ("qhTe", j - 1, NGD - 2) if j >= 1 else None
                    elif g == 1:
                        pg2 = ("qxTe", j)
                    else:
                        pg2 = ("qhTe", j, g - 2)
                    if pg2 is not None:
                        P.wait(tensor, pg2)
                    pt = psq[1] if g % 2 == 0 else psq[0]
                    for ici in ics:
                        ins = tensor.matmul(
                            pt[:, (ici - g * 4) * 128:(ici - g * 4 + 1) * 128],
                            qh[:, ici * 128:(ici + 1) * 128], identb[:],
                            is_transpose=True,
                            start=(ici == ics[0]), stop=(ici == ics[-1]))
                    P.inc(ins, "s_pe", ("qhT", j, g))
                    if g == 3:
                        emit_down(j, 0)
                if j + 1 < NCH_B:
                    emit_qxT(j + 1)
            emit_down(NCH_B - 1, 1)

        @blk.vector
        def _(vector):
            def _spv():
                vector.tensor_copy(spcC[:], spcA[:])

            def emit_hm(j, hg):
                h0 = HG[hg][0]
                ta = taxs[(hg + 1) % 2]
                for ib in (2 * hg, 2 * hg + 1):
                    par = ib % 2
                    i0, ni = IBS[ib]
                    P.wait(vector, ("silu", j, ib))
                    P.wait(vector, ("pu", j, ib))
                    ins = vector.tensor_tensor(
                        out=ta[:, i0 - h0:i0 - h0 + ni],
                        in0=sg_sb[par][:, 0:ni],
                        in1=psu[par][:, 0:ni], op=ALU.mult)
                    P.inc(ins, "s_dve", ("hb", j, ib))

            def emit_resid(j, nb):
                i = j % 3
                P.wait(vector, ("pd", j, nb))
                ins = vector.tensor_tensor(
                    out=x2c[i][:, nb * 512:(nb + 1) * 512],
                    in0=psd[nb][:],
                    in1=x2c[i][:, nb * 512:(nb + 1) * 512], op=ALU.add)
                P.inc(ins, "s_dve", ("resid2nb", j, nb))

            def emit_xq(jj):
                P.wait(vector, ("x2c", jj))
                _q_prex(P, vector, taxX[:, 0:D], D, m16x[:], scx[:], rcx[:],
                        schx[:], tx[:, 0:D], ("absx", jj), ("tx", jj), _spv,
                        rstd_ap=rstd2a[:, jj:jj + 1])
                ins = _q_chainAB(vector, cAx[:], cBx[:], tx[:, 0:D])
                P.inc(ins, "s_dve", ("txd", jj))
                if jj >= 1:
                    P.wait(vector, ("qxT", jj - 1))
                _q_post(P, vector, cAx[:], cBx[:], sgx[:], schx[:], qx[:],
                        ("Sx", jj), ("qx", jj), su_key=("sux", jj))

            emit_xq(0)
            for j in range(NCH_B):
                b, rt = j // 16, j % 16
                g = b * 16 + rt
                i = j % 2
                if j + 1 < NCH_B:
                    emit_xq(j + 1)
                if j >= 1:
                    emit_resid(j - 1, 0)
                    emit_resid(j - 1, 1)
                emit_hm(j, 0)
                emit_hm(j, 1)
                for hg in range(3):
                    h0, ng = HG[hg]
                    nbh = ng // 16
                    sp = (hg + 1) % 2
                    ta, tb = taxs[sp], txs[sp]
                    if hg == 1:
                        emit_hm(j, 2)
                    P.wait(vector, ("habs", j, hg))
                    vector.tensor_reduce(
                        out=m16s[sp][:, 0:nbh].rearrange(
                            "p (nb one) -> p nb one", one=1),
                        in_=tb[:, 0:ng].rearrange("p (nb k) -> p nb k", k=16),
                        axis=AX.X, op=ALU.max)
                    _spv()
                    vector.tensor_scalar(out=scs[sp][:, 0:nbh],
                                         in0=m16s[sp][:, 0:nbh],
                                         scalar1=1.0 / 6.0, scalar2=1e-12,
                                         op0=ALU.mult, op1=ALU.max)
                    _spv()
                    vector.reciprocal(out=rcs[sp][:, 0:nbh],
                                      in_=scs[sp][:, 0:nbh])
                    vector.tensor_scalar_mul(schs[sp][:, 0:nbh],
                                             scs[sp][:, 0:nbh], 0.5)
                    _spv()
                    b0, b1 = broadcast_tensor_aps(
                        tb[:, 0:ng].rearrange("p (nb k) -> p nb k", k=16),
                        rcs[sp][:, 0:nbh].rearrange(
                            "p (nb one) -> p nb one", one=1))
                    ins = vector.tensor_tensor(
                        out=ta[:, 0:ng].rearrange("p (nb k) -> p nb k", k=16),
                        in0=b0, in1=b1, op=ALU.mult)
                    P.inc(ins, "s_dve", ("htx", j, hg))
                    ins = _q_chainAB(vector, cAs[sp][:, 0:ng],
                                     cBs[sp][:, 0:ng], ta[:, 0:ng])
                    P.inc(ins, "s_dve", ("hchd", j, hg))
                    _q_post(P, vector, cAs[sp][:, 0:ng], cBs[sp][:, 0:ng],
                            sgxs[sp][:, 0:ng], schs[sp][:, 0:nbh],
                            qh[:, h0:h0 + ng], None, ("qhg", j, hg),
                            su_key=("hsu", j, hg))
            emit_resid(NCH_B - 1, 0)
            emit_resid(NCH_B - 1, 1)


        @blk.scalar
        def _(scalar):
            def emit_outst(jj):
                bs, rts = jj // 16, jj % 16
                P.wait(scalar, ("resid2nb", jj, 1))
                sem_o = "s_st" if jj % 2 == 0 else "s_stB"
                st = scalar.dma_start(
                    out=out_h[bs, rts * 128:(rts + 1) * 128, :],
                    in_=x2c[jj % 3][:])
                st.then_inc(SEM[sem_o], 16)
                P.inc(None, sem_o, ("outst", jj), 16)

            def emit_absx(jj):
                P.wait(scalar, ("x2c", jj))
                if jj >= 1:
                    P.wait(scalar, ("tx", jj - 1))
                    if jj >= 2:
                        P.wait(scalar, ("hsu", jj - 2, 1))
                    else:
                        P.wait(scalar, ("sux", jj - 1))
                scalar.activation(out=taxX[:, 0:D], in_=x2c[jj % 3][:],
                                  func=AF.Abs, bias=nm2a[:, jj:jj + 1])
                ins = scalar.activation(out=sgx[:], in_=x2c[jj % 3][:],
                                        func=AF.Sign, bias=nm2a[:, jj:jj + 1])
                P.inc(ins, "s_act", ("absx", jj))

            def emit_qxTe(jj):
                P.wait(scalar, ("qxT", jj))
                ins = scalar.activation(
                    out=qxT[:].rearrange("p a b -> p (a b)"), in_=psq[0][:],
                    func=AF.Copy)
                P.inc(ins, "s_act", ("qxTe", jj))

            emit_absx(0)
            emit_absx(1)
            emit_qxTe(0)
            for j in range(NCH_B):
                b, rt = j // 16, j % 16
                g = b * 16 + rt
                i = j % 2
                for hg in range(3):
                    h0, ng = HG[hg]
                    for ib in (2 * hg, 2 * hg + 1):
                        par = ib % 2
                        ni = IBS[ib][1]
                        P.wait(scalar, ("pg", j, ib))
                        ins = scalar.activation(out=sg_sb[par][:, 0:ni],
                                                in_=psg[par][:, 0:ni],
                                                func=AF.Silu)
                        P.inc(ins, "s_act", ("silu", j, ib))
                    P.wait(scalar, ("hb", j, 2 * hg + 1))
                    if hg == 0:
                        if j >= 1:
                            P.wait(scalar, ("hchd", j - 1, 2))
                            P.wait(scalar, ("hsu", j - 1, 2))
                    elif hg == 1:
                        if j >= 1:
                            P.wait(scalar, ("hchd", j - 1, 1))
                        P.wait(scalar, ("sux", min(j + 1, NCH_B - 1)))
                        P.wait(scalar, ("txd", min(j + 1, NCH_B - 1)))
                    else:
                        P.wait(scalar, ("hchd", j, 0))
                        P.wait(scalar, ("hsu", j, 0))
                    sp = (hg + 1) % 2
                    scalar.activation(out=txs[sp][:, 0:ng],
                                      in_=taxs[sp][:, 0:ng], func=AF.Abs)
                    ins = scalar.activation(out=sgxs[sp][:, 0:ng],
                                            in_=taxs[sp][:, 0:ng],
                                            func=AF.Sign)
                    P.inc(ins, "s_act", ("habs", j, hg))
                    if hg == 0 and j >= 1:
                        emit_outst(j - 1)
                for gq in range(NGD):
                    ngc = min(IC, gq * 4 + 4) - gq * 4
                    P.wait(scalar, ("qhT", j, gq))
                    pt = psq[1] if gq % 2 == 0 else psq[0]
                    ins = scalar.activation(
                        out=qhT[:, gq * 4:gq * 4 + ngc, :].rearrange(
                            "p a b -> p (a b)"),
                        in_=pt[:, 0:ngc * 128], func=AF.Copy)
                    P.inc(ins, "s_act", ("qhTe", j, gq))
                    if gq == 3 and j + 2 < NCH_B:
                        emit_absx(j + 2)
                if j + 1 < NCH_B:
                    emit_qxTe(j + 1)
            emit_outst(NCH_B - 1)
            P.inc(None, "s_st", ("final_st",), 0)
            P.inc(None, "s_stB", ("final_stB",), 0)

    pb.close()
    outer.close()
    return nc, P.tok


def kernel(**inputs):
    x = np.ascontiguousarray(np.asarray(inputs["L0_attn_h_prev_layer"], dtype=np.float32))
    wz = np.ascontiguousarray(np.asarray(inputs["L0_attn_W_z"], dtype=np.float32))
    bz = np.ascontiguousarray(np.asarray(inputs["L0_attn_b_z"], dtype=np.float32))
    wh = np.ascontiguousarray(np.asarray(inputs["L0_attn_W_h"], dtype=np.float32))
    bh = np.ascontiguousarray(np.asarray(inputs["L0_attn_b_h"], dtype=np.float32))
    wg = np.ascontiguousarray(np.asarray(inputs["L0_ffn_w_gate"], dtype=np.float32))
    wu = np.ascontiguousarray(np.asarray(inputs["L0_ffn_w_up"], dtype=np.float32))
    wd = np.ascontiguousarray(np.asarray(inputs["L0_ffn_w_down"], dtype=np.float32))

    _, tok = build_program(None)
    nc, _ = build_program(tok)

    in_maps = [
        dict(x=x[c * B:(c + 1) * B], wz=wz, bz=bz, wh=wh, bh=bh,
             wg=wg, wu=wu, wd=wd)
        for c in range(NCORES)
    ]
    res = run_bass_kernel_spmd(nc, in_maps, core_ids=list(range(NCORES)))
    out = np.empty((BFULL, T, D), np.float32)
    for c in range(NCORES):
        out[c * B:(c + 1) * B] = np.asarray(res.results[c]["out"])
    return out


# version 34 (final)
# speedup vs baseline: 1.1084x; 1.0104x over previous
import numpy as np
from contextlib import ExitStack

import concourse.bass as bass
import concourse.mybir as mybir
from concourse.bass import broadcast_tensor_aps
from concourse.bass_utils import run_bass_kernel_spmd

F32 = mybir.dt.float32
BF16 = mybir.dt.bfloat16
AF = mybir.ActivationFunctionType
ALU = mybir.AluOpType
AX = mybir.AxisListType

NCORES = 8
BFULL, T, D, DI = 32, 2048, 1024, 2816
B = BFULL // NCORES
CH_A = 512
NCH_A = B * T // CH_A
RPC = CH_A // 128
NCH_B = B * T // 128
DC = D // 128
IC = DI // 128
NB_X = D // 16
EPS = 1e-5
NIB = (DI + 511) // 512
IBS = [(i * 512, min(512, DI - i * 512)) for i in range(NIB)]
NGD = (IC + 3) // 4


class Prog:

    def __init__(self, prev_tok):
        self.sems = {}
        self.tok = {}
        self.prev = prev_tok

    def add_sem(self, name, handle):
        self.sems[name] = [handle, 0]

    def inc(self, instr, semname, key=None, k=1):
        s = self.sems[semname]
        if instr is not None and k > 0:
            instr.then_inc(s[0], k)
        s[1] += k
        if key is not None:
            assert key not in self.tok, key
            self.tok[key] = (semname, s[1])

    def wait(self, eng, key):
        if self.prev is None:
            return
        semname, v = self.prev[key]
        if v > 0:
            eng.wait_ge(self.sems[semname][0], v)


def _q_pre(P, vec, src, n, m16, sc, rc, sch, tax, tx, abs_key, tx_key, sp):
    vec.tensor_reduce(
        out=m16.rearrange("p (nb one) -> p nb one", one=1),
        in_=src.rearrange("p (nb k) -> p nb k", k=16),
        axis=AX.X, op=ALU.max, apply_absolute_value=True)
    sp()
    vec.tensor_scalar(out=sc, in0=m16, scalar1=1.0 / 6.0, scalar2=1e-12,
                      op0=ALU.mult, op1=ALU.max)
    sp()
    vec.reciprocal(out=rc, in_=sc)
    vec.tensor_scalar_mul(sch, sc, 0.5)
    sp()
    P.wait(vec, abs_key)
    b0, b1 = broadcast_tensor_aps(
        tax.rearrange("p (nb k) -> p nb k", k=16),
        rc.rearrange("p (nb one) -> p nb one", one=1))
    ins = vec.tensor_tensor(out=tx.rearrange("p (nb k) -> p nb k", k=16),
                            in0=b0, in1=b1, op=ALU.mult)
    if tx_key is not None:
        P.inc(ins, "s_dve", tx_key)


def _q_prex(P, vec, tax, n, m16, sc, rc, sch, tx, abs_key, tx_key, sp,
            rstd_ap=None):
    P.wait(vec, abs_key)
    vec.tensor_reduce(
        out=m16.rearrange("p (nb one) -> p nb one", one=1),
        in_=tax.rearrange("p (nb k) -> p nb k", k=16),
        axis=AX.X, op=ALU.max)
    sp()
    vec.tensor_scalar(out=sc, in0=m16, scalar1=1.0 / 6.0, scalar2=1e-12,
                      op0=ALU.mult, op1=ALU.max)
    sp()
    vec.reciprocal(out=rc, in_=sc)
    if rstd_ap is not None:
        vec.tensor_scalar(out=sch, in0=sc, scalar1=rstd_ap, scalar2=0.5,
                          op0=ALU.mult, op1=ALU.mult)
    else:
        vec.tensor_scalar_mul(sch, sc, 0.5)
    sp()
    b0, b1 = broadcast_tensor_aps(
        tax.rearrange("p (nb k) -> p nb k", k=16),
        rc.rearrange("p (nb one) -> p nb one", one=1))
    ins = vec.tensor_tensor(out=tx.rearrange("p (nb k) -> p nb k", k=16),
                            in0=b0, in1=b1, op=ALU.mult)
    if tx_key is not None:
        P.inc(ins, "s_dve", tx_key)


def _q_chainAB(vec, cA, cB, tx):
    vec.tensor_scalar(out=cA, in0=tx, scalar1=0.25, scalar2=1.0,
                      op0=ALU.is_gt, op1=ALU.mult)
    vec.tensor_scalar(out=cB, in0=tx, scalar1=5.0, scalar2=2.0,
                      op0=ALU.is_gt, op1=ALU.mult)
    vec.scalar_tensor_tensor(out=cA, in0=tx, scalar=0.75, in1=cA,
                             op0=ALU.is_gt, op1=ALU.add)
    vec.scalar_tensor_tensor(out=cB, in0=tx, scalar=2.5, in1=cB,
                             op0=ALU.is_gt, op1=ALU.add)
    vec.scalar_tensor_tensor(out=cA, in0=tx, scalar=1.25, in1=cA,
                             op0=ALU.is_gt, op1=ALU.add)
    vec.scalar_tensor_tensor(out=cB, in0=tx, scalar=3.5, in1=cB,
                             op0=ALU.is_gt, op1=ALU.add)
    return vec.scalar_tensor_tensor(out=cA, in0=tx, scalar=1.75, in1=cA,
                                    op0=ALU.is_gt, op1=ALU.add)


def _q_chainA(vec, cA, tx):
    vec.tensor_scalar(out=cA, in0=tx, scalar1=0.25, scalar2=1.0,
                      op0=ALU.is_gt, op1=ALU.mult)
    for thr in (0.75, 1.25, 1.75):
        ins = vec.scalar_tensor_tensor(out=cA, in0=tx, scalar=thr, in1=cA,
                                       op0=ALU.is_gt, op1=ALU.add)
    return ins


def _q_chainB(eng, cB, tx):
    eng.tensor_scalar(out=cB, in0=tx, scalar1=5.0, scalar2=2.0,
                      op0=ALU.is_gt, op1=ALU.mult)
    for thr in (2.5, 3.5):
        ins = eng.scalar_tensor_tensor(out=cB, in0=tx, scalar=thr, in1=cB,
                                       op0=ALU.is_gt, op1=ALU.add)
    return ins


def _q_chainB_pool(eng, cB, tmp, tx):
    eng.tensor_single_scalar(out=tmp, in_=tx, scalar=2.5, op=ALU.is_gt)
    eng.tensor_scalar(out=cB, in0=tx, scalar1=5.0, scalar2=2.0,
                      op0=ALU.is_gt, op1=ALU.mult)
    eng.tensor_add(cB, cB, tmp)
    eng.tensor_single_scalar(out=tmp, in_=tx, scalar=3.5, op=ALU.is_gt)
    return eng.tensor_add(cB, cB, tmp)


def _q_post(P, vec, cA, cB, sgn, sch, q_out, s_key, q_key, su_key=None):
    ins = vec.scalar_tensor_tensor(out=cA, in0=cB, scalar=2.0, in1=cA,
                                   op0=ALU.mult, op1=ALU.add)
    if s_key is not None:
        P.inc(ins, "s_dve", s_key)
    ins = vec.tensor_mul(cA, cA, sgn)
    if su_key is not None:
        P.inc(ins, "s_dve", su_key)
    b0, b1 = broadcast_tensor_aps(
        cA.rearrange("p (nb k) -> p nb k", k=16),
        sch.rearrange("p (nb one) -> p nb one", one=1))
    ins = vec.tensor_tensor(out=q_out.rearrange("p (nb k) -> p nb k", k=16),
                            in0=b0, in1=b1, op=ALU.mult)
    if q_key is not None:
        P.inc(ins, "s_dve", q_key)


def _nr_rsqrt(vec, y, v, t1, iters=8):
    vec.tensor_scalar_max(t1, v, 1.0)
    vec.reciprocal(out=y, in_=t1)
    for _ in range(iters):
        vec.tensor_mul(t1, y, y)
        vec.tensor_mul(t1, t1, v)
        vec.tensor_scalar(out=t1, in0=t1, scalar1=-0.5, scalar2=1.5,
                          op0=ALU.mult, op1=ALU.add)
        vec.tensor_mul(y, y, t1)
    return


def build_program(prev_tok=None):
    import os as _osrd
    nc = bass.Bass("TRN2", target_bir_lowering=False, debug=False,
                   detect_race_conditions=bool(_osrd.environ.get("K_RACEDET")))
    P = Prog(prev_tok)

    x_h = nc.dram_tensor("x", [B, T, D], F32, kind="ExternalInput").ap()
    wz_h = nc.dram_tensor("wz", [D, D], F32, kind="ExternalInput").ap()
    bz_h = nc.dram_tensor("bz", [D], F32, kind="ExternalInput").ap()
    wh_h = nc.dram_tensor("wh", [D, D], F32, kind="ExternalInput").ap()
    bh_h = nc.dram_tensor("bh", [D], F32, kind="ExternalInput").ap()
    wg_h = nc.dram_tensor("wg", [DI, D], F32, kind="ExternalInput").ap()
    wu_h = nc.dram_tensor("wu", [DI, D], F32, kind="ExternalInput").ap()
    wd_h = nc.dram_tensor("wd", [D, DI], F32, kind="ExternalInput").ap()
    out_h = nc.dram_tensor("out", [B, T, D], F32, kind="ExternalOutput").ap()
    import os as _osd
    _dbg2 = _osd.environ.get("K_DBG2")
    if _dbg2:
        dbg2_h = nc.dram_tensor("dbg2", [128, 8, NCH_B], F32, kind="ExternalOutput").ap()
    _xkind = "ExternalOutput" if _osd.environ.get("K_DBGX2S") else "Internal"
    x2s_h = nc.dram_tensor("x2s", [B, T, D], F32, kind=_xkind).ap()
    _qkind = "ExternalOutput" if _osd.environ.get("K_DBGQW") else "Internal"
    qwg_h = nc.dram_tensor("qwg_s", [128, DC, DI], BF16, kind=_qkind).ap()
    qwu_h = nc.dram_tensor("qwu_s", [128, DC, DI], BF16, kind=_qkind).ap()
    qwd_h = nc.dram_tensor("qwd_s", [128, IC, D], BF16, kind=_qkind).ap()

    outer = ExitStack()
    osb = lambda name, shape, dt: outer.enter_context(nc.sbuf_tensor(name, shape, dt))

    identb = osb("identb", [128, 128], BF16)
    st2_sum = osb("st2_sum", [128, NCH_B], F32)
    st2_sq = osb("st2_sq", [128, NCH_B], F32)
    m2a = osb("m2a", [128, NCH_B], F32)
    rstd2a = osb("rstd2a", [128, NCH_B], F32)
    nm2a = osb("nm2a", [128, NCH_B], F32)
    qwg = osb("qwg", [128, DC, DI], BF16)
    spcA = osb("spcA", [128, 512], F32)
    spcC = osb("spcC", [128, 512], F32)
    carry = osb("carry", [128, DC, B], F32)
    span_pa = ExitStack()
    ssb = lambda name, shape, dt: span_pa.enter_context(nc.sbuf_tensor(name, shape, dt))
    spcB = ssb("spcB", [128, 512], F32)
    identf = ssb("identf", [128, 128], F32)
    st1_sum = ssb("st1_sum", [128, NCH_B], F32)
    st1_sq = ssb("st1_sq", [128, NCH_B], F32)
    m1a = ssb("m1a", [128, NCH_B], F32)
    rstd1a = ssb("rstd1a", [128, NCH_B], F32)
    sqscr2 = ssb("sqscr2", [128, NCH_B], F32)
    msqs = ssb("msqs", [128, NCH_B], F32)
    wzT = ssb("wzT", [128, DC, D], BF16)
    whT = ssb("whT", [128, DC, D], BF16)
    bz2 = ssb("bz2", [128, DC], F32)
    bh2 = ssb("bh2", [128, DC], F32)
    bh05 = ssb("bh05", [128, DC], F32)

    for name in ("s_ldx", "s_ldB", "s_ldQ", "s_ldS", "s_ldSB", "s_st",
                 "s_stB", "s_pe", "s_act", "s_dve", "s_pool", "s_dbg"):
        P.add_sem(name, outer.enter_context(nc.semaphore(name)))
    SEM = {k: v[0] for k, v in P.sems.items()}
    _dbgaw = _osd.environ.get("K_DBGAW")
    if _dbgaw:
        dbgw_h = nc.dram_tensor("dbgw", [128, 2, DC, D], BF16,
                                kind="ExternalOutput").ap()
        dbgb_h = nc.dram_tensor("dbgb", [128, 3, DC], F32,
                                kind="ExternalOutput").ap()
        dbgs_h = nc.dram_tensor("dbgs", [128, 6, NCH_B], F32,
                                kind="ExternalOutput").ap()
        dbgi_h = nc.dram_tensor("dbgi", [128, 128], BF16,
                                kind="ExternalOutput").ap()
    _dbgwz = _osd.environ.get("K_DBGWZ")
    if _dbgwz:
        dbgz_h = nc.dram_tensor("dbgz", [128, 2, 8192], F32,
                                kind="ExternalOutput").ap()

    gu_slices = []
    for wi in (0, 1):
        for s in range((IC + 3) // 4):
            gu_slices.append((wi, s * 4, min(4, IC - s * 4)))
    all_slices = list(gu_slices) + [(2, dcb, 0) for dcb in range(DC)]
    NSL = len(all_slices)

    def slice_n(si):
        wi = all_slices[si][0]
        return (all_slices[si][2] * D) if wi != 2 else DI

    prep = ExitStack()
    psb = lambda name, shape, dt: prep.enter_context(nc.sbuf_tensor(name, shape, dt))
    w32 = psb("w32", [128, DC * D], F32)
    wbf = psb("wbf", [128, DC * D], BF16)
    bbz = psb("bbz", [8, 128], F32)
    bbh = psb("bbh", [8, 128], F32)
    wsl = [psb(f"wsl{i}", [128, 4 * D], F32) for i in range(2)]
    _qwgflat = qwg[:].rearrange("p a b -> p (a b)")
    q_tax = _qwgflat.bitcast(F32)[:, 0:4 * D]
    q_tx = q_tax
    q_cA = _qwgflat[:, 2 * 4 * D:3 * 4 * D]
    q_cB = _qwgflat[:, 3 * 4 * D:4 * 4 * D]
    q_sgn = _qwgflat[:, 4 * 4 * D:5 * 4 * D]
    q_m16 = psb("q_m16", [128, 4 * D // 16], F32)
    q_sc = psb("q_sc", [128, 4 * D // 16], F32)
    q_rc = psb("q_rc", [128, 4 * D // 16], F32)
    q_sch = psb("q_sch", [128, 4 * D // 16], F32)
    qsl = psb("qsl", [128, 4 * D], BF16)
    stg = psb("stg", [128, DC, 512], BF16)
    stgd = psb("stgd", [128, IC, 128], BF16)
    xst = [w32[:, i * RPC * D:(i + 1) * RPC * D].rearrange(
        "p (r d) -> p r d", d=D) for i in range(2)]
    sqs = wbf[:].bitcast(F32).rearrange("p (r d) -> p r d", d=D)
    psum_w = [prep.enter_context(nc.psum_tensor(f"psw{i}", [128, 1024], BF16))
              for i in range(2)]
    psum_b = prep.enter_context(nc.psum_tensor("psb", [128, 8], F32))

    with nc.Block("prep") as blk:

        @blk.sync
        def _(sync):
            if _osd.environ.get("K_TBL1"):
                P.wait(sync, ("tbl1",))
            else:
                P.wait(sync, ("tbl0",))
            sync.dma_start(out=w32[:].rearrange("p (ec d) -> p ec d", d=D),
                           in_=wz_h.rearrange("(ec p) d -> p ec d", p=128)
                           ).then_inc(SEM["s_ldx"], 16)
            P.inc(None, "s_ldx", ("ld_wz",), 16)
            sync.dma_start(out=bbz[:], in_=bz_h.rearrange("(ec p) -> ec p", p=128)
                           ).then_inc(SEM["s_ldx"], 16)
            P.inc(None, "s_ldx", ("ld_bbz",), 16)
            sync.dma_start(out=bbh[:], in_=bh_h.rearrange("(ec p) -> ec p", p=128)
                           ).then_inc(SEM["s_ldx"], 16)
            P.inc(None, "s_ldx", ("ld_bbh",), 16)
            if _dbgwz:
                P.wait(sync, ("ld_wz",))
                sync.dma_start(out=dbgz_h[:, 0, :],
                               in_=w32[:]).then_inc(SEM["s_dbg"], 16)
            P.wait(sync, ("wzb",))
            if _dbgwz:
                sync.dma_start(out=dbgz_h[:, 1, 0:4096],
                               in_=wbf[:].bitcast(F32)).then_inc(SEM["s_dbg"], 16)
                sync.wait_ge(SEM["s_dbg"], 32)
            sync.dma_start(out=w32[:].rearrange("p (ec d) -> p ec d", d=D),
                           in_=wh_h.rearrange("(ec p) d -> p ec d", p=128)
                           ).then_inc(SEM["s_ldQ"], 16)
            P.inc(None, "s_ldQ", ("ld_wh",), 16)
            for si in range(NSL):
                wi = all_slices[si][0]
                sem_sl = "s_ldx" if si % 2 == 0 else "s_ldB"
                if si >= 2:
                    P.wait(sync, ("fabs", si - 2))
                    P.wait(sync, ("txp", si - 2))
                if wi != 2:
                    _, icb, nic = all_slices[si]
                    src = wg_h if wi == 0 else wu_h
                    sync.dma_start(
                        out=wsl[si % 2][:].rearrange("p (ic d) -> p ic d", d=D)[
                            :, 0:nic, :],
                        in_=src.rearrange("(ic p) d -> p ic d", p=128)[
                            :, icb:icb + nic, :],
                    ).then_inc(SEM[sem_sl], 16)
                else:
                    dcb = all_slices[si][1]
                    sync.dma_start(
                        out=wsl[si % 2][:, 0:DI],
                        in_=wd_h[dcb * 128:(dcb + 1) * 128, :],
                    ).then_inc(SEM[sem_sl], 16)
                P.inc(None, sem_sl, ("ld_sl", si), 16)

        @blk.gpsimd
        def _(gpsimd):
            def stats_loads():
                P.wait(gpsimd, ("whb",))
                for k in range(NCH_A):
                    b, c = k % B, k // B
                    sem_x = "s_ldS" if k % 2 == 0 else "s_ldSB"
                    if k >= 2:
                        P.wait(gpsimd, ("ssq", k - 2))
                    gpsimd.dma_start(
                        out=xst[k % 2],
                        in_=x_h[b, c * CH_A:(c + 1) * CH_A, :].rearrange(
                            "(r p) d -> p r d", p=128),
                    ).then_inc(SEM[sem_x], 16)
                    P.inc(None, sem_x, ("xs", k), 16)
            P.wait(gpsimd, ("tbl0",))
            gpsimd.memset(carry[:], 0.0)
            gpsimd.memset(spcA[:], 1.0)
            gpsimd.memset(spcB[:], 1.0)
            gpsimd.memset(spcC[:], 1.0)
            gpsimd.memset(identf[:], 1.0)
            gpsimd.affine_select(out=identf[:], in_=identf[:],
                                 pattern=[[-1, 128]], compare_op=ALU.is_equal,
                                 fill=0.0, base=0, channel_multiplier=1)
            ins = gpsimd.tensor_copy(identb[:], identf[:])
            P.inc(ins, "s_pool", ("ident",))
            stats_loads()

        @blk.tensor
        def _(tensor):
            psw_last = {0: None, 1: None}
            P.wait(tensor, ("ident",))
            if _osd.environ.get("K_WARM"):
                ins = tensor.matmul(psum_w[0][:, 0:128], identb[:], identb[:],
                                    is_transpose=True, start=True, stop=True)
                P.inc(ins, "s_pe", ("warmT",))
                psw_last[0] = ("warmE",)
            for wi in (0, 1):
                P.wait(tensor, ("wzb",) if wi == 0 else ("whb",))
                for dc in range(DC):
                    for h in range(2):
                        bank = (dc * 2 + h) % 2
                        if psw_last[bank] is not None:
                            P.wait(tensor, psw_last[bank])
                        for ec in range(4 * h, 4 * h + 4):
                            ins = tensor.matmul(
                                psum_w[bank][:, (ec - 4 * h) * 128:
                                             (ec - 4 * h + 1) * 128],
                                wbf[:].rearrange("p (ec d) -> p ec d", d=D)[
                                    :, ec, dc * 128:(dc + 1) * 128],
                                identb[:], is_transpose=True,
                                start=(ec == 4 * h), stop=(ec == 4 * h + 3))
                        P.inc(ins, "s_pe", ("wT", wi, dc, h))
                        psw_last[bank] = ("wTe", wi, dc, h)
            P.wait(tensor, ("ld_bbh",))
            ins = tensor.transpose(psum_b[:], bbz[:], identf[0:8, 0:8])
            P.inc(ins, "s_pe", ("bzT",))
            P.wait(tensor, ("bz_evac",))
            P.wait(tensor, ("ld_bbh",))
            ins = tensor.transpose(psum_b[:], bbh[:], identf[0:8, 0:8])
            P.inc(ins, "s_pe", ("bhT",))
            for si in range(NSL):
                wi = all_slices[si][0]
                P.wait(tensor, ("fq", si))
                if wi != 2:
                    nic = all_slices[si][2]
                    for dc in range(DC):
                        if psw_last[dc % 2] is not None:
                            P.wait(tensor, psw_last[dc % 2])
                        for ici in range(nic):
                            ins = tensor.matmul(
                                psum_w[dc % 2][:, ici * 128:(ici + 1) * 128],
                                qsl[:].rearrange("p (ic d) -> p ic d", d=D)[
                                    :, ici, dc * 128:(dc + 1) * 128],
                                identb[:], is_transpose=True,
                                start=(ici == 0), stop=(ici == nic - 1))
                        P.inc(ins, "s_pe", ("fT", si, dc))
                        psw_last[dc % 2] = ("fTe", si, dc)
                else:
                    for g in range(NGD):
                        ics = list(range(g * 4, min(IC, g * 4 + 4)))
                        if psw_last[g % 2] is not None:
                            P.wait(tensor, psw_last[g % 2])
                        for ici in ics:
                            ins = tensor.matmul(
                                psum_w[g % 2][:, (ici - g * 4) * 128:
                                              (ici - g * 4 + 1) * 128],
                                qsl[:, ici * 128:(ici + 1) * 128], identb[:],
                                is_transpose=True,
                                start=(ici == ics[0]), stop=(ici == ics[-1]))
                        P.inc(ins, "s_pe", ("fT", si, g))
                        psw_last[g % 2] = ("fTe", si, g)
            P.inc(tensor.nop(), "s_pe", ("bar0_pe",))

        @blk.vector
        def _(vector):
            def _spv():
                vector.tensor_copy(spcC[:], spcA[:])
            if _osd.environ.get("K_WARM"):
                P.wait(vector, ("warmT",))
                ins = vector.tensor_copy(spcC[:].bitcast(BF16)[:, 0:128],
                                         psum_w[0][:, 0:128])
                P.inc(ins, "s_dve", ("warmE",))
            P.wait(vector, ("ld_bbh",))
            ins = vector.tensor_copy(wbf[:], w32[:])
            P.inc(ins, "s_dve", ("wzb",))
            for dc in range(DC):
                for h in range(2):
                    bank = h % 2
                    P.wait(vector, ("wT", 0, dc, h))
                    ins = vector.tensor_copy(
                        wzT[:, dc, h * 512:(h + 1) * 512],
                        psum_w[bank][:, 0:512])
                    P.inc(ins, "s_dve", ("wTe", 0, dc, h))
            P.wait(vector, ("ld_wh",))
            ins = vector.tensor_copy(wbf[:], w32[:])
            P.inc(ins, "s_dve", ("whb",))
            for dc in range(DC):
                for h in range(2):
                    bank = h % 2
                    P.wait(vector, ("wT", 1, dc, h))
                    ins = vector.tensor_copy(
                        whT[:, dc, h * 512:(h + 1) * 512],
                        psum_w[bank][:, 0:512])
                    P.inc(ins, "s_dve", ("wTe", 1, dc, h))
            P.wait(vector, ("bzT",))
            ins = vector.tensor_scalar_mul(bz2[:], psum_b[:], 0.5)
            P.inc(ins, "s_dve", ("bz_evac",))
            P.wait(vector, ("bhT",))
            vector.tensor_scalar_mul(bh2[:], psum_b[:], 0.5)
            vector.tensor_scalar_add(bh05[:], psum_b[:], 0.5)
            for si in range(NSL):
                n = slice_n(si)
                nblk = n // 16
                wi = all_slices[si][0]
                P.wait(vector, ("ld_sl", si))
                src = wsl[si % 2][:, 0:n]
                _q_pre(P, vector, src, n, q_m16[:, 0:nblk], q_sc[:, 0:nblk],
                       q_rc[:, 0:nblk], q_sch[:, 0:nblk], q_tax[:, 0:n],
                       q_tx[:, 0:n], ("fabs", si), ("txp", si), _spv)
                _q_chainA(vector, q_cA[:, 0:n], q_tx[:, 0:n])
                _q_chainB(vector, q_cB[:, 0:n], q_tx[:, 0:n])
                _q_post(P, vector, q_cA[:, 0:n], q_cB[:, 0:n], q_sgn[:, 0:n],
                        q_sch[:, 0:nblk], qsl[:, 0:n], None, ("fq", si))
                if si >= 1:
                    P.wait(vector, ("fst", si - 1))
                if wi != 2:
                    nic = all_slices[si][2]
                    for dc in range(DC):
                        P.wait(vector, ("fT", si, dc))
                        ins = vector.tensor_copy(
                            stg[:, dc, 0:nic * 128],
                            psum_w[dc % 2][:, 0:nic * 128])
                        P.inc(ins, "s_dve", ("fTe", si, dc))
                else:
                    for g in range(NGD):
                        ngc = min(IC, g * 4 + 4) - g * 4
                        P.wait(vector, ("fT", si, g))
                        ins = vector.tensor_copy(
                            stgd[:, g * 4:g * 4 + ngc, :].rearrange(
                                "p a b -> p (a b)"),
                            psum_w[g % 2][:, 0:ngc * 128])
                        P.inc(ins, "s_dve", ("fTe", si, g))
            P.wait(vector, ("ssqF",))
            _spv()
            vector.tensor_scalar_mul(m1a[:], st1_sum[:], 1.0 / D)
            vector.tensor_scalar(out=sqscr2[:], in0=st1_sq[:], scalar1=1.0 / D,
                                 scalar2=EPS, op0=ALU.mult, op1=ALU.add)
            _spv()
            vector.tensor_mul(msqs[:], m1a[:], m1a[:])
            _spv()
            vector.memset(spcA[:, NCH_B:512], 1.0)
            vector.tensor_sub(spcA[:, 0:NCH_B], sqscr2[:], msqs[:])
            _spv()
            _nr_rsqrt(vector, spcB[:], spcA[:], spcC[:])
            _spv()
            ins = vector.tensor_copy(rstd1a[:], spcB[:, 0:NCH_B])
            P.inc(ins, "s_dve", ("rstd1",))
            P.inc(vector.engine_nop(), "s_dve", ("bar0_dve",))

        @blk.scalar
        def _(scalar):
            ins = scalar.preload_act_table(AF.Silu)
            P.inc(ins, "s_act", ("tbl0",))
            if _osd.environ.get("K_TBL1"):
                for _t1 in range(4):
                    ins = scalar.activation(out=w32[:, 0:2048],
                                            in_=w32[:, 0:2048], func=AF.Square)
                P.inc(ins, "s_act", ("tbl1",))
            for si in range(NSL):
                if si < NCH_A:
                    k = si
                    gt0 = (k % B) * 16 + (k // B) * RPC
                    P.wait(scalar, ("xs", k))
                    if k == 0:
                        P.wait(scalar, ("wTe", 1, DC - 1, 1))
                    for r in range(RPC):
                        scalar.activation(out=sqs[:, r, :],
                                          in_=xst[k % 2][:, r, :],
                                          func=AF.Identity,
                                          accum_out=st1_sum[:, gt0 + r:
                                                            gt0 + r + 1])
                        ins = scalar.activation(out=sqs[:, r, :],
                                                in_=xst[k % 2][:, r, :],
                                                func=AF.Square,
                                                accum_out=st1_sq[:, gt0 + r:
                                                                 gt0 + r + 1])
                    P.inc(ins, "s_act", ("ssq", k))
                n = slice_n(si)
                wi = all_slices[si][0]
                P.wait(scalar, ("ld_sl", si))
                if si >= 1:
                    P.wait(scalar, ("fq", si - 1))
                src = wsl[si % 2][:, 0:n]
                scalar.activation(out=q_tax[:, 0:n], in_=src, func=AF.Abs)
                ins = scalar.activation(out=q_sgn[:, 0:n], in_=src, func=AF.Sign)
                P.inc(ins, "s_act", ("fabs", si))
                if wi != 2:
                    _, icb, nic = all_slices[si]
                    dsth = qwg_h if wi == 0 else qwu_h
                    P.wait(scalar, ("fTe", si, DC - 1))
                    st = scalar.dma_start(
                        out=dsth[:, :, icb * 128:(icb + nic) * 128],
                        in_=stg[:, :, 0:nic * 128])
                else:
                    dcb = all_slices[si][1]
                    P.wait(scalar, ("fTe", si, NGD - 1))
                    st = scalar.dma_start(
                        out=qwd_h[:, :, dcb * 128:(dcb + 1) * 128], in_=stgd[:])
                st.then_inc(SEM["s_st"], 16)
                P.inc(None, "s_st", ("fst", si), 16)
            scalar.activation(out=sqs[:, 0, 0:NCH_B], in_=st1_sum[:],
                              func=AF.Identity)
            ins = scalar.activation(out=sqs[:, 0, 0:NCH_B], in_=st1_sq[:],
                                    func=AF.Identity)
            P.inc(ins, "s_act", ("ssqF",))
            P.inc(scalar.nop(), "s_act", ("bar0_act",))
            P.inc(None, "s_st", ("bar0_st",), 0)

    prep.close()

    pa = ExitStack()
    asb = lambda name, shape, dt: pa.enter_context(nc.sbuf_tensor(name, shape, dt))
    xin = [asb(f"xin{i}", [128, RPC, D], F32) for i in range(2)]
    x2 = [asb(f"x2_{i}", [128, RPC, D], F32) for i in range(2)]
    xln = asb("xln", [128, RPC, D], BF16)
    xlnT = asb("xlnT", [128, DC, CH_A], BF16)
    hsc = asb("hsc", [128, DC, CH_A], BF16)
    abuf = [asb(f"abuf{i}", [128, CH_A], F32) for i in range(2)]
    zbuf = [asb(f"zbuf{i}", [128, CH_A], F32) for i in range(2)]
    sbuf = [asb(f"sbuf{i}", [128, CH_A], F32) for i in range(2)]
    tbuf = [asb(f"tbuf{i}", [128, CH_A], F32) for i in range(2)]
    sqscr = asb("sqscr", [128, D], F32)
    psx = [pa.enter_context(nc.psum_tensor(f"psx{i}", [128, CH_A], BF16))
           for i in range(2)]
    psk = [pa.enter_context(nc.psum_tensor(f"psk{i}", [128, CH_A], F32))
           for i in range(2)]
    psh = [pa.enter_context(nc.psum_tensor(f"psh{i}", [128, CH_A], F32))
           for i in range(2)]
    psh2 = [pa.enter_context(nc.psum_tensor(f"psh2{i}", [128, D], BF16))
            for i in range(2)]

    with nc.Block("phA") as blk:

        @blk.sync
        def _(sync):
            for k in range(NCH_A):
                b, c = k % B, k // B
                sem_x = "s_ldx" if k % 2 == 0 else "s_ldB"
                if k >= 2:
                    P.wait(sync, ("resid_r", k - 2, RPC - 1))
                else:
                    P.wait(sync, ("rstd1",))
                sync.dma_start(
                    out=xin[k % 2][:],
                    in_=x_h[b, c * CH_A:(c + 1) * CH_A, :].rearrange(
                        "(r p) d -> p r d", p=128),
                ).then_inc(SEM[sem_x], 16)
                P.inc(None, sem_x, ("xin", k), 16)
            if _dbgaw:
                P.wait(sync, ("barA_pe",))
                P.wait(sync, ("barA_dve",))
                P.wait(sync, ("barA_act",))
                for di, src in enumerate((wzT, whT)):
                    sync.dma_start(out=dbgw_h[:, di],
                                   in_=src[:]).then_inc(SEM["s_dbg"], 16)
                for di, src in enumerate((bz2, bh2, bh05)):
                    sync.dma_start(out=dbgb_h[:, di],
                                   in_=src[:]).then_inc(SEM["s_dbg"], 16)
                for di, src in enumerate((m1a, rstd1a, m2a, rstd2a,
                                          st1_sum, st1_sq)):
                    sync.dma_start(out=dbgs_h[:, di],
                                   in_=src[:]).then_inc(SEM["s_dbg"], 16)
                sync.dma_start(out=dbgi_h[:],
                               in_=identb[:]).then_inc(SEM["s_dbg"], 16)
                sync.wait_ge(SEM["s_dbg"], 12 * 16)

        @blk.gpsimd
        def _(gpsimd):
            P.wait(gpsimd, ("bar0_st",))
            gpsimd.dma_start(out=qwg[:], in_=qwg_h).then_inc(SEM["s_ldS"], 16)
            P.inc(None, "s_ldS", ("ld_qwg",), 16)


        @blk.tensor
        def _(tensor):
            for k in range(NCH_A):
                P.wait(tensor, ("xln", k))
                for dc in range(DC):
                    pd = ("xTe", k, dc - 2) if dc >= 2 else (
                        ("xTe", k - 1, DC - 2 + dc) if k >= 1 else None)
                    if pd is not None:
                        P.wait(tensor, pd)
                    for r in range(RPC):
                        ins = tensor.matmul(
                            psx[dc % 2][:, r * 128:(r + 1) * 128],
                            xln[:, r, dc * 128:(dc + 1) * 128], identb[:],
                            is_transpose=True,
                            start=(r == 0), stop=(r == RPC - 1))
                    P.inc(ins, "s_pe", ("xT", k, dc))
                P.wait(tensor, ("xTe", k, DC - 1))
                for ec in range(DC):
                    par = ec % 2
                    pk = ("az", k, ec - 2) if ec >= 2 else (
                        ("az", k - 1, DC - 2 + ec) if k >= 1 else None)
                    if pk is not None:
                        P.wait(tensor, pk)
                    for dc in range(DC):
                        ins = tensor.matmul(
                            psk[par][:], wzT[:, dc, ec * 128:(ec + 1) * 128],
                            xlnT[:, dc, :],
                            start=(dc == 0), stop=(dc == DC - 1))
                    P.inc(ins, "s_pe", ("pk", k, ec))
                    ph = ("st1", k, ec - 2) if ec >= 2 else (
                        ("st1", k - 1, DC - 2 + ec) if k >= 1 else None)
                    if ph is not None:
                        P.wait(tensor, ph)
                    for dc in range(DC):
                        ins = tensor.matmul(
                            psh[par][:], whT[:, dc, ec * 128:(ec + 1) * 128],
                            xlnT[:, dc, :],
                            start=(dc == 0), stop=(dc == DC - 1))
                    P.inc(ins, "s_pe", ("ph", k, ec))
                P.wait(tensor, ("scan", k, DC - 1))
                for r in range(RPC):
                    pr = ("resid_r", k, r - 2) if r >= 2 else (
                        ("resid_r", k - 1, RPC - 2 + r) if k >= 1 else None)
                    if pr is not None:
                        P.wait(tensor, pr)
                    for ec in range(DC):
                        ins = tensor.matmul(
                            psh2[r % 2][:, ec * 128:(ec + 1) * 128],
                            hsc[:, ec, r * 128:(r + 1) * 128], identb[:],
                            is_transpose=True,
                            start=(ec == 0), stop=(ec == DC - 1))
                    P.inc(ins, "s_pe", ("h2", k, r))
            P.inc(tensor.nop(), "s_pe", ("barA_pe",))

        @blk.vector
        def _(vector):
            for k in range(NCH_A):
                b, c = k % B, k // B
                i = k % 2
                gt0 = b * 16 + c * RPC
                P.wait(vector, ("xin", k))
                if k == 0:
                    P.wait(vector, ("ident",))
                    if _dbg2:
                        P.wait(vector, ("d2", 6))
                for r in range(RPC):
                    ins = vector.tensor_scalar(
                        out=xln[:, r, :], in0=xin[i][:, r, :],
                        scalar1=m1a[:, gt0 + r:gt0 + r + 1],
                        scalar2=rstd1a[:, gt0 + r:gt0 + r + 1],
                        op0=ALU.subtract, op1=ALU.mult)
                P.inc(ins, "s_dve", ("xln", k))
                for ec in range(DC):
                    par = ec % 2
                    P.wait(vector, ("st1", k, ec))
                    vector.tensor_scalar(out=abuf[par][:], in0=abuf[par][:],
                                         scalar1=-0.5, scalar2=0.5,
                                         op0=ALU.mult, op1=ALU.add)
                    vector.tensor_scalar(out=sbuf[par][:], in0=sbuf[par][:],
                                         scalar1=0.5, scalar2=0.5,
                                         op0=ALU.mult, op1=ALU.add)
                    vector.tensor_max(tbuf[par][:], sbuf[par][:], tbuf[par][:])
                    vector.tensor_scalar(out=zbuf[par][:], in0=abuf[par][:],
                                         scalar1=-1.0, scalar2=1.0,
                                         op0=ALU.mult, op1=ALU.add)
                    vector.tensor_mul(zbuf[par][:], zbuf[par][:], tbuf[par][:])
                    ins = vector.tensor_tensor_scan(
                        out=hsc[:, ec, :], data0=abuf[par][:], data1=zbuf[par][:],
                        initial=carry[:, ec, b:b + 1], op0=ALU.mult, op1=ALU.add)
                    P.inc(ins, "s_dve", ("scan", k, ec))
                vector.tensor_copy(spcC[:], spcA[:])
                for ec in range(DC):
                    vector.tensor_copy(carry[:, ec, b:b + 1],
                                       hsc[:, ec, CH_A - 1:CH_A])
                for r in range(RPC):
                    P.wait(vector, ("h2", k, r))
                    if k >= 2 and r == 0:
                        P.wait(vector, ("x2st", k - 2))
                        P.wait(vector, ("ss2", k - 2))
                    ins = vector.tensor_tensor(
                        out=x2[i][:, r, :], in0=psh2[r % 2][:],
                        in1=xin[i][:, r, :], op=ALU.add)
                    P.inc(ins, "s_dve", ("resid_r", k, r))
            def _sp():
                vector.tensor_copy(spcC[:], spcA[:])
            P.wait(vector, ("ss2F",))
            _sp()
            vector.tensor_scalar_mul(m2a[:], st2_sum[:], 1.0 / D)
            vector.tensor_scalar(out=sqscr2[:], in0=st2_sq[:], scalar1=1.0 / D,
                                 scalar2=EPS, op0=ALU.mult, op1=ALU.add)
            _sp()
            vector.tensor_mul(msqs[:], m2a[:], m2a[:])
            _sp()
            vector.memset(spcA[:, NCH_B:512], 1.0)
            vector.tensor_sub(spcA[:, 0:NCH_B], sqscr2[:], msqs[:])
            _sp()
            _nr_rsqrt(vector, spcB[:], spcA[:], spcC[:])
            _sp()
            vector.tensor_copy(rstd2a[:], spcB[:, 0:NCH_B])
            vector.tensor_scalar_mul(nm2a[:], m2a[:], -1.0)
            _sp()
            P.inc(vector.engine_nop(), "s_dve", ("barA_dve",))

        @blk.scalar
        def _(scalar):
            for k in range(NCH_A):
                b, c = k % B, k // B
                i = k % 2
                P.wait(scalar, ("xin", k))
                for dc in range(DC):
                    P.wait(scalar, ("xT", k, dc))
                    ins = scalar.activation(out=xlnT[:, dc, :],
                                            in_=psx[dc % 2][:], func=AF.Copy)
                    P.inc(ins, "s_act", ("xTe", k, dc))

                for ec in range(DC):
                    par = ec % 2
                    P.wait(scalar, ("pk", k, ec))
                    ps = ("scan", k, ec - 2) if ec >= 2 else (
                        ("scan", k - 1, DC - 2 + ec) if k >= 1 else None)
                    if ps is not None:
                        P.wait(scalar, ps)
                    ins = scalar.activation(out=abuf[par][:], in_=psk[par][:],
                                            func=AF.Tanh, scale=0.5,
                                            bias=bz2[:, ec:ec + 1])
                    P.inc(ins, "s_act", ("az", k, ec))
                    P.wait(scalar, ("ph", k, ec))
                    scalar.activation(out=sbuf[par][:], in_=psh[par][:],
                                      func=AF.Tanh, scale=0.5,
                                      bias=bh2[:, ec:ec + 1])
                    ins = scalar.activation(out=tbuf[par][:], in_=psh[par][:],
                                            func=AF.Identity,
                                            bias=bh05[:, ec:ec + 1])
                    P.inc(ins, "s_act", ("st1", k, ec))
                P.wait(scalar, ("resid_r", k, RPC - 1))
                gt0s = (k % B) * 16 + (k // B) * RPC
                for r in range(RPC):
                    scalar.activation(
                        out=sqscr[:], in_=x2[i][:, r, :],
                        func=AF.Identity,
                        accum_out=st2_sum[:, gt0s + r:gt0s + r + 1])
                    ins = scalar.activation(
                        out=sqscr[:], in_=x2[i][:, r, :], func=AF.Square,
                        accum_out=st2_sq[:, gt0s + r:gt0s + r + 1])
                P.inc(ins, "s_act", ("ss2", k))
                sem_s = "s_st" if k % 2 == 0 else "s_stB"
                st = scalar.dma_start(
                    out=x2s_h[b, c * CH_A:(c + 1) * CH_A, :].rearrange(
                        "(r p) d -> p r d", p=128),
                    in_=x2[i][:])
                st.then_inc(SEM[sem_s], 16)
                P.inc(None, sem_s, ("x2st", k), 16)
            scalar.activation(out=sqscr[:, 0:NCH_B], in_=st2_sum[:],
                              func=AF.Identity)
            ins = scalar.activation(out=sqscr[:, 0:NCH_B], in_=st2_sq[:],
                                    func=AF.Identity)
            P.inc(ins, "s_act", ("ss2F",))
            P.inc(scalar.nop(), "s_act", ("barA_act",))
            P.inc(None, "s_st", ("barA_st",), 0)
            P.inc(None, "s_stB", ("barA_stB",), 0)

    pa.close()
    span_pa.close()

    pb = ExitStack()
    bsb = lambda name, shape, dt: pb.enter_context(nc.sbuf_tensor(name, shape, dt))
    qwu = bsb("qwu", [128, DC, DI], BF16)
    qwd = bsb("qwd", [128, IC, D], BF16)
    x2c = [bsb(f"x2c{i}", [128, D], F32) for i in range(3)]
    tax = bsb("tax", [128, D], F32)
    tx = bsb("tx", [128, D], F32)
    cAx = bsb("cAx", [128, D], BF16)
    cBx = bsb("cBx", [128, D], BF16)
    sgx = bsb("sgx", [128, D], BF16)
    qx = bsb("qx", [128, D], BF16)
    qxT = bsb("qxT", [128, DC, 128], BF16)
    m16x = bsb("m16x", [128, NB_X], F32)
    scx = bsb("scx", [128, NB_X], F32)
    rcx = bsb("rcx", [128, NB_X], F32)
    schx = bsb("schx", [128, NB_X], F32)
    sg_sb = [bsb(f"sg_sb{i}", [128, 512], F32) for i in range(2)]
    HG = [(0, 1024), (1024, 1024), (2048, DI - 2048)]
    tax2 = bsb("tax2", [128, 1024], F32)
    tx2 = bsb("tx2", [128, 1024], F32)
    cAx2 = bsb("cAx2", [128, 1024], BF16)
    cBx2 = bsb("cBx2", [128, 1024], BF16)
    sgx2 = bsb("sgx2", [128, 1024], BF16)
    m16x2 = bsb("m16x2", [128, NB_X], F32)
    scx2 = bsb("scx2", [128, NB_X], F32)
    rcx2 = bsb("rcx2", [128, NB_X], F32)
    schx2 = bsb("schx2", [128, NB_X], F32)
    qh = bsb("qh", [128, DI], BF16)
    taxs, txs, sgxs = [tax, tax2], [tx, tx2], [sgx, sgx2]
    cAs, cBs = [cAx, cAx2], [cBx, cBx2]
    m16s, scs, rcs, schs = ([m16x, m16x2], [scx, scx2],
                            [rcx, rcx2], [schx, schx2])
    taxX = bsb("taxX", [128, 1024], F32)
    qhT = bsb("qhT", [128, IC, 128], BF16)
    psq = [pb.enter_context(nc.psum_tensor(f"psq{i}", [128, 1024], BF16))
           for i in range(2)]
    psg = [pb.enter_context(nc.psum_tensor(f"psg{i}", [128, 512], F32))
           for i in range(2)]
    psu = [pb.enter_context(nc.psum_tensor(f"psu{i}", [128, 512], F32))
           for i in range(2)]
    psd = [pb.enter_context(nc.psum_tensor(f"psd{i}", [128, 512], F32))
           for i in range(2)]

    with nc.Block("phB") as blk:

        @blk.sync
        def _(sync):
            P.wait(sync, ("barA_pe",))
            P.wait(sync, ("barA_dve",))
            P.wait(sync, ("barA_st",))
            P.wait(sync, ("barA_stB",))
            for nm, hbm, dst in (("qwu", qwu_h, qwu),
                                 ("qwd", qwd_h, qwd)):
                sync.dma_start(out=dst[:], in_=hbm).then_inc(SEM["s_ldQ"], 16)
                P.inc(None, "s_ldQ", ("ld_" + nm,), 16)
            for j in range(NCH_B):
                b, rt = j // 16, j % 16
                sem_x = "s_ldx" if j % 2 == 0 else "s_ldB"
                if j >= 3:
                    P.wait(sync, ("outst", j - 3))
                sync.dma_start(
                    out=x2c[j % 3][:],
                    in_=x2s_h[b, rt * 128:(rt + 1) * 128, :],
                ).then_inc(SEM[sem_x], 16)
                P.inc(None, sem_x, ("x2c", j), 16)
            P.wait(sync, ("final_st",))
            P.wait(sync, ("final_stB",))

        @blk.tensor
        def _(tensor):
            def emit_qxT(jj):
                P.wait(tensor, ("qx", jj))
                if jj >= 1:
                    P.wait(tensor, ("qhTe", jj - 1, NGD - 1))
                for dc in range(DC):
                    ins = tensor.matmul(
                        psq[0][:, dc * 128:(dc + 1) * 128],
                        qx[:, dc * 128:(dc + 1) * 128], identb[:],
                        is_transpose=True,
                        start=(dc == 0), stop=(dc == DC - 1))
                P.inc(ins, "s_pe", ("qxT", jj))

            def emit_down(jj, part):
                ic0, ic1 = (0, 16) if part == 0 else (16, IC)
                P.wait(tensor, ("qhTe", jj, 3 if part == 0 else NGD - 1))
                if jj == 0 and part == 0:
                    P.wait(tensor, ("ld_qwd",))
                for nb in range(2):
                    if part == 0 and jj >= 1:
                        P.wait(tensor, ("resid2nb", jj - 1, nb))
                    for ici in range(ic0, ic1):
                        ins = tensor.matmul(
                            psd[nb][:], qhT[:, ici, :],
                            qwd[:, ici, nb * 512:(nb + 1) * 512],
                            start=(ici == 0), stop=(ici == IC - 1))
                    if part == 1:
                        P.inc(ins, "s_pe", ("pd", jj, nb))

            P.wait(tensor, ("ld_qwg",))
            emit_qxT(0)
            for j in range(NCH_B):
                P.wait(tensor, ("qxTe", j))
                for ib in range(NIB):
                    par = ib % 2
                    i0, ni = IBS[ib]
                    pg = ("silu", j, ib - 2) if ib >= 2 else (
                        ("silu", j - 1, NIB - 2 + ib) if j >= 1 else None)
                    if pg is not None:
                        P.wait(tensor, pg)
                    for dc in range(DC):
                        ins = tensor.matmul(
                            psg[par][:, 0:ni], qxT[:, dc, :],
                            qwg[:, dc, i0:i0 + ni],
                            start=(dc == 0), stop=(dc == DC - 1))
                    P.inc(ins, "s_pe", ("pg", j, ib))
                    pu = ("hb", j, ib - 2) if ib >= 2 else (
                        ("hb", j - 1, NIB - 2 + ib) if j >= 1 else None)
                    if pu is not None:
                        P.wait(tensor, pu)
                    if j == 0 and ib == 0:
                        P.wait(tensor, ("ld_qwu",))
                    for dc in range(DC):
                        ins = tensor.matmul(
                            psu[par][:, 0:ni], qxT[:, dc, :],
                            qwu[:, dc, i0:i0 + ni],
                            start=(dc == 0), stop=(dc == DC - 1))
                    P.inc(ins, "s_pe", ("pu", j, ib))
                    if ib == 0 and j >= 1:
                        emit_down(j - 1, 1)
                for g in range(NGD):
                    ics = list(range(g * 4, min(IC, g * 4 + 4)))
                    P.wait(tensor, ("qhg", j, g // 2))
                    if g == 0:
                        pg2 = ("qhTe", j - 1, NGD - 2) if j >= 1 else None
                    elif g == 1:
                        pg2 = ("qxTe", j)
                    else:
                        pg2 = ("qhTe", j, g - 2)
                    if pg2 is not None:
                        P.wait(tensor, pg2)
                    pt = psq[1] if g % 2 == 0 else psq[0]
                    for ici in ics:
                        ins = tensor.matmul(
                            pt[:, (ici - g * 4) * 128:(ici - g * 4 + 1) * 128],
                            qh[:, ici * 128:(ici + 1) * 128], identb[:],
                            is_transpose=True,
                            start=(ici == ics[0]), stop=(ici == ics[-1]))
                    P.inc(ins, "s_pe", ("qhT", j, g))
                    if g == 3:
                        emit_down(j, 0)
                if j + 1 < NCH_B:
                    emit_qxT(j + 1)
            emit_down(NCH_B - 1, 1)

        @blk.vector
        def _(vector):
            def _spv():
                vector.tensor_copy(spcC[:], spcA[:])

            def emit_hm(j, hg):
                h0 = HG[hg][0]
                ta = taxs[(hg + 1) % 2]
                for ib in (2 * hg, 2 * hg + 1):
                    par = ib % 2
                    i0, ni = IBS[ib]
                    P.wait(vector, ("silu", j, ib))
                    P.wait(vector, ("pu", j, ib))
                    ins = vector.tensor_tensor(
                        out=ta[:, i0 - h0:i0 - h0 + ni],
                        in0=sg_sb[par][:, 0:ni],
                        in1=psu[par][:, 0:ni], op=ALU.mult)
                    P.inc(ins, "s_dve", ("hb", j, ib))

            def emit_resid(j, nb):
                i = j % 3
                P.wait(vector, ("pd", j, nb))
                ins = vector.tensor_tensor(
                    out=x2c[i][:, nb * 512:(nb + 1) * 512],
                    in0=psd[nb][:],
                    in1=x2c[i][:, nb * 512:(nb + 1) * 512], op=ALU.add)
                P.inc(ins, "s_dve", ("resid2nb", j, nb))

            def emit_xq(jj):
                P.wait(vector, ("x2c", jj))
                _q_prex(P, vector, taxX[:, 0:D], D, m16x[:], scx[:], rcx[:],
                        schx[:], tx[:, 0:D], ("absx", jj), ("tx", jj), _spv,
                        rstd_ap=rstd2a[:, jj:jj + 1])
                ins = _q_chainAB(vector, cAx[:], cBx[:], tx[:, 0:D])
                P.inc(ins, "s_dve", ("txd", jj))
                if jj >= 1:
                    P.wait(vector, ("qxT", jj - 1))
                _q_post(P, vector, cAx[:], cBx[:], sgx[:], schx[:], qx[:],
                        ("Sx", jj), ("qx", jj), su_key=("sux", jj))

            emit_xq(0)
            for j in range(NCH_B):
                b, rt = j // 16, j % 16
                g = b * 16 + rt
                i = j % 2
                if j + 1 < NCH_B:
                    emit_xq(j + 1)
                if j >= 1:
                    emit_resid(j - 1, 0)
                    emit_resid(j - 1, 1)
                emit_hm(j, 0)
                for hg in range(3):
                    h0, ng = HG[hg]
                    nbh = ng // 16
                    sp = (hg + 1) % 2
                    ta, tb = taxs[sp], txs[sp]
                    if hg == 1:
                        emit_hm(j, 2)
                    P.wait(vector, ("habs", j, hg))
                    vector.tensor_reduce(
                        out=m16s[sp][:, 0:nbh].rearrange(
                            "p (nb one) -> p nb one", one=1),
                        in_=tb[:, 0:ng].rearrange("p (nb k) -> p nb k", k=16),
                        axis=AX.X, op=ALU.max)
                    _spv()
                    vector.tensor_scalar(out=scs[sp][:, 0:nbh],
                                         in0=m16s[sp][:, 0:nbh],
                                         scalar1=1.0 / 6.0, scalar2=1e-12,
                                         op0=ALU.mult, op1=ALU.max)
                    _spv()
                    vector.reciprocal(out=rcs[sp][:, 0:nbh],
                                      in_=scs[sp][:, 0:nbh])
                    vector.tensor_scalar_mul(schs[sp][:, 0:nbh],
                                             scs[sp][:, 0:nbh], 0.5)
                    _spv()
                    b0, b1 = broadcast_tensor_aps(
                        tb[:, 0:ng].rearrange("p (nb k) -> p nb k", k=16),
                        rcs[sp][:, 0:nbh].rearrange(
                            "p (nb one) -> p nb one", one=1))
                    ins = vector.tensor_tensor(
                        out=ta[:, 0:ng].rearrange("p (nb k) -> p nb k", k=16),
                        in0=b0, in1=b1, op=ALU.mult)
                    P.inc(ins, "s_dve", ("htx", j, hg))
                    if hg == 0:
                        emit_hm(j, 1)
                    ins = _q_chainAB(vector, cAs[sp][:, 0:ng],
                                     cBs[sp][:, 0:ng], ta[:, 0:ng])
                    P.inc(ins, "s_dve", ("hchd", j, hg))
                    _q_post(P, vector, cAs[sp][:, 0:ng], cBs[sp][:, 0:ng],
                            sgxs[sp][:, 0:ng], schs[sp][:, 0:nbh],
                            qh[:, h0:h0 + ng], None, ("qhg", j, hg),
                            su_key=("hsu", j, hg))
            emit_resid(NCH_B - 1, 0)
            emit_resid(NCH_B - 1, 1)


        @blk.scalar
        def _(scalar):
            def emit_outst(jj):
                bs, rts = jj // 16, jj % 16
                P.wait(scalar, ("resid2nb", jj, 1))
                sem_o = "s_st" if jj % 2 == 0 else "s_stB"
                st = scalar.dma_start(
                    out=out_h[bs, rts * 128:(rts + 1) * 128, :],
                    in_=x2c[jj % 3][:])
                st.then_inc(SEM[sem_o], 16)
                P.inc(None, sem_o, ("outst", jj), 16)

            def emit_absx(jj):
                P.wait(scalar, ("x2c", jj))
                if jj >= 1:
                    P.wait(scalar, ("tx", jj - 1))
                    if jj >= 2:
                        P.wait(scalar, ("hsu", jj - 2, 1))
                    else:
                        P.wait(scalar, ("sux", jj - 1))
                scalar.activation(out=taxX[:, 0:D], in_=x2c[jj % 3][:],
                                  func=AF.Abs, bias=nm2a[:, jj:jj + 1])
                ins = scalar.activation(out=sgx[:], in_=x2c[jj % 3][:],
                                        func=AF.Sign, bias=nm2a[:, jj:jj + 1])
                P.inc(ins, "s_act", ("absx", jj))

            def emit_qxTe(jj):
                P.wait(scalar, ("qxT", jj))
                ins = scalar.activation(
                    out=qxT[:].rearrange("p a b -> p (a b)"), in_=psq[0][:],
                    func=AF.Copy)
                P.inc(ins, "s_act", ("qxTe", jj))

            emit_absx(0)
            emit_absx(1)
            emit_qxTe(0)
            for j in range(NCH_B):
                b, rt = j // 16, j % 16
                g = b * 16 + rt
                i = j % 2
                for hg in range(3):
                    h0, ng = HG[hg]
                    for ib in (2 * hg, 2 * hg + 1):
                        par = ib % 2
                        ni = IBS[ib][1]
                        P.wait(scalar, ("pg", j, ib))
                        ins = scalar.activation(out=sg_sb[par][:, 0:ni],
                                                in_=psg[par][:, 0:ni],
                                                func=AF.Silu)
                        P.inc(ins, "s_act", ("silu", j, ib))
                    P.wait(scalar, ("hb", j, 2 * hg + 1))
                    if hg == 0:
                        if j >= 1:
                            P.wait(scalar, ("hchd", j - 1, 2))
                            P.wait(scalar, ("hsu", j - 1, 2))
                    elif hg == 1:
                        if j >= 1:
                            P.wait(scalar, ("hchd", j - 1, 1))
                        P.wait(scalar, ("sux", min(j + 1, NCH_B - 1)))
                        P.wait(scalar, ("txd", min(j + 1, NCH_B - 1)))
                    else:
                        P.wait(scalar, ("hchd", j, 0))
                        P.wait(scalar, ("hsu", j, 0))
                    sp = (hg + 1) % 2
                    scalar.activation(out=txs[sp][:, 0:ng],
                                      in_=taxs[sp][:, 0:ng], func=AF.Abs)
                    ins = scalar.activation(out=sgxs[sp][:, 0:ng],
                                            in_=taxs[sp][:, 0:ng],
                                            func=AF.Sign)
                    P.inc(ins, "s_act", ("habs", j, hg))
                    if hg == 0 and j >= 1:
                        emit_outst(j - 1)
                for gq in range(NGD):
                    ngc = min(IC, gq * 4 + 4) - gq * 4
                    P.wait(scalar, ("qhT", j, gq))
                    pt = psq[1] if gq % 2 == 0 else psq[0]
                    ins = scalar.activation(
                        out=qhT[:, gq * 4:gq * 4 + ngc, :].rearrange(
                            "p a b -> p (a b)"),
                        in_=pt[:, 0:ngc * 128], func=AF.Copy)
                    P.inc(ins, "s_act", ("qhTe", j, gq))
                    if gq == 3 and j + 2 < NCH_B:
                        emit_absx(j + 2)
                if j + 1 < NCH_B:
                    emit_qxTe(j + 1)
            emit_outst(NCH_B - 1)
            P.inc(None, "s_st", ("final_st",), 0)
            P.inc(None, "s_stB", ("final_stB",), 0)

    pb.close()
    outer.close()
    return nc, P.tok


def kernel(**inputs):
    x = np.ascontiguousarray(np.asarray(inputs["L0_attn_h_prev_layer"], dtype=np.float32))
    wz = np.ascontiguousarray(np.asarray(inputs["L0_attn_W_z"], dtype=np.float32))
    bz = np.ascontiguousarray(np.asarray(inputs["L0_attn_b_z"], dtype=np.float32))
    wh = np.ascontiguousarray(np.asarray(inputs["L0_attn_W_h"], dtype=np.float32))
    bh = np.ascontiguousarray(np.asarray(inputs["L0_attn_b_h"], dtype=np.float32))
    wg = np.ascontiguousarray(np.asarray(inputs["L0_ffn_w_gate"], dtype=np.float32))
    wu = np.ascontiguousarray(np.asarray(inputs["L0_ffn_w_up"], dtype=np.float32))
    wd = np.ascontiguousarray(np.asarray(inputs["L0_ffn_w_down"], dtype=np.float32))

    _, tok = build_program(None)
    nc, _ = build_program(tok)

    in_maps = [
        dict(x=x[c * B:(c + 1) * B], wz=wz, bz=bz, wh=wh, bh=bh,
             wg=wg, wu=wu, wd=wd)
        for c in range(NCORES)
    ]
    res = run_bass_kernel_spmd(nc, in_maps, core_ids=list(range(NCORES)))
    out = np.empty((BFULL, T, D), np.float32)
    for c in range(NCORES):
        out[c * B:(c + 1) * B] = np.asarray(res.results[c]["out"])
    return out


# version 35
# speedup vs baseline: 1.1274x; 1.0171x over previous
import numpy as np
from contextlib import ExitStack

import concourse.bass as bass
import concourse.mybir as mybir
from concourse.bass import broadcast_tensor_aps
from concourse.bass_utils import run_bass_kernel_spmd

F32 = mybir.dt.float32
BF16 = mybir.dt.bfloat16
AF = mybir.ActivationFunctionType
ALU = mybir.AluOpType
AX = mybir.AxisListType

NCORES = 8
BFULL, T, D, DI = 32, 2048, 1024, 2816
B = BFULL // NCORES
CH_A = 512
NCH_A = B * T // CH_A
RPC = CH_A // 128
NCH_B = B * T // 128
DC = D // 128
IC = DI // 128
NB_X = D // 16
EPS = 1e-5
NIB = (DI + 511) // 512
IBS = [(i * 512, min(512, DI - i * 512)) for i in range(NIB)]
NGD = (IC + 3) // 4


class Prog:

    def __init__(self, prev_tok):
        self.sems = {}
        self.tok = {}
        self.prev = prev_tok

    def add_sem(self, name, handle):
        self.sems[name] = [handle, 0]

    def inc(self, instr, semname, key=None, k=1):
        s = self.sems[semname]
        if instr is not None and k > 0:
            instr.then_inc(s[0], k)
        s[1] += k
        if key is not None:
            assert key not in self.tok, key
            self.tok[key] = (semname, s[1])

    def wait(self, eng, key):
        if self.prev is None:
            return
        semname, v = self.prev[key]
        if v > 0:
            eng.wait_ge(self.sems[semname][0], v)


def _q_pre(P, vec, src, n, m16, sc, rc, sch, tax, tx, abs_key, tx_key, sp):
    vec.tensor_reduce(
        out=m16.rearrange("p (nb one) -> p nb one", one=1),
        in_=src.rearrange("p (nb k) -> p nb k", k=16),
        axis=AX.X, op=ALU.max, apply_absolute_value=True)
    sp()
    vec.tensor_scalar(out=sc, in0=m16, scalar1=1.0 / 6.0, scalar2=1e-12,
                      op0=ALU.mult, op1=ALU.max)
    sp()
    vec.reciprocal(out=rc, in_=sc)
    vec.tensor_scalar_mul(sch, sc, 0.5)
    sp()
    P.wait(vec, abs_key)
    b0, b1 = broadcast_tensor_aps(
        tax.rearrange("p (nb k) -> p nb k", k=16),
        rc.rearrange("p (nb one) -> p nb one", one=1))
    ins = vec.tensor_tensor(out=tx.rearrange("p (nb k) -> p nb k", k=16),
                            in0=b0, in1=b1, op=ALU.mult)
    if tx_key is not None:
        P.inc(ins, "s_dve", tx_key)


def _q_prex(P, vec, tax, n, m16, sc, rc, sch, tx, abs_key, tx_key, sp,
            rstd_ap=None):
    P.wait(vec, abs_key)
    vec.tensor_reduce(
        out=m16.rearrange("p (nb one) -> p nb one", one=1),
        in_=tax.rearrange("p (nb k) -> p nb k", k=16),
        axis=AX.X, op=ALU.max)
    sp()
    vec.tensor_scalar(out=sc, in0=m16, scalar1=1.0 / 6.0, scalar2=1e-12,
                      op0=ALU.mult, op1=ALU.max)
    sp()
    vec.reciprocal(out=rc, in_=sc)
    if rstd_ap is not None:
        vec.tensor_scalar(out=sch, in0=sc, scalar1=rstd_ap, scalar2=0.5,
                          op0=ALU.mult, op1=ALU.mult)
    else:
        vec.tensor_scalar_mul(sch, sc, 0.5)
    sp()
    b0, b1 = broadcast_tensor_aps(
        tax.rearrange("p (nb k) -> p nb k", k=16),
        rc.rearrange("p (nb one) -> p nb one", one=1))
    ins = vec.tensor_tensor(out=tx.rearrange("p (nb k) -> p nb k", k=16),
                            in0=b0, in1=b1, op=ALU.mult)
    if tx_key is not None:
        P.inc(ins, "s_dve", tx_key)


def _q_chainAB(vec, cA, cB, tx):
    vec.tensor_scalar(out=cA, in0=tx, scalar1=0.25, scalar2=1.0,
                      op0=ALU.is_gt, op1=ALU.mult)
    vec.tensor_scalar(out=cB, in0=tx, scalar1=5.0, scalar2=2.0,
                      op0=ALU.is_gt, op1=ALU.mult)
    vec.scalar_tensor_tensor(out=cA, in0=tx, scalar=0.75, in1=cA,
                             op0=ALU.is_gt, op1=ALU.add)
    vec.scalar_tensor_tensor(out=cB, in0=tx, scalar=2.5, in1=cB,
                             op0=ALU.is_gt, op1=ALU.add)
    vec.scalar_tensor_tensor(out=cA, in0=tx, scalar=1.25, in1=cA,
                             op0=ALU.is_gt, op1=ALU.add)
    vec.scalar_tensor_tensor(out=cB, in0=tx, scalar=3.5, in1=cB,
                             op0=ALU.is_gt, op1=ALU.add)
    return vec.scalar_tensor_tensor(out=cA, in0=tx, scalar=1.75, in1=cA,
                                    op0=ALU.is_gt, op1=ALU.add)


def _q_chainA(vec, cA, tx):
    vec.tensor_scalar(out=cA, in0=tx, scalar1=0.25, scalar2=1.0,
                      op0=ALU.is_gt, op1=ALU.mult)
    for thr in (0.75, 1.25, 1.75):
        ins = vec.scalar_tensor_tensor(out=cA, in0=tx, scalar=thr, in1=cA,
                                       op0=ALU.is_gt, op1=ALU.add)
    return ins


def _q_chainB(eng, cB, tx):
    eng.tensor_scalar(out=cB, in0=tx, scalar1=5.0, scalar2=2.0,
                      op0=ALU.is_gt, op1=ALU.mult)
    for thr in (2.5, 3.5):
        ins = eng.scalar_tensor_tensor(out=cB, in0=tx, scalar=thr, in1=cB,
                                       op0=ALU.is_gt, op1=ALU.add)
    return ins


def _q_chainB_pool(eng, cB, tmp, tx):
    eng.tensor_single_scalar(out=tmp, in_=tx, scalar=2.5, op=ALU.is_gt)
    eng.tensor_scalar(out=cB, in0=tx, scalar1=5.0, scalar2=2.0,
                      op0=ALU.is_gt, op1=ALU.mult)
    eng.tensor_add(cB, cB, tmp)
    eng.tensor_single_scalar(out=tmp, in_=tx, scalar=3.5, op=ALU.is_gt)
    return eng.tensor_add(cB, cB, tmp)


def _q_post(P, vec, cA, cB, sgn, sch, q_out, s_key, q_key, su_key=None):
    ins = vec.scalar_tensor_tensor(out=cA, in0=cB, scalar=2.0, in1=cA,
                                   op0=ALU.mult, op1=ALU.add)
    if s_key is not None:
        P.inc(ins, "s_dve", s_key)
    ins = vec.tensor_mul(cA, cA, sgn)
    if su_key is not None:
        P.inc(ins, "s_dve", su_key)
    b0, b1 = broadcast_tensor_aps(
        cA.rearrange("p (nb k) -> p nb k", k=16),
        sch.rearrange("p (nb one) -> p nb one", one=1))
    ins = vec.tensor_tensor(out=q_out.rearrange("p (nb k) -> p nb k", k=16),
                            in0=b0, in1=b1, op=ALU.mult)
    if q_key is not None:
        P.inc(ins, "s_dve", q_key)


def _nr_rsqrt(vec, y, v, t1, iters=8):
    vec.tensor_scalar_max(t1, v, 1.0)
    vec.reciprocal(out=y, in_=t1)
    for _ in range(iters):
        vec.tensor_mul(t1, y, y)
        vec.tensor_mul(t1, t1, v)
        vec.tensor_scalar(out=t1, in0=t1, scalar1=-0.5, scalar2=1.5,
                          op0=ALU.mult, op1=ALU.add)
        vec.tensor_mul(y, y, t1)
    return


def build_program(prev_tok=None):
    import os as _osrd
    nc = bass.Bass("TRN2", target_bir_lowering=False, debug=False,
                   detect_race_conditions=bool(_osrd.environ.get("K_RACEDET")))
    P = Prog(prev_tok)

    x_h = nc.dram_tensor("x", [B, T, D], F32, kind="ExternalInput").ap()
    wz_h = nc.dram_tensor("wz", [D, D], F32, kind="ExternalInput").ap()
    bz_h = nc.dram_tensor("bz", [D], F32, kind="ExternalInput").ap()
    wh_h = nc.dram_tensor("wh", [D, D], F32, kind="ExternalInput").ap()
    bh_h = nc.dram_tensor("bh", [D], F32, kind="ExternalInput").ap()
    wg_h = nc.dram_tensor("wg", [DI, D], F32, kind="ExternalInput").ap()
    wu_h = nc.dram_tensor("wu", [DI, D], F32, kind="ExternalInput").ap()
    wd_h = nc.dram_tensor("wd", [D, DI], F32, kind="ExternalInput").ap()
    out_h = nc.dram_tensor("out", [B, T, D], F32, kind="ExternalOutput").ap()
    import os as _osd
    _dbg2 = _osd.environ.get("K_DBG2")
    if _dbg2:
        dbg2_h = nc.dram_tensor("dbg2", [128, 8, NCH_B], F32, kind="ExternalOutput").ap()
    _xkind = "ExternalOutput" if _osd.environ.get("K_DBGX2S") else "Internal"
    x2s_h = nc.dram_tensor("x2s", [B, T, D], F32, kind=_xkind).ap()
    _qkind = "ExternalOutput" if _osd.environ.get("K_DBGQW") else "Internal"
    qwg_h = nc.dram_tensor("qwg_s", [128, DC, DI], BF16, kind=_qkind).ap()
    qwu_h = nc.dram_tensor("qwu_s", [128, DC, DI], BF16, kind=_qkind).ap()
    qwd_h = nc.dram_tensor("qwd_s", [128, IC, D], BF16, kind=_qkind).ap()

    outer = ExitStack()
    osb = lambda name, shape, dt: outer.enter_context(nc.sbuf_tensor(name, shape, dt))

    identb = osb("identb", [128, 128], BF16)
    st2_sum = osb("st2_sum", [128, NCH_B], F32)
    st2_sq = osb("st2_sq", [128, NCH_B], F32)
    m2a = osb("m2a", [128, NCH_B], F32)
    rstd2a = osb("rstd2a", [128, NCH_B], F32)
    nm2a = osb("nm2a", [128, NCH_B], F32)
    qwg = osb("qwg", [128, DC, DI], BF16)
    spcA = osb("spcA", [128, 512], F32)
    spcC = osb("spcC", [128, 512], F32)
    carry = osb("carry", [128, DC, B], F32)
    span_pa = ExitStack()
    ssb = lambda name, shape, dt: span_pa.enter_context(nc.sbuf_tensor(name, shape, dt))
    spcB = ssb("spcB", [128, 512], F32)
    identf = ssb("identf", [128, 128], F32)
    st1_sum = ssb("st1_sum", [128, NCH_B], F32)
    st1_sq = ssb("st1_sq", [128, NCH_B], F32)
    m1a = ssb("m1a", [128, NCH_B], F32)
    rstd1a = ssb("rstd1a", [128, NCH_B], F32)
    sqscr2 = ssb("sqscr2", [128, NCH_B], F32)
    msqs = ssb("msqs", [128, NCH_B], F32)
    wzT = ssb("wzT", [128, DC, D], BF16)
    whT = ssb("whT", [128, DC, D], BF16)
    bz2 = ssb("bz2", [128, DC], F32)
    bh2 = ssb("bh2", [128, DC], F32)
    bh05 = ssb("bh05", [128, DC], F32)

    for name in ("s_ldx", "s_ldB", "s_ldQ", "s_ldS", "s_ldSB", "s_st",
                 "s_stB", "s_pe", "s_act", "s_dve", "s_pool", "s_dbg"):
        P.add_sem(name, outer.enter_context(nc.semaphore(name)))
    SEM = {k: v[0] for k, v in P.sems.items()}
    _dbgaw = _osd.environ.get("K_DBGAW")
    if _dbgaw:
        dbgw_h = nc.dram_tensor("dbgw", [128, 2, DC, D], BF16,
                                kind="ExternalOutput").ap()
        dbgb_h = nc.dram_tensor("dbgb", [128, 3, DC], F32,
                                kind="ExternalOutput").ap()
        dbgs_h = nc.dram_tensor("dbgs", [128, 6, NCH_B], F32,
                                kind="ExternalOutput").ap()
        dbgi_h = nc.dram_tensor("dbgi", [128, 128], BF16,
                                kind="ExternalOutput").ap()
    _dbgwz = _osd.environ.get("K_DBGWZ")
    if _dbgwz:
        dbgz_h = nc.dram_tensor("dbgz", [128, 2, 8192], F32,
                                kind="ExternalOutput").ap()

    gu_slices = []
    for wi in (0, 1):
        for s in range((IC + 3) // 4):
            gu_slices.append((wi, s * 4, min(4, IC - s * 4)))
    all_slices = list(gu_slices) + [(2, dcb, 0) for dcb in range(DC)]
    NSL = len(all_slices)

    def slice_n(si):
        wi = all_slices[si][0]
        return (all_slices[si][2] * D) if wi != 2 else DI

    prep = ExitStack()
    psb = lambda name, shape, dt: prep.enter_context(nc.sbuf_tensor(name, shape, dt))
    w32 = psb("w32", [128, DC * D], F32)
    wbf = psb("wbf", [128, DC * D], BF16)
    bbz = psb("bbz", [8, 128], F32)
    bbh = psb("bbh", [8, 128], F32)
    wsl = [psb(f"wsl{i}", [128, 4 * D], F32) for i in range(2)]
    _qwgflat = qwg[:].rearrange("p a b -> p (a b)")
    q_tax = _qwgflat.bitcast(F32)[:, 0:4 * D]
    q_tx = q_tax
    q_cA = _qwgflat[:, 2 * 4 * D:3 * 4 * D]
    q_cB = _qwgflat[:, 3 * 4 * D:4 * 4 * D]
    q_sgn = _qwgflat[:, 4 * 4 * D:5 * 4 * D]
    q_m16 = psb("q_m16", [128, 4 * D // 16], F32)
    q_sc = psb("q_sc", [128, 4 * D // 16], F32)
    q_rc = psb("q_rc", [128, 4 * D // 16], F32)
    q_sch = psb("q_sch", [128, 4 * D // 16], F32)
    qsl = psb("qsl", [128, 4 * D], BF16)
    stg = psb("stg", [128, DC, 512], BF16)
    stgd = psb("stgd", [128, IC, 128], BF16)
    xst = [w32[:, i * RPC * D:(i + 1) * RPC * D].rearrange(
        "p (r d) -> p r d", d=D) for i in range(2)]
    sqs = wbf[:].bitcast(F32).rearrange("p (r d) -> p r d", d=D)
    psum_w = [prep.enter_context(nc.psum_tensor(f"psw{i}", [128, 1024], BF16))
              for i in range(2)]
    psum_b = prep.enter_context(nc.psum_tensor("psb", [128, 8], F32))

    with nc.Block("prep") as blk:

        @blk.sync
        def _(sync):
            if _osd.environ.get("K_TBL1"):
                P.wait(sync, ("tbl1",))
            else:
                P.wait(sync, ("tbl0",))
            sync.dma_start(out=w32[:].rearrange("p (ec d) -> p ec d", d=D),
                           in_=wz_h.rearrange("(ec p) d -> p ec d", p=128)
                           ).then_inc(SEM["s_ldx"], 16)
            P.inc(None, "s_ldx", ("ld_wz",), 16)
            sync.dma_start(out=bbz[:], in_=bz_h.rearrange("(ec p) -> ec p", p=128)
                           ).then_inc(SEM["s_ldx"], 16)
            P.inc(None, "s_ldx", ("ld_bbz",), 16)
            sync.dma_start(out=bbh[:], in_=bh_h.rearrange("(ec p) -> ec p", p=128)
                           ).then_inc(SEM["s_ldx"], 16)
            P.inc(None, "s_ldx", ("ld_bbh",), 16)
            if _dbgwz:
                P.wait(sync, ("ld_wz",))
                sync.dma_start(out=dbgz_h[:, 0, :],
                               in_=w32[:]).then_inc(SEM["s_dbg"], 16)
            P.wait(sync, ("wzb",))
            if _dbgwz:
                sync.dma_start(out=dbgz_h[:, 1, 0:4096],
                               in_=wbf[:].bitcast(F32)).then_inc(SEM["s_dbg"], 16)
                sync.wait_ge(SEM["s_dbg"], 32)
            sync.dma_start(out=w32[:].rearrange("p (ec d) -> p ec d", d=D),
                           in_=wh_h.rearrange("(ec p) d -> p ec d", p=128)
                           ).then_inc(SEM["s_ldQ"], 16)
            P.inc(None, "s_ldQ", ("ld_wh",), 16)
            for si in range(NSL):
                wi = all_slices[si][0]
                sem_sl = "s_ldx" if si % 2 == 0 else "s_ldB"
                if si >= 2:
                    P.wait(sync, ("fabs", si - 2))
                    P.wait(sync, ("txp", si - 2))
                if wi != 2:
                    _, icb, nic = all_slices[si]
                    src = wg_h if wi == 0 else wu_h
                    sync.dma_start(
                        out=wsl[si % 2][:].rearrange("p (ic d) -> p ic d", d=D)[
                            :, 0:nic, :],
                        in_=src.rearrange("(ic p) d -> p ic d", p=128)[
                            :, icb:icb + nic, :],
                    ).then_inc(SEM[sem_sl], 16)
                else:
                    dcb = all_slices[si][1]
                    sync.dma_start(
                        out=wsl[si % 2][:, 0:DI],
                        in_=wd_h[dcb * 128:(dcb + 1) * 128, :],
                    ).then_inc(SEM[sem_sl], 16)
                P.inc(None, sem_sl, ("ld_sl", si), 16)

        @blk.gpsimd
        def _(gpsimd):
            def stats_loads():
                P.wait(gpsimd, ("whb",))
                for k in range(NCH_A):
                    b, c = k % B, k // B
                    sem_x = "s_ldS" if k % 2 == 0 else "s_ldSB"
                    if k >= 2:
                        P.wait(gpsimd, ("ssq", k - 2))
                    gpsimd.dma_start(
                        out=xst[k % 2],
                        in_=x_h[b, c * CH_A:(c + 1) * CH_A, :].rearrange(
                            "(r p) d -> p r d", p=128),
                    ).then_inc(SEM[sem_x], 16)
                    P.inc(None, sem_x, ("xs", k), 16)
            P.wait(gpsimd, ("tbl0",))
            gpsimd.memset(carry[:], 0.0)
            gpsimd.memset(spcA[:], 1.0)
            gpsimd.memset(spcB[:], 1.0)
            gpsimd.memset(spcC[:], 1.0)
            gpsimd.memset(identf[:], 1.0)
            gpsimd.affine_select(out=identf[:], in_=identf[:],
                                 pattern=[[-1, 128]], compare_op=ALU.is_equal,
                                 fill=0.0, base=0, channel_multiplier=1)
            ins = gpsimd.tensor_copy(identb[:], identf[:])
            P.inc(ins, "s_pool", ("ident",))
            stats_loads()

        @blk.tensor
        def _(tensor):
            psw_last = {0: None, 1: None}
            P.wait(tensor, ("ident",))
            if _osd.environ.get("K_WARM"):
                ins = tensor.matmul(psum_w[0][:, 0:128], identb[:], identb[:],
                                    is_transpose=True, start=True, stop=True)
                P.inc(ins, "s_pe", ("warmT",))
                psw_last[0] = ("warmE",)
            for wi in (0, 1):
                P.wait(tensor, ("wzb",) if wi == 0 else ("whb",))
                for dc in range(DC):
                    for h in range(2):
                        bank = (dc * 2 + h) % 2
                        if psw_last[bank] is not None:
                            P.wait(tensor, psw_last[bank])
                        for ec in range(4 * h, 4 * h + 4):
                            ins = tensor.matmul(
                                psum_w[bank][:, (ec - 4 * h) * 128:
                                             (ec - 4 * h + 1) * 128],
                                wbf[:].rearrange("p (ec d) -> p ec d", d=D)[
                                    :, ec, dc * 128:(dc + 1) * 128],
                                identb[:], is_transpose=True,
                                start=(ec == 4 * h), stop=(ec == 4 * h + 3))
                        P.inc(ins, "s_pe", ("wT", wi, dc, h))
                        psw_last[bank] = ("wTe", wi, dc, h)
            P.wait(tensor, ("ld_bbh",))
            ins = tensor.transpose(psum_b[:], bbz[:], identf[0:8, 0:8])
            P.inc(ins, "s_pe", ("bzT",))
            P.wait(tensor, ("bz_evac",))
            P.wait(tensor, ("ld_bbh",))
            ins = tensor.transpose(psum_b[:], bbh[:], identf[0:8, 0:8])
            P.inc(ins, "s_pe", ("bhT",))
            for si in range(NSL):
                wi = all_slices[si][0]
                P.wait(tensor, ("fq", si))
                if wi != 2:
                    nic = all_slices[si][2]
                    for dc in range(DC):
                        if psw_last[dc % 2] is not None:
                            P.wait(tensor, psw_last[dc % 2])
                        for ici in range(nic):
                            ins = tensor.matmul(
                                psum_w[dc % 2][:, ici * 128:(ici + 1) * 128],
                                qsl[:].rearrange("p (ic d) -> p ic d", d=D)[
                                    :, ici, dc * 128:(dc + 1) * 128],
                                identb[:], is_transpose=True,
                                start=(ici == 0), stop=(ici == nic - 1))
                        P.inc(ins, "s_pe", ("fT", si, dc))
                        psw_last[dc % 2] = ("fTe", si, dc)
                else:
                    for g in range(NGD):
                        ics = list(range(g * 4, min(IC, g * 4 + 4)))
                        if psw_last[g % 2] is not None:
                            P.wait(tensor, psw_last[g % 2])
                        for ici in ics:
                            ins = tensor.matmul(
                                psum_w[g % 2][:, (ici - g * 4) * 128:
                                              (ici - g * 4 + 1) * 128],
                                qsl[:, ici * 128:(ici + 1) * 128], identb[:],
                                is_transpose=True,
                                start=(ici == ics[0]), stop=(ici == ics[-1]))
                        P.inc(ins, "s_pe", ("fT", si, g))
                        psw_last[g % 2] = ("fTe", si, g)
            P.inc(tensor.nop(), "s_pe", ("bar0_pe",))

        @blk.vector
        def _(vector):
            def _spv():
                vector.tensor_copy(spcC[:], spcA[:])
            if _osd.environ.get("K_WARM"):
                P.wait(vector, ("warmT",))
                ins = vector.tensor_copy(spcC[:].bitcast(BF16)[:, 0:128],
                                         psum_w[0][:, 0:128])
                P.inc(ins, "s_dve", ("warmE",))
            P.wait(vector, ("ld_bbh",))
            ins = vector.tensor_copy(wbf[:], w32[:])
            P.inc(ins, "s_dve", ("wzb",))
            for dc in range(DC):
                for h in range(2):
                    bank = h % 2
                    P.wait(vector, ("wT", 0, dc, h))
                    ins = vector.tensor_copy(
                        wzT[:, dc, h * 512:(h + 1) * 512],
                        psum_w[bank][:, 0:512])
                    P.inc(ins, "s_dve", ("wTe", 0, dc, h))
            P.wait(vector, ("ld_wh",))
            ins = vector.tensor_copy(wbf[:], w32[:])
            P.inc(ins, "s_dve", ("whb",))
            for dc in range(DC):
                for h in range(2):
                    bank = h % 2
                    P.wait(vector, ("wT", 1, dc, h))
                    ins = vector.tensor_copy(
                        whT[:, dc, h * 512:(h + 1) * 512],
                        psum_w[bank][:, 0:512])
                    P.inc(ins, "s_dve", ("wTe", 1, dc, h))
            P.wait(vector, ("bzT",))
            ins = vector.tensor_scalar_mul(bz2[:], psum_b[:], 0.5)
            P.inc(ins, "s_dve", ("bz_evac",))
            P.wait(vector, ("bhT",))
            vector.tensor_scalar_mul(bh2[:], psum_b[:], 0.5)
            vector.tensor_scalar_add(bh05[:], psum_b[:], 0.5)
            for si in range(NSL):
                n = slice_n(si)
                nblk = n // 16
                wi = all_slices[si][0]
                P.wait(vector, ("ld_sl", si))
                src = wsl[si % 2][:, 0:n]
                _q_pre(P, vector, src, n, q_m16[:, 0:nblk], q_sc[:, 0:nblk],
                       q_rc[:, 0:nblk], q_sch[:, 0:nblk], q_tax[:, 0:n],
                       q_tx[:, 0:n], ("fabs", si), ("txp", si), _spv)
                _q_chainA(vector, q_cA[:, 0:n], q_tx[:, 0:n])
                _q_chainB(vector, q_cB[:, 0:n], q_tx[:, 0:n])
                _q_post(P, vector, q_cA[:, 0:n], q_cB[:, 0:n], q_sgn[:, 0:n],
                        q_sch[:, 0:nblk], qsl[:, 0:n], None, ("fq", si))
                if si >= 1:
                    P.wait(vector, ("fst", si - 1))
                if wi != 2:
                    nic = all_slices[si][2]
                    for dc in range(DC):
                        P.wait(vector, ("fT", si, dc))
                        ins = vector.tensor_copy(
                            stg[:, dc, 0:nic * 128],
                            psum_w[dc % 2][:, 0:nic * 128])
                        P.inc(ins, "s_dve", ("fTe", si, dc))
                else:
                    for g in range(NGD):
                        ngc = min(IC, g * 4 + 4) - g * 4
                        P.wait(vector, ("fT", si, g))
                        ins = vector.tensor_copy(
                            stgd[:, g * 4:g * 4 + ngc, :].rearrange(
                                "p a b -> p (a b)"),
                            psum_w[g % 2][:, 0:ngc * 128])
                        P.inc(ins, "s_dve", ("fTe", si, g))
            P.wait(vector, ("ssqF",))
            _spv()
            vector.tensor_scalar_mul(m1a[:], st1_sum[:], 1.0 / D)
            vector.tensor_scalar(out=sqscr2[:], in0=st1_sq[:], scalar1=1.0 / D,
                                 scalar2=EPS, op0=ALU.mult, op1=ALU.add)
            _spv()
            vector.tensor_mul(msqs[:], m1a[:], m1a[:])
            _spv()
            vector.memset(spcA[:, NCH_B:512], 1.0)
            vector.tensor_sub(spcA[:, 0:NCH_B], sqscr2[:], msqs[:])
            _spv()
            _nr_rsqrt(vector, spcB[:], spcA[:], spcC[:])
            _spv()
            ins = vector.tensor_copy(rstd1a[:], spcB[:, 0:NCH_B])
            P.inc(ins, "s_dve", ("rstd1",))
            P.inc(vector.engine_nop(), "s_dve", ("bar0_dve",))

        @blk.scalar
        def _(scalar):
            ins = scalar.preload_act_table(AF.Silu)
            P.inc(ins, "s_act", ("tbl0",))
            if _osd.environ.get("K_TBL1"):
                for _t1 in range(4):
                    ins = scalar.activation(out=w32[:, 0:2048],
                                            in_=w32[:, 0:2048], func=AF.Square)
                P.inc(ins, "s_act", ("tbl1",))
            for si in range(NSL):
                if si < NCH_A:
                    k = si
                    gt0 = (k % B) * 16 + (k // B) * RPC
                    P.wait(scalar, ("xs", k))
                    if k == 0:
                        P.wait(scalar, ("wTe", 1, DC - 1, 1))
                    for r in range(RPC):
                        scalar.activation(out=sqs[:, r, :],
                                          in_=xst[k % 2][:, r, :],
                                          func=AF.Identity,
                                          accum_out=st1_sum[:, gt0 + r:
                                                            gt0 + r + 1])
                        ins = scalar.activation(out=sqs[:, r, :],
                                                in_=xst[k % 2][:, r, :],
                                                func=AF.Square,
                                                accum_out=st1_sq[:, gt0 + r:
                                                                 gt0 + r + 1])
                    P.inc(ins, "s_act", ("ssq", k))
                n = slice_n(si)
                wi = all_slices[si][0]
                P.wait(scalar, ("ld_sl", si))
                if si >= 1:
                    P.wait(scalar, ("fq", si - 1))
                src = wsl[si % 2][:, 0:n]
                scalar.activation(out=q_tax[:, 0:n], in_=src, func=AF.Abs)
                ins = scalar.activation(out=q_sgn[:, 0:n], in_=src, func=AF.Sign)
                P.inc(ins, "s_act", ("fabs", si))
                if wi != 2:
                    _, icb, nic = all_slices[si]
                    dsth = qwg_h if wi == 0 else qwu_h
                    P.wait(scalar, ("fTe", si, DC - 1))
                    st = scalar.dma_start(
                        out=dsth[:, :, icb * 128:(icb + nic) * 128],
                        in_=stg[:, :, 0:nic * 128])
                else:
                    dcb = all_slices[si][1]
                    P.wait(scalar, ("fTe", si, NGD - 1))
                    st = scalar.dma_start(
                        out=qwd_h[:, :, dcb * 128:(dcb + 1) * 128], in_=stgd[:])
                st.then_inc(SEM["s_st"], 16)
                P.inc(None, "s_st", ("fst", si), 16)
            scalar.activation(out=sqs[:, 0, 0:NCH_B], in_=st1_sum[:],
                              func=AF.Identity)
            ins = scalar.activation(out=sqs[:, 0, 0:NCH_B], in_=st1_sq[:],
                                    func=AF.Identity)
            P.inc(ins, "s_act", ("ssqF",))
            P.inc(scalar.nop(), "s_act", ("bar0_act",))
            P.inc(None, "s_st", ("bar0_st",), 0)

    prep.close()

    pa = ExitStack()
    asb = lambda name, shape, dt: pa.enter_context(nc.sbuf_tensor(name, shape, dt))
    xin = [asb(f"xin{i}", [128, RPC, D], F32) for i in range(2)]
    x2 = [asb(f"x2_{i}", [128, RPC, D], F32) for i in range(2)]
    xln = asb("xln", [128, RPC, D], BF16)
    xlnT = asb("xlnT", [128, DC, CH_A], BF16)
    hsc = asb("hsc", [128, DC, CH_A], BF16)
    abuf = [asb(f"abuf{i}", [128, CH_A], F32) for i in range(2)]
    zbuf = [asb(f"zbuf{i}", [128, CH_A], F32) for i in range(2)]
    sbuf = [asb(f"sbuf{i}", [128, CH_A], F32) for i in range(2)]
    tbuf = [asb(f"tbuf{i}", [128, CH_A], F32) for i in range(2)]
    sqscr = asb("sqscr", [128, D], F32)
    psx = [pa.enter_context(nc.psum_tensor(f"psx{i}", [128, CH_A], BF16))
           for i in range(2)]
    psk = [pa.enter_context(nc.psum_tensor(f"psk{i}", [128, CH_A], F32))
           for i in range(2)]
    psh = [pa.enter_context(nc.psum_tensor(f"psh{i}", [128, CH_A], F32))
           for i in range(2)]
    psh2 = [pa.enter_context(nc.psum_tensor(f"psh2{i}", [128, D], BF16))
            for i in range(2)]

    with nc.Block("phA") as blk:

        @blk.sync
        def _(sync):
            for k in range(NCH_A):
                b, c = k % B, k // B
                sem_x = "s_ldx" if k % 2 == 0 else "s_ldB"
                if k >= 2:
                    P.wait(sync, ("resid_r", k - 2, RPC - 1))
                else:
                    P.wait(sync, ("rstd1",))
                sync.dma_start(
                    out=xin[k % 2][:],
                    in_=x_h[b, c * CH_A:(c + 1) * CH_A, :].rearrange(
                        "(r p) d -> p r d", p=128),
                ).then_inc(SEM[sem_x], 16)
                P.inc(None, sem_x, ("xin", k), 16)
            if _dbgaw:
                P.wait(sync, ("barA_pe",))
                P.wait(sync, ("barA_dve",))
                P.wait(sync, ("barA_act",))
                for di, src in enumerate((wzT, whT)):
                    sync.dma_start(out=dbgw_h[:, di],
                                   in_=src[:]).then_inc(SEM["s_dbg"], 16)
                for di, src in enumerate((bz2, bh2, bh05)):
                    sync.dma_start(out=dbgb_h[:, di],
                                   in_=src[:]).then_inc(SEM["s_dbg"], 16)
                for di, src in enumerate((m1a, rstd1a, m2a, rstd2a,
                                          st1_sum, st1_sq)):
                    sync.dma_start(out=dbgs_h[:, di],
                                   in_=src[:]).then_inc(SEM["s_dbg"], 16)
                sync.dma_start(out=dbgi_h[:],
                               in_=identb[:]).then_inc(SEM["s_dbg"], 16)
                sync.wait_ge(SEM["s_dbg"], 12 * 16)

        @blk.gpsimd
        def _(gpsimd):
            P.wait(gpsimd, ("bar0_st",))
            gpsimd.dma_start(out=qwg[:], in_=qwg_h).then_inc(SEM["s_ldS"], 16)
            P.inc(None, "s_ldS", ("ld_qwg",), 16)


        @blk.tensor
        def _(tensor):
            for k in range(NCH_A):
                P.wait(tensor, ("xln", k))
                for dc in range(DC):
                    pd = ("xTe", k, dc - 2) if dc >= 2 else (
                        ("xTe", k - 1, DC - 2 + dc) if k >= 1 else None)
                    if pd is not None:
                        P.wait(tensor, pd)
                    for r in range(RPC):
                        ins = tensor.matmul(
                            psx[dc % 2][:, r * 128:(r + 1) * 128],
                            xln[:, r, dc * 128:(dc + 1) * 128], identb[:],
                            is_transpose=True,
                            start=(r == 0), stop=(r == RPC - 1))
                    P.inc(ins, "s_pe", ("xT", k, dc))
                P.wait(tensor, ("xTe", k, DC - 1))
                for ec in range(DC):
                    par = ec % 2
                    pk = ("az", k, ec - 2) if ec >= 2 else (
                        ("az", k - 1, DC - 2 + ec) if k >= 1 else None)
                    if pk is not None:
                        P.wait(tensor, pk)
                    for dc in range(DC):
                        ins = tensor.matmul(
                            psk[par][:], wzT[:, dc, ec * 128:(ec + 1) * 128],
                            xlnT[:, dc, :],
                            start=(dc == 0), stop=(dc == DC - 1))
                    P.inc(ins, "s_pe", ("pk", k, ec))
                    ph = ("st1", k, ec - 2) if ec >= 2 else (
                        ("st1", k - 1, DC - 2 + ec) if k >= 1 else None)
                    if ph is not None:
                        P.wait(tensor, ph)
                    for dc in range(DC):
                        ins = tensor.matmul(
                            psh[par][:], whT[:, dc, ec * 128:(ec + 1) * 128],
                            xlnT[:, dc, :],
                            start=(dc == 0), stop=(dc == DC - 1))
                    P.inc(ins, "s_pe", ("ph", k, ec))
                P.wait(tensor, ("scan", k, DC - 1))
                for r in range(RPC):
                    pr = ("resid_r", k, r - 2) if r >= 2 else (
                        ("resid_r", k - 1, RPC - 2 + r) if k >= 1 else None)
                    if pr is not None:
                        P.wait(tensor, pr)
                    for ec in range(DC):
                        ins = tensor.matmul(
                            psh2[r % 2][:, ec * 128:(ec + 1) * 128],
                            hsc[:, ec, r * 128:(r + 1) * 128], identb[:],
                            is_transpose=True,
                            start=(ec == 0), stop=(ec == DC - 1))
                    P.inc(ins, "s_pe", ("h2", k, r))
            P.inc(tensor.nop(), "s_pe", ("barA_pe",))

        @blk.vector
        def _(vector):
            for k in range(NCH_A):
                b, c = k % B, k // B
                i = k % 2
                gt0 = b * 16 + c * RPC
                P.wait(vector, ("xin", k))
                if k == 0:
                    P.wait(vector, ("ident",))
                    if _dbg2:
                        P.wait(vector, ("d2", 6))
                for r in range(RPC):
                    ins = vector.tensor_scalar(
                        out=xln[:, r, :], in0=xin[i][:, r, :],
                        scalar1=m1a[:, gt0 + r:gt0 + r + 1],
                        scalar2=rstd1a[:, gt0 + r:gt0 + r + 1],
                        op0=ALU.subtract, op1=ALU.mult)
                P.inc(ins, "s_dve", ("xln", k))
                for ec in range(DC):
                    par = ec % 2
                    P.wait(vector, ("st1", k, ec))
                    vector.tensor_scalar(out=abuf[par][:], in0=abuf[par][:],
                                         scalar1=-0.5, scalar2=0.5,
                                         op0=ALU.mult, op1=ALU.add)
                    vector.tensor_scalar(out=sbuf[par][:], in0=sbuf[par][:],
                                         scalar1=0.5, scalar2=0.5,
                                         op0=ALU.mult, op1=ALU.add)
                    vector.tensor_max(tbuf[par][:], sbuf[par][:], tbuf[par][:])
                    vector.tensor_scalar(out=zbuf[par][:], in0=abuf[par][:],
                                         scalar1=-1.0, scalar2=1.0,
                                         op0=ALU.mult, op1=ALU.add)
                    vector.tensor_mul(zbuf[par][:], zbuf[par][:], tbuf[par][:])
                    ins = vector.tensor_tensor_scan(
                        out=hsc[:, ec, :], data0=abuf[par][:], data1=zbuf[par][:],
                        initial=carry[:, ec, b:b + 1], op0=ALU.mult, op1=ALU.add)
                    P.inc(ins, "s_dve", ("scan", k, ec))
                vector.tensor_copy(spcC[:], spcA[:])
                for ec in range(DC):
                    vector.tensor_copy(carry[:, ec, b:b + 1],
                                       hsc[:, ec, CH_A - 1:CH_A])
                for r in range(RPC):
                    P.wait(vector, ("h2", k, r))
                    if k >= 2 and r == 0:
                        P.wait(vector, ("x2st", k - 2))
                        P.wait(vector, ("ss2", k - 2))
                    ins = vector.tensor_tensor(
                        out=x2[i][:, r, :], in0=psh2[r % 2][:],
                        in1=xin[i][:, r, :], op=ALU.add)
                    P.inc(ins, "s_dve", ("resid_r", k, r))
            def _sp():
                vector.tensor_copy(spcC[:], spcA[:])
            P.wait(vector, ("ss2F",))
            _sp()
            vector.tensor_scalar_mul(m2a[:], st2_sum[:], 1.0 / D)
            vector.tensor_scalar(out=sqscr2[:], in0=st2_sq[:], scalar1=1.0 / D,
                                 scalar2=EPS, op0=ALU.mult, op1=ALU.add)
            _sp()
            vector.tensor_mul(msqs[:], m2a[:], m2a[:])
            _sp()
            vector.memset(spcA[:, NCH_B:512], 1.0)
            vector.tensor_sub(spcA[:, 0:NCH_B], sqscr2[:], msqs[:])
            _sp()
            _nr_rsqrt(vector, spcB[:], spcA[:], spcC[:])
            _sp()
            vector.tensor_copy(rstd2a[:], spcB[:, 0:NCH_B])
            vector.tensor_scalar_mul(nm2a[:], m2a[:], -1.0)
            _sp()
            P.inc(vector.engine_nop(), "s_dve", ("barA_dve",))

        @blk.scalar
        def _(scalar):
            for k in range(NCH_A):
                b, c = k % B, k // B
                i = k % 2
                P.wait(scalar, ("xin", k))
                for dc in range(DC):
                    P.wait(scalar, ("xT", k, dc))
                    ins = scalar.activation(out=xlnT[:, dc, :],
                                            in_=psx[dc % 2][:], func=AF.Copy)
                    P.inc(ins, "s_act", ("xTe", k, dc))

                for ec in range(DC):
                    par = ec % 2
                    P.wait(scalar, ("pk", k, ec))
                    ps = ("scan", k, ec - 2) if ec >= 2 else (
                        ("scan", k - 1, DC - 2 + ec) if k >= 1 else None)
                    if ps is not None:
                        P.wait(scalar, ps)
                    ins = scalar.activation(out=abuf[par][:], in_=psk[par][:],
                                            func=AF.Tanh, scale=0.5,
                                            bias=bz2[:, ec:ec + 1])
                    P.inc(ins, "s_act", ("az", k, ec))
                    P.wait(scalar, ("ph", k, ec))
                    scalar.activation(out=sbuf[par][:], in_=psh[par][:],
                                      func=AF.Tanh, scale=0.5,
                                      bias=bh2[:, ec:ec + 1])
                    ins = scalar.activation(out=tbuf[par][:], in_=psh[par][:],
                                            func=AF.Identity,
                                            bias=bh05[:, ec:ec + 1])
                    P.inc(ins, "s_act", ("st1", k, ec))
                P.wait(scalar, ("resid_r", k, RPC - 1))
                gt0s = (k % B) * 16 + (k // B) * RPC
                for r in range(RPC):
                    scalar.activation(
                        out=sqscr[:], in_=x2[i][:, r, :],
                        func=AF.Identity,
                        accum_out=st2_sum[:, gt0s + r:gt0s + r + 1])
                    ins = scalar.activation(
                        out=sqscr[:], in_=x2[i][:, r, :], func=AF.Square,
                        accum_out=st2_sq[:, gt0s + r:gt0s + r + 1])
                P.inc(ins, "s_act", ("ss2", k))
                sem_s = "s_st" if k % 2 == 0 else "s_stB"
                st = scalar.dma_start(
                    out=x2s_h[b, c * CH_A:(c + 1) * CH_A, :].rearrange(
                        "(r p) d -> p r d", p=128),
                    in_=x2[i][:])
                st.then_inc(SEM[sem_s], 16)
                P.inc(None, sem_s, ("x2st", k), 16)
            scalar.activation(out=sqscr[:, 0:NCH_B], in_=st2_sum[:],
                              func=AF.Identity)
            ins = scalar.activation(out=sqscr[:, 0:NCH_B], in_=st2_sq[:],
                                    func=AF.Identity)
            P.inc(ins, "s_act", ("ss2F",))
            P.inc(scalar.nop(), "s_act", ("barA_act",))
            P.inc(None, "s_st", ("barA_st",), 0)
            P.inc(None, "s_stB", ("barA_stB",), 0)

    pa.close()
    span_pa.close()

    pb = ExitStack()
    bsb = lambda name, shape, dt: pb.enter_context(nc.sbuf_tensor(name, shape, dt))
    qwu = bsb("qwu", [128, DC, DI], BF16)
    qwd = bsb("qwd", [128, IC, D], BF16)
    x2c = [bsb(f"x2c{i}", [128, D], F32) for i in range(3)]
    tax = bsb("tax", [128, D], F32)
    tx = bsb("tx", [128, D], F32)
    cAx = bsb("cAx", [128, D], BF16)
    cBx = bsb("cBx", [128, D], BF16)
    sgx = bsb("sgx", [128, D], BF16)
    qx = bsb("qx", [128, D], BF16)
    qxT = bsb("qxT", [128, DC, 128], BF16)
    m16x = bsb("m16x", [128, NB_X], F32)
    scx = bsb("scx", [128, NB_X], F32)
    rcx = bsb("rcx", [128, NB_X], F32)
    schx = bsb("schx", [128, NB_X], F32)
    sg_sb = [bsb(f"sg_sb{i}", [128, 512], F32) for i in range(2)]
    HG = [(0, 1024), (1024, 1024), (2048, DI - 2048)]
    tax2 = bsb("tax2", [128, 1024], F32)
    tx2 = bsb("tx2", [128, 1024], F32)
    cAx2 = bsb("cAx2", [128, 1024], BF16)
    cBx2 = bsb("cBx2", [128, 1024], BF16)
    sgx2 = bsb("sgx2", [128, 1024], BF16)
    m16x2 = bsb("m16x2", [128, NB_X], F32)
    scx2 = bsb("scx2", [128, NB_X], F32)
    rcx2 = bsb("rcx2", [128, NB_X], F32)
    schx2 = bsb("schx2", [128, NB_X], F32)
    qh = bsb("qh", [128, DI], BF16)
    taxs, txs, sgxs = [tax, tax2], [tx, tx2], [sgx, sgx2]
    cAs, cBs = [cAx, cAx2], [cBx, cBx2]
    m16s, scs, rcs, schs = ([m16x, m16x2], [scx, scx2],
                            [rcx, rcx2], [schx, schx2])
    taxX = bsb("taxX", [128, 1024], F32)
    qhT = bsb("qhT", [128, IC, 128], BF16)
    psq = [pb.enter_context(nc.psum_tensor(f"psq{i}", [128, 1024], BF16))
           for i in range(2)]
    psg = [pb.enter_context(nc.psum_tensor(f"psg{i}", [128, 512], F32))
           for i in range(2)]
    psu = [pb.enter_context(nc.psum_tensor(f"psu{i}", [128, 512], F32))
           for i in range(2)]
    psd = [pb.enter_context(nc.psum_tensor(f"psd{i}", [128, 512], F32))
           for i in range(2)]

    with nc.Block("phB") as blk:

        @blk.sync
        def _(sync):
            P.wait(sync, ("barA_pe",))
            P.wait(sync, ("barA_dve",))
            P.wait(sync, ("barA_st",))
            P.wait(sync, ("barA_stB",))
            for nm, hbm, dst in (("qwu", qwu_h, qwu),
                                 ("qwd", qwd_h, qwd)):
                sync.dma_start(out=dst[:], in_=hbm).then_inc(SEM["s_ldQ"], 16)
                P.inc(None, "s_ldQ", ("ld_" + nm,), 16)
            for j in range(NCH_B):
                b, rt = j // 16, j % 16
                sem_x = "s_ldx" if j % 2 == 0 else "s_ldB"
                if j >= 3:
                    P.wait(sync, ("outst", j - 3))
                sync.dma_start(
                    out=x2c[j % 3][:],
                    in_=x2s_h[b, rt * 128:(rt + 1) * 128, :],
                ).then_inc(SEM[sem_x], 16)
                P.inc(None, sem_x, ("x2c", j), 16)
            P.wait(sync, ("final_st",))
            P.wait(sync, ("final_stB",))

        @blk.tensor
        def _(tensor):
            def emit_qxT(jj):
                P.wait(tensor, ("qx", jj))
                if jj >= 1:
                    P.wait(tensor, ("qhTe", jj - 1, NGD - 1))
                for dc in range(DC):
                    ins = tensor.matmul(
                        psq[0][:, dc * 128:(dc + 1) * 128],
                        qx[:, dc * 128:(dc + 1) * 128], identb[:],
                        is_transpose=True,
                        start=(dc == 0), stop=(dc == DC - 1))
                P.inc(ins, "s_pe", ("qxT", jj))

            def emit_down(jj, part):
                ic0, ic1 = (0, 16) if part == 0 else (16, IC)
                P.wait(tensor, ("qhTe", jj, 3 if part == 0 else NGD - 1))
                if jj == 0 and part == 0:
                    P.wait(tensor, ("ld_qwd",))
                for nb in range(2):
                    if part == 0 and jj >= 1:
                        P.wait(tensor, ("resid2nb", jj - 1, nb))
                    for ici in range(ic0, ic1):
                        ins = tensor.matmul(
                            psd[nb][:], qhT[:, ici, :],
                            qwd[:, ici, nb * 512:(nb + 1) * 512],
                            start=(ici == 0), stop=(ici == IC - 1))
                    if part == 1:
                        P.inc(ins, "s_pe", ("pd", jj, nb))

            P.wait(tensor, ("ld_qwg",))
            emit_qxT(0)
            for j in range(NCH_B):
                P.wait(tensor, ("qxTe", j))
                for ib in range(NIB):
                    par = ib % 2
                    i0, ni = IBS[ib]
                    pg = ("silu", j, ib - 2) if ib >= 2 else (
                        ("silu", j - 1, NIB - 2 + ib) if j >= 1 else None)
                    if pg is not None:
                        P.wait(tensor, pg)
                    for dc in range(DC):
                        ins = tensor.matmul(
                            psg[par][:, 0:ni], qxT[:, dc, :],
                            qwg[:, dc, i0:i0 + ni],
                            start=(dc == 0), stop=(dc == DC - 1))
                    P.inc(ins, "s_pe", ("pg", j, ib))
                    pu = ("hb", j, ib - 2) if ib >= 2 else (
                        ("hb", j - 1, NIB - 2 + ib) if j >= 1 else None)
                    if pu is not None:
                        P.wait(tensor, pu)
                    if j == 0 and ib == 0:
                        P.wait(tensor, ("ld_qwu",))
                    for dc in range(DC):
                        ins = tensor.matmul(
                            psu[par][:, 0:ni], qxT[:, dc, :],
                            qwu[:, dc, i0:i0 + ni],
                            start=(dc == 0), stop=(dc == DC - 1))
                    P.inc(ins, "s_pe", ("pu", j, ib))
                    if ib == 0 and j >= 1:
                        emit_down(j - 1, 1)
                for g in range(NGD):
                    ics = list(range(g * 4, min(IC, g * 4 + 4)))
                    P.wait(tensor, ("qhg", j, g // 2))
                    if g == 0:
                        pg2 = ("qhTe", j - 1, NGD - 2) if j >= 1 else None
                    elif g == 1:
                        pg2 = ("qxTe", j)
                    else:
                        pg2 = ("qhTe", j, g - 2)
                    if pg2 is not None:
                        P.wait(tensor, pg2)
                    pt = psq[1] if g % 2 == 0 else psq[0]
                    for ici in ics:
                        ins = tensor.matmul(
                            pt[:, (ici - g * 4) * 128:(ici - g * 4 + 1) * 128],
                            qh[:, ici * 128:(ici + 1) * 128], identb[:],
                            is_transpose=True,
                            start=(ici == ics[0]), stop=(ici == ics[-1]))
                    P.inc(ins, "s_pe", ("qhT", j, g))
                    if g == 3:
                        emit_down(j, 0)
                if j + 1 < NCH_B:
                    emit_qxT(j + 1)
            emit_down(NCH_B - 1, 1)

        @blk.vector
        def _(vector):
            def _spv():
                vector.tensor_copy(spcC[:], spcA[:])

            def emit_hm(j, hg):
                h0 = HG[hg][0]
                ta = taxs[(hg + 1) % 2]
                for ib in (2 * hg, 2 * hg + 1):
                    par = ib % 2
                    i0, ni = IBS[ib]
                    P.wait(vector, ("silu", j, ib))
                    P.wait(vector, ("pu", j, ib))
                    ins = vector.tensor_tensor(
                        out=ta[:, i0 - h0:i0 - h0 + ni],
                        in0=sg_sb[par][:, 0:ni],
                        in1=psu[par][:, 0:ni], op=ALU.mult)
                    P.inc(ins, "s_dve", ("hb", j, ib))

            def emit_resid(j, nb):
                i = j % 3
                P.wait(vector, ("pd", j, nb))
                ins = vector.tensor_tensor(
                    out=x2c[i][:, nb * 512:(nb + 1) * 512],
                    in0=psd[nb][:],
                    in1=x2c[i][:, nb * 512:(nb + 1) * 512], op=ALU.add)
                P.inc(ins, "s_dve", ("resid2nb", j, nb))

            def emit_xq(jj):
                P.wait(vector, ("x2c", jj))
                _q_prex(P, vector, taxX[:, 0:D], D, m16x[:], scx[:], rcx[:],
                        schx[:], tx[:, 0:D], ("absx", jj), ("tx", jj), _spv,
                        rstd_ap=rstd2a[:, jj:jj + 1])
                ins = _q_chainAB(vector, cAx[:], cBx[:], tx[:, 0:D])
                P.inc(ins, "s_dve", ("txd", jj))
                if jj >= 1:
                    P.wait(vector, ("qxT", jj - 1))
                _q_post(P, vector, cAx[:], cBx[:], sgx[:], schx[:], qx[:],
                        ("Sx", jj), ("qx", jj), su_key=("sux", jj))

            emit_xq(0)
            for j in range(NCH_B):
                b, rt = j // 16, j % 16
                g = b * 16 + rt
                i = j % 2
                if j + 1 < NCH_B:
                    emit_xq(j + 1)
                if j >= 1:
                    emit_resid(j - 1, 0)
                    emit_resid(j - 1, 1)
                emit_hm(j, 0)
                for hg in range(3):
                    h0, ng = HG[hg]
                    nbh = ng // 16
                    sp = (hg + 1) % 2
                    ta, tb = taxs[sp], txs[sp]
                    P.wait(vector, ("habs", j, hg))
                    vector.tensor_reduce(
                        out=m16s[sp][:, 0:nbh].rearrange(
                            "p (nb one) -> p nb one", one=1),
                        in_=tb[:, 0:ng].rearrange("p (nb k) -> p nb k", k=16),
                        axis=AX.X, op=ALU.max)
                    _spv()
                    vector.tensor_scalar(out=scs[sp][:, 0:nbh],
                                         in0=m16s[sp][:, 0:nbh],
                                         scalar1=1.0 / 6.0, scalar2=1e-12,
                                         op0=ALU.mult, op1=ALU.max)
                    _spv()
                    vector.reciprocal(out=rcs[sp][:, 0:nbh],
                                      in_=scs[sp][:, 0:nbh])
                    vector.tensor_scalar_mul(schs[sp][:, 0:nbh],
                                             scs[sp][:, 0:nbh], 0.5)
                    _spv()
                    b0, b1 = broadcast_tensor_aps(
                        tb[:, 0:ng].rearrange("p (nb k) -> p nb k", k=16),
                        rcs[sp][:, 0:nbh].rearrange(
                            "p (nb one) -> p nb one", one=1))
                    ins = vector.tensor_tensor(
                        out=ta[:, 0:ng].rearrange("p (nb k) -> p nb k", k=16),
                        in0=b0, in1=b1, op=ALU.mult)
                    P.inc(ins, "s_dve", ("htx", j, hg))
                    if hg < 2:
                        emit_hm(j, hg + 1)
                    ins = _q_chainAB(vector, cAs[sp][:, 0:ng],
                                     cBs[sp][:, 0:ng], ta[:, 0:ng])
                    P.inc(ins, "s_dve", ("hchd", j, hg))
                    _q_post(P, vector, cAs[sp][:, 0:ng], cBs[sp][:, 0:ng],
                            sgxs[sp][:, 0:ng], schs[sp][:, 0:nbh],
                            qh[:, h0:h0 + ng], None, ("qhg", j, hg),
                            su_key=("hsu", j, hg))
            emit_resid(NCH_B - 1, 0)
            emit_resid(NCH_B - 1, 1)


        @blk.scalar
        def _(scalar):
            def emit_outst(jj):
                bs, rts = jj // 16, jj % 16
                P.wait(scalar, ("resid2nb", jj, 1))
                sem_o = "s_st" if jj % 2 == 0 else "s_stB"
                st = scalar.dma_start(
                    out=out_h[bs, rts * 128:(rts + 1) * 128, :],
                    in_=x2c[jj % 3][:])
                st.then_inc(SEM[sem_o], 16)
                P.inc(None, sem_o, ("outst", jj), 16)

            def emit_absx(jj):
                P.wait(scalar, ("x2c", jj))
                if jj >= 1:
                    P.wait(scalar, ("tx", jj - 1))
                    if jj >= 2:
                        P.wait(scalar, ("hsu", jj - 2, 1))
                    else:
                        P.wait(scalar, ("sux", jj - 1))
                scalar.activation(out=taxX[:, 0:D], in_=x2c[jj % 3][:],
                                  func=AF.Abs, bias=nm2a[:, jj:jj + 1])
                ins = scalar.activation(out=sgx[:], in_=x2c[jj % 3][:],
                                        func=AF.Sign, bias=nm2a[:, jj:jj + 1])
                P.inc(ins, "s_act", ("absx", jj))

            def emit_qxTe(jj):
                P.wait(scalar, ("qxT", jj))
                ins = scalar.activation(
                    out=qxT[:].rearrange("p a b -> p (a b)"), in_=psq[0][:],
                    func=AF.Copy)
                P.inc(ins, "s_act", ("qxTe", jj))

            emit_absx(0)
            emit_absx(1)
            emit_qxTe(0)
            for j in range(NCH_B):
                b, rt = j // 16, j % 16
                g = b * 16 + rt
                i = j % 2
                for hg in range(3):
                    h0, ng = HG[hg]
                    for ib in (2 * hg, 2 * hg + 1):
                        par = ib % 2
                        ni = IBS[ib][1]
                        P.wait(scalar, ("pg", j, ib))
                        ins = scalar.activation(out=sg_sb[par][:, 0:ni],
                                                in_=psg[par][:, 0:ni],
                                                func=AF.Silu)
                        P.inc(ins, "s_act", ("silu", j, ib))
                    P.wait(scalar, ("hb", j, 2 * hg + 1))
                    if hg == 0:
                        if j >= 1:
                            P.wait(scalar, ("hchd", j - 1, 2))
                            P.wait(scalar, ("hsu", j - 1, 2))
                    elif hg == 1:
                        if j >= 1:
                            P.wait(scalar, ("hchd", j - 1, 1))
                        P.wait(scalar, ("sux", min(j + 1, NCH_B - 1)))
                        P.wait(scalar, ("txd", min(j + 1, NCH_B - 1)))
                    else:
                        P.wait(scalar, ("hchd", j, 0))
                        P.wait(scalar, ("hsu", j, 0))
                    sp = (hg + 1) % 2
                    scalar.activation(out=txs[sp][:, 0:ng],
                                      in_=taxs[sp][:, 0:ng], func=AF.Abs)
                    ins = scalar.activation(out=sgxs[sp][:, 0:ng],
                                            in_=taxs[sp][:, 0:ng],
                                            func=AF.Sign)
                    P.inc(ins, "s_act", ("habs", j, hg))
                    if hg == 0 and j >= 1:
                        emit_outst(j - 1)
                for gq in range(NGD):
                    ngc = min(IC, gq * 4 + 4) - gq * 4
                    P.wait(scalar, ("qhT", j, gq))
                    pt = psq[1] if gq % 2 == 0 else psq[0]
                    ins = scalar.activation(
                        out=qhT[:, gq * 4:gq * 4 + ngc, :].rearrange(
                            "p a b -> p (a b)"),
                        in_=pt[:, 0:ngc * 128], func=AF.Copy)
                    P.inc(ins, "s_act", ("qhTe", j, gq))
                    if gq == 3 and j + 2 < NCH_B:
                        emit_absx(j + 2)
                if j + 1 < NCH_B:
                    emit_qxTe(j + 1)
            emit_outst(NCH_B - 1)
            P.inc(None, "s_st", ("final_st",), 0)
            P.inc(None, "s_stB", ("final_stB",), 0)

    pb.close()
    outer.close()
    return nc, P.tok


def kernel(**inputs):
    x = np.ascontiguousarray(np.asarray(inputs["L0_attn_h_prev_layer"], dtype=np.float32))
    wz = np.ascontiguousarray(np.asarray(inputs["L0_attn_W_z"], dtype=np.float32))
    bz = np.ascontiguousarray(np.asarray(inputs["L0_attn_b_z"], dtype=np.float32))
    wh = np.ascontiguousarray(np.asarray(inputs["L0_attn_W_h"], dtype=np.float32))
    bh = np.ascontiguousarray(np.asarray(inputs["L0_attn_b_h"], dtype=np.float32))
    wg = np.ascontiguousarray(np.asarray(inputs["L0_ffn_w_gate"], dtype=np.float32))
    wu = np.ascontiguousarray(np.asarray(inputs["L0_ffn_w_up"], dtype=np.float32))
    wd = np.ascontiguousarray(np.asarray(inputs["L0_ffn_w_down"], dtype=np.float32))

    _, tok = build_program(None)
    nc, _ = build_program(tok)

    in_maps = [
        dict(x=x[c * B:(c + 1) * B], wz=wz, bz=bz, wh=wh, bh=bh,
             wg=wg, wu=wu, wd=wd)
        for c in range(NCORES)
    ]
    res = run_bass_kernel_spmd(nc, in_maps, core_ids=list(range(NCORES)))
    out = np.empty((BFULL, T, D), np.float32)
    for c in range(NCORES):
        out[c * B:(c + 1) * B] = np.asarray(res.results[c]["out"])
    return out
